# version 03
# baseline (speedup 1.0000x reference)
import numpy as np
import ml_dtypes
from contextlib import ExitStack

import concourse.bass as bass
import concourse.mybir as mybir
from concourse.bass_utils import run_bass_kernel_spmd

F32 = mybir.dt.float32
BF16 = mybir.dt.bfloat16
AF = mybir.ActivationFunctionType
ALU = mybir.AluOpType
AX = mybir.AxisListType

MAGIC = 12582912.0
NEG = -1.0e30
D = 1024
L = 512
FF = 4096
H = 16
DH = 64
EPS = 1e-5


def build_nc(T=4096):
    nc = bass.Bass("TRN2", target_bir_lowering=False, detect_race_conditions=False)
    nt = T // 128
    ns = T // 512

    x_h = nc.dram_tensor("x", [T, D], F32, kind="ExternalInput")
    wcd_h = nc.dram_tensor("wcd", [D, L], F32, kind="ExternalInput")
    wq_h = nc.dram_tensor("wq", [L, D], F32, kind="ExternalInput")
    wkv_h = nc.dram_tensor("wkv", [L, 2 * DH], F32, kind="ExternalInput")
    wpr_h = nc.dram_tensor("wpr", [D, D], F32, kind="ExternalInput")
    w1_h = nc.dram_tensor("w1", [FF, D], F32, kind="ExternalInput")
    b1_h = nc.dram_tensor("b1", [1, FF], F32, kind="ExternalInput")
    w2_h = nc.dram_tensor("w2", [D, FF], F32, kind="ExternalInput")
    b2_h = nc.dram_tensor("b2", [1, D], F32, kind="ExternalInput")
    msk_h = nc.dram_tensor("msk", [128, 4 * 512], F32, kind="ExternalInput")
    out_h = nc.dram_tensor("out", [T, D], F32, kind="ExternalOutput")
    spill_h = nc.dram_tensor("ffn_spill", [T, D], BF16, kind="ExternalOutput")

    st = ExitStack()
    sb = lambda name, shape, dt: st.enter_context(nc.sbuf_tensor(name, shape, dt))
    sem = lambda name: st.enter_context(nc.semaphore(name))

    xb = [sb("xb0", [128, D], F32), sb("xb1", [128, D], F32)]
    b2rep = sb("b2rep", [128, D], BF16)
    ones1f = sb("ones1f", [1, 128], F32)
    ones1b = sb("ones1b", [1, 128], BF16)
    onesff = sb("onesff", [128, 128], F32)
    ones64 = sb("ones64", [64, 1], BF16)
    stt = sb("stt", [128, 16 * nt], F32)

    def stc(i):
        return stt[:, i * nt:(i + 1) * nt]

    w1s = sb("w1s", [128, 1], F32)
    mw1s = sb("mw1s", [128, 1], F32)
    w2s = sb("w2s", [128, 1], F32)
    mw2s = sb("mw2s", [128, 1], F32)
    wabs = sb("wabs", [128, 40], F32)
    brow = sb("brow", [1, 512], F32)
    tmp1 = sb("tmp1", [128, 1], F32)
    tmp2 = sb("tmp2", [128, 1], F32)

    ps = st.enter_context(nc.psum_tensor("ps", [128, 4096], F32))

    SY, V, S, TE, G = nc.sync, nc.vector, nc.scalar, nc.tensor, nc.gpsimd

    class Ctr:
        def __init__(self, name):
            self.h = sem(name)
            self.v = 0

        def inc(self, inst, n=1):
            inst.then_inc(self.h, n)
            self.v += n

        def wait(self, eng, val=None):
            val = self.v if val is None else val
            if val > 0:
                eng.wait_ge(self.h, val)

    class DCtr:

        def __init__(self, name):
            self.h = [sem(name + "a"), sem(name + "b")]
            self.v = [0, 0]
            self.b = 0
            self.cur = None

        def begin(self, eng):
            p = self.b % 2
            self.b += 1
            self.cur = p
            if self.v[p] > 0:
                eng.wait_ge(self.h[p], self.v[p])
            return p

        def issue(self, inst):
            p = self.cur
            inst.then_inc(self.h[p], 16)
            self.v[p] += 16

        def mark(self):
            return (self.cur, self.v[self.cur])

        def wait_mark(self, eng, mk):
            if mk is not None:
                eng.wait_ge(self.h[mk[0]], mk[1])

        def wait_all(self, eng):
            for p in (0, 1):
                if self.v[p] > 0:
                    eng.wait_ge(self.h[p], self.v[p])

    s_xld = DCtr("s_xld")
    s_xfr = Ctr("s_xfr")
    s_xfs = Ctr("s_xfs")
    s_bld = DCtr("s_bld")
    s_mm = Ctr("s_mm")
    s_vec = Ctr("s_vec")
    s_sca = Ctr("s_sca")
    s_wld = DCtr("s_wld")
    s_wfr = Ctr("s_wfr")
    s_tp = DCtr("s_tp")
    s_ff1 = Ctr("s_ff1")
    s_hev = Ctr("s_hev")
    s_ff2 = Ctr("s_ff2")
    s_f2e = Ctr("s_f2e")
    s_sp = DCtr("s_sp")
    s_q1 = Ctr("s_q1")
    s_q2 = Ctr("s_q2")
    s_g = Ctr("s_g")
    s_lat = Ctr("s_lat")
    s_sc = Ctr("s_sc")
    s_msk = Ctr("s_msk")
    s_exp = Ctr("s_exp")
    s_pv = Ctr("s_pv")
    s_ye = Ctr("s_ye")
    s_qp = Ctr("s_qp")
    s_nb = Ctr("s_nb")
    s_out = DCtr("s_out")
    s_fb = DCtr("s_fb")
    s_pr = Ctr("s_pr")
    s_asm = Ctr("s_asm")
    s_kv = Ctr("s_kv")
    s_wc = DCtr("s_wc")

    def vsp():
        V.tensor_scalar(tmp1[:, :], tmp1[:, :], 1.0, None, ALU.mult)

    V.memset(ones1f[:, :], 1.0)
    V.memset(ones1b[:, :], 1.0)
    V.memset(onesff[:, :], 1.0)
    s_vec.inc(V.memset(ones64[:, :], 1.0))

    p1st = ExitStack()
    sb1 = lambda name, shape, dt: p1st.enter_context(nc.sbuf_tensor(name, shape, dt))
    big_a = sb1("big_a", [128, FF], F32)
    t1b = sb1("t1b", [128, D], F32)
    b1rep = sb1("b1rep", [128, FF], BF16)
    w1qT = sb1("w1qT", [128, 8 * FF], BF16)
    w2qT = sb1("w2qT", [128, 32 * D], BF16)
    wq_sb = sb1("wqs0", [128, 1024], BF16)

    for i in range(nt):
        b = xb[i % 2]
        if i >= 2:
            s_xfr.wait(SY, i - 1)
            s_xfs.wait(SY, i - 1)
        s_xld.begin(SY)
        s_xld.issue(SY.dma_start(b[:, :], x_h[i * 128:(i + 1) * 128, :]))
        xmk = s_xld.mark()
        s_xld.wait_mark(V, xmk)
        s_xfr.inc(V.reduce_sum(stc(0)[:, i:i + 1], b[:, :], axis=AX.X))
        V.reduce_max(stc(2)[:, i:i + 1], b[:, :], axis=AX.X)
        V.tensor_reduce(stc(3)[:, i:i + 1], b[:, :], axis=AX.X, op=ALU.min)
        s_xld.wait_mark(S, xmk)
        s_xfs.inc(S.activation(t1b[:, :], b[:, :], AF.Square,
                               accum_out=stc(1)[:, i:i + 1]))

    s_xld.wait_all(V)
    s_xfr.wait(V, nt)
    s_xfs.wait(V, nt)
    inv_d = 1.0 / D
    m = stc(12)
    V.tensor_scalar(m, stc(0), inv_d, None, ALU.mult)
    V.tensor_scalar(stc(13), stc(1), inv_d, None, ALU.mult)
    V.tensor_mul(stc(14), m, m)
    vsp()
    V.tensor_sub(stc(1), stc(13), stc(14))
    vsp()
    s_vec.inc(V.tensor_scalar(stc(15), stc(1), EPS, None, ALU.add))
    s_vec.wait(S)
    s_sca.inc(S.activation(stc(14), stc(15), AF.Ln))
    s_sca.wait(V)
    s_vec.inc(V.tensor_scalar(stc(14), stc(14), -0.5, None, ALU.mult))
    s_vec.wait(S)
    s_sca.inc(S.activation(stc(4), stc(14), AF.Exp))
    s_sca.wait(V)
    V.scalar_tensor_tensor(stc(5), stc(4), -1.0, m, ALU.mult, ALU.mult)
    V.tensor_sub(stc(2), stc(2), m)
    V.tensor_sub(stc(3), m, stc(3))
    vsp()
    V.tensor_max(stc(2), stc(2), stc(3))
    vsp()
    V.tensor_mul(stc(2), stc(2), stc(4))
    V.tensor_mul(stc(14), stc(4), stc(4))
    vsp()
    V.tensor_mul(stc(14), stc(1), stc(14))
    vsp()
    s_vec.inc(V.tensor_scalar(stc(14), stc(14), EPS, None, ALU.add))
    s_vec.wait(S)
    s_sca.inc(S.activation(stc(15), stc(14), AF.Ln))
    s_sca.wait(V)
    s_vec.inc(V.tensor_scalar(stc(15), stc(15), -0.5, None, ALU.mult))
    s_vec.wait(S)
    s_sca.inc(S.activation(stc(3), stc(15), AF.Exp))
    s_sca.wait(V)
    V.tensor_mul(stc(2), stc(2), stc(3))
    vsp()
    s_vec.inc(V.tensor_scalar(stc(2), stc(2), 1e-5, None, ALU.max))
    s_vec.wait(S)
    s_sca.inc(S.activation(stc(15), stc(2), AF.Ln))
    s_sca.wait(V)
    s_vec.inc(V.tensor_scalar(stc(15), stc(15), -1.0, None, ALU.mult))
    s_vec.wait(S)
    s_sca.inc(S.activation(stc(14), stc(15), AF.Exp))
    s_sca.wait(V)
    V.tensor_scalar(stc(14), stc(14), 127.0, None, ALU.mult)
    V.tensor_mul(stc(6), stc(4), stc(3))
    V.tensor_mul(stc(7), stc(5), stc(3))
    V.tensor_mul(stc(6), stc(6), stc(14))
    s_vec.inc(V.tensor_mul(stc(7), stc(7), stc(14)))


    s_vec.wait(TE)
    for c in range(FF // 512 + D // 512):
        if c < FF // 512:
            src_ap = b1_h[0:1, c * 512:(c + 1) * 512]
            dst = b1rep[:, c * 512:(c + 1) * 512]
        else:
            c2 = c - FF // 512
            src_ap = b2_h[0:1, c2 * 512:(c2 + 1) * 512]
            dst = b2rep[:, c2 * 512:(c2 + 1) * 512]
        pslot = ps[:, (c % 2) * 512:(c % 2) * 512 + 512]
        if c >= 1:
            s_mm.wait(SY)
        s_bld.begin(SY)
        s_bld.issue(SY.dma_start(brow[:, :], src_ap))
        s_bld.wait_mark(TE, s_bld.mark())
        if c >= 2:
            s_vec.wait(TE, s_vec.v - 1)
        s_mm.inc(TE.matmul(pslot, ones1f[:, :], brow[:, :], start=True, stop=True))
        s_mm.wait(V)
        s_vec.inc(V.tensor_copy(dst, pslot))


    def w_pass1(w_h, cols, nstr, off):
        for sx in range(nstr):
            b = big_a
            s_wfr.wait(G)
            s_wld.begin(G)
            s_wld.issue(G.dma_start(b[:, 0:cols], w_h[sx * 128:(sx + 1) * 128, :]))
            s_wld.wait_mark(V, s_wld.mark())
            s_wfr.inc(V.reduce_sum(wabs[:, off + sx:off + sx + 1], b[:, 0:cols],
                                   axis=AX.X, apply_absolute_value=True))

    w_pass1(w1_h, D, FF // 128, 0)
    w_pass1(w2_h, FF, D // 128, 32)
    s_wfr.wait(V)
    s_vec.inc(V.engine_nop())
    s_vec.wait(TE)
    s_mm.inc(TE.matmul(ps[:, 2048:2088], onesff[:, :], wabs[:, 0:40],
                       start=True, stop=True))
    s_mm.wait(V)
    V.reduce_sum(w1s[:, :], ps[:, 2048:2080], axis=AX.X)
    V.reduce_sum(w2s[:, :], ps[:, 2080:2088], axis=AX.X)
    for wsx, mwx, nel in ((w1s, mw1s, FF * D), (w2s, mw2s, D * FF)):
        V.tensor_scalar(tmp2[:, :], wsx[:, :], 1.0 / nel, None, ALU.mult)
        vsp()
        s_vec.inc(V.tensor_scalar(tmp2[:, :], tmp2[:, :], 1e-5, None, ALU.max))
        s_vec.wait(S)
        s_sca.inc(S.activation(tmp1[:, :], tmp2[:, :], AF.Copy))
        s_sca.wait(V)
        V.reciprocal(wsx[:, :], tmp1[:, :])
        s_vec.inc(V.tensor_scalar(mwx[:, :], tmp2[:, :], 1.0 / 127.0, None, ALU.mult))
    s_vec.inc(V.engine_nop())
    V.tensor_scalar(stc(8), stc(2), mw1s[:, 0:1], None, ALU.mult)

    def w_pass2(w_h, cols, nstr, s_rec, qT, chunk_cols):
        q = wq_sb
        for sx in range(nstr):
            b = big_a
            s_wfr.wait(G)
            s_wld.begin(G)
            s_wld.issue(G.dma_start(b[:, 0:cols], w_h[sx * 128:(sx + 1) * 128, :]))
            s_wld.wait_mark(V, s_wld.mark())
            V.tensor_scalar(b[:, 0:cols], b[:, 0:cols], s_rec[:, 0:1], MAGIC,
                            ALU.mult, ALU.add)
            V.tensor_scalar(b[:, 0:cols], b[:, 0:cols], MAGIC, 1.0,
                            ALU.subtract, ALU.min)
            for qc in range(cols // 1024):
                s_tp.wait_all(V)
                s_vec.inc(V.tensor_scalar(q[:, :], b[:, 1024 * qc:1024 * qc + 1024],
                                           -1.0, None, ALU.max))
                if qc == cols // 1024 - 1:
                    s_wfr.inc(V.engine_nop())
                s_vec.wait(SY)
                s_tp.begin(SY)
                for k in range(8):
                    kk = 8 * qc + k
                    s_tp.issue(SY.dma_start(
                        qT[:, chunk_cols * kk + 128 * sx:chunk_cols * kk + 128 * sx + 128],
                        q[:, 128 * k:128 * k + 128], transpose=True))

    w_pass2(w1_h, D, FF // 128, w1s, w1qT, FF)
    w_pass2(w2_h, FF, D // 128, w2s, w2qT, D)

    xq1 = sb1("xq1", [128, D], BF16)
    xq1T = sb1("xq1T", [128, D], BF16)
    hq = sb1("hq", [128, D], BF16)
    hqT = sb1("hqT", [128, FF], BF16)
    spst = [sb1("spst0", [128, D], BF16)]
    sp_marks = []

    for i in range(nt):
        b = xb[i % 2]
        if i == 0:
            s_xfr.wait(SY)
            s_xfs.wait(SY)
            base_xfr = s_xfr.v
        else:
            s_xfr.wait(SY, base_xfr + i - 1)
        s_xld.begin(SY)
        s_xld.issue(SY.dma_start(b[:, :], x_h[i * 128:(i + 1) * 128, :]))
        s_xld.wait_mark(V, s_xld.mark())
        s_xfr.inc(V.tensor_scalar(t1b[:, :], b[:, :], stc(6)[:, i:i + 1],
                                  stc(7)[:, i:i + 1], ALU.mult, ALU.add))
        if i > 0:
            s_tp.wait_mark(V, xq1_mk)
        s_q1.inc(V.tensor_scalar(xq1[:, :], t1b[:, :], MAGIC, MAGIC,
                                 ALU.add, ALU.subtract))
        s_q1.wait(SY)
        if i > 0:
            s_ff1.wait(SY)
        s_tp.begin(SY)
        for k in range(8):
            s_tp.issue(SY.dma_start(xq1T[:, 128 * k:128 * k + 128],
                                    xq1[:, 128 * k:128 * k + 128], transpose=True))
        xq1_mk = s_tp.mark()
        s_tp.wait_all(TE)
        for fc in range(8):
            pslot = ps[:, (fc % 2) * 512:(fc % 2) * 512 + 512]
            s_hev.wait(TE, s_hev.v - 1)
            for dc in range(8):
                mm = TE.matmul(pslot, xq1T[:, 128 * dc:128 * dc + 128],
                               w1qT[:, FF * dc + 512 * fc:FF * dc + 512 * fc + 512],
                               start=(dc == 0), stop=(dc == 7))
            s_ff1.inc(mm)
            s_ff1.wait(V)
            s_hev.inc(V.scalar_tensor_tensor(
                big_a[:, 512 * fc:512 * fc + 512], pslot, stc(8)[:, i:i + 1],
                b1rep[:, 512 * fc:512 * fc + 512], ALU.mult, ALU.add))
        s_hev.wait(S)
        if i > 0:
            s_ff2.wait(S)
        s_g.inc(S.activation(big_a[:, :], big_a[:, :], AF.Gelu))
        s_g.wait(V)
        s_vec.inc(V.reduce_max(stc(13)[:, i:i + 1], big_a[:, :], axis=AX.X,
                               apply_absolute_value=True))
        s_vec.wait(S)
        s_sca.inc(S.activation(hqT[:, :], big_a[:, :], AF.Square,
                               accum_out=stc(9)[:, i:i + 1]))
        s_sca.inc(S.activation(stc(15)[:, i:i + 1], stc(13)[:, i:i + 1], AF.Copy))
        s_sca.wait(V)
        V.reciprocal(stc(14)[:, i:i + 1], stc(15)[:, i:i + 1])
        V.tensor_scalar(stc(10)[:, i:i + 1], stc(13)[:, i:i + 1], mw2s[:, 0:1],
                        None, ALU.mult)
        s_vec.inc(V.tensor_scalar(stc(14)[:, i:i + 1], stc(14)[:, i:i + 1],
                                  127.0, None, ALU.mult))
        s_vec.wait(S)
        s_sca.inc(S.activation(big_a[:, :], big_a[:, :], AF.Copy,
                               scale=stc(14)[:, i:i + 1]))
        s_sca.wait(V)
        hq_mk = None
        for kc in range(4):
            s_tp.wait_mark(V, hq_mk)
            s_q2.inc(V.tensor_scalar(hq[:, :], big_a[:, 1024 * kc:1024 * kc + 1024],
                                     MAGIC, MAGIC, ALU.add, ALU.subtract))
            s_q2.wait(SY)
            if i > 0 and kc == 0:
                s_ff2.wait(SY)
            s_tp.begin(SY)
            for k in range(8):
                s_tp.issue(SY.dma_start(
                    hqT[:, 1024 * kc + 128 * k:1024 * kc + 128 * k + 128],
                    hq[:, 128 * k:128 * k + 128], transpose=True))
            hq_mk = s_tp.mark()
        s_tp.wait_all(TE)
        s_f2e.wait(TE)
        sp = spst[0]
        for dc2 in range(2):
            pslot = ps[:, 1024 + dc2 * 512:1024 + dc2 * 512 + 512]
            for fc in range(32):
                mm = TE.matmul(pslot, hqT[:, 128 * fc:128 * fc + 128],
                               w2qT[:, D * fc + 512 * dc2:D * fc + 512 * dc2 + 512],
                               start=(fc == 0), stop=(fc == 31))
            s_ff2.inc(mm)
            s_ff2.wait(V)
            if i >= 1 and dc2 == 0:
                s_sp.wait_mark(V, sp_marks[i - 1])
            s_f2e.inc(V.tensor_scalar(sp[:, 512 * dc2:512 * dc2 + 512], pslot,
                                      stc(10)[:, i:i + 1], None, ALU.mult))
        s_f2e.wait(G)
        s_sp.begin(G)
        s_sp.issue(G.dma_start(spill_h[i * 128:(i + 1) * 128, :], sp[:, :]))
        sp_marks.append(s_sp.mark())

    p1st.close()
    for eng in (G, SY, V, S, TE):
        s_sp.wait_all(eng)
    V.tensor_scalar(stc(15), stc(9), 1.0 / FF, EPS, ALU.mult, ALU.add)
    s_vec.inc(V.engine_nop())
    s_vec.wait(S)
    S.activation(stc(15), stc(15), AF.Ln)
    s_sca.inc(S.activation(stc(15), stc(15), AF.Exp, scale=0.5))
    s_sca.wait(V)
    s_vec.inc(V.reciprocal(stc(11), stc(15)))

    wcd_bf = sb("wcd_bf", [128, 8 * 512], BF16)
    wq_bf = sb("wq_bf", [128, 4 * 1024], BF16)
    wkv_bf = sb("wkv_bf", [128, 4 * 128], BF16)
    wpr_bf = sb("wpr_bf", [128, 8 * 1024], BF16)
    wtmp = sb("wtmp", [128, 1024], F32)

    def load_cast(src_h, nchunk, ccols, dst, scale=None):
        for c in range(nchunk):
            s_vec.wait(G)
            s_wc.begin(G)
            s_wc.issue(G.dma_start(wtmp[:, 0:ccols], src_h[c * 128:(c + 1) * 128, :]))
            s_wc.wait_mark(V, s_wc.mark())
            if scale is None:
                s_vec.inc(V.tensor_copy(dst[:, c * ccols:(c + 1) * ccols],
                                        wtmp[:, 0:ccols]))
            else:
                s_vec.inc(V.tensor_scalar(dst[:, c * ccols:(c + 1) * ccols],
                                          wtmp[:, 0:ccols], scale, None, ALU.mult))

    load_cast(wcd_h, 8, 512, wcd_bf)
    load_cast(wq_h, 4, 1024, wq_bf, scale=0.125)
    load_cast(wkv_h, 4, 128, wkv_bf)
    load_cast(wpr_h, 8, 1024, wpr_bf)
    wts_ready = s_vec.v
    mask4 = sb("mask4", [128, 2048], F32)
    s_bld.begin(SY)
    s_bld.issue(SY.dma_start(mask4[:, :], msk_h[:, :]))
    mask_mk = s_bld.mark()

    outst = [sb("outst0", [128, D], F32), sb("outst1", [128, D], F32)]
    fbk = [sb("fbk0", [128, D], BF16), sb("fbk1", [128, D], BF16)]
    kT64 = sb("kT64", [64, T], BF16)
    vext = sb("vext", [128, 66 * nt], BF16)
    latT = sb("latT", [128, 4 * T], BF16)
    xnbf = sb("xnbf", [128, D], BF16)
    xnT = sb("xnT", [128, 8 * 512], BF16)
    q64 = sb("q64", [64, 16 * 512], BF16)
    sq64 = sb("sq64", [64, 16 * 512], BF16)
    arow = sb("arow", [1, 512], BF16)
    amaxv = sb("amaxv", [128, 1], F32)
    nbias = sb("nbias", [128, 1], F32)
    knt = sb("knt", [128, 1], F32)
    kn2t = sb("kn2t", [128, 1], F32)
    sqk = sb("sqk", [64, 512], BF16)
    pbf = [sb("pbf0", [128, 1024], BF16), sb("pbf1", [128, 1024], BF16)]
    ybuf = sb("ybuf", [128, 4 * D], BF16)
    yT = sb("yT", [128, 8 * 512], BF16)
    drb = sb("drb", [128, 4], F32)

    s_vec.inc(V.memset(vext[:, :], 1.0))
    vext_ones = s_vec.v

    lat_marks = []
    for I in range(ns):
        for t4 in range(4):
            i = 4 * I + t4
            b = xb[i % 2]
            s_xfr.wait(SY, s_xfr.v - (1 if i > 0 else 0))
            s_xld.begin(SY)
            s_xld.issue(SY.dma_start(b[:, :], x_h[i * 128:(i + 1) * 128, :]))
            s_xld.wait_mark(V, s_xld.mark())
            if i > 0:
                s_tp.wait_mark(V, xn_mk)
            s_xfr.inc(V.tensor_scalar(xnbf[:, :], b[:, :], stc(4)[:, i:i + 1],
                                      stc(5)[:, i:i + 1], ALU.mult, ALU.add))
            s_xfr.wait(SY)
            if I > 0 and t4 == 0:
                s_mm.wait(SY, lat_marks[-1])
            s_tp.begin(SY)
            for k in range(8):
                s_tp.issue(SY.dma_start(
                    xnT[:, 512 * k + 128 * t4:512 * k + 128 * t4 + 128],
                    xnbf[:, 128 * k:128 * k + 128], transpose=True))
            xn_mk = s_tp.mark()
        s_tp.wait_all(TE)
        s_vec.wait(TE, wts_ready)
        for lc in range(4):
            pslot = ps[:, 3584:4096]
            if not (I == 0 and lc == 0):
                s_lat.wait(TE)
            for dc in range(8):
                mm = TE.matmul(pslot,
                               wcd_bf[:, 512 * dc + 128 * lc:512 * dc + 128 * lc + 128],
                               xnT[:, 512 * dc:512 * dc + 512],
                               start=(dc == 0), stop=(dc == 7))
            s_mm.inc(mm)
            s_mm.wait(S)
            s_lat.inc(S.activation(latT[:, T * lc + 512 * I:T * lc + 512 * I + 512],
                                   pslot, AF.Copy))
        lat_marks.append(s_mm.v)

    s_vec.wait(S, vext_ones)
    for I in range(ns):
        s_kv.wait(TE)
        s_lat.wait(TE)
        for lc in range(4):
            mm = TE.matmul(ps[:, 3584:4096], wkv_bf[:, 128 * lc:128 * lc + 128],
                           latT[:, T * lc + 512 * I:T * lc + 512 * I + 512],
                           start=(lc == 0), stop=(lc == 3))
        s_mm.inc(mm)
        s_mm.wait(V)
        s_kv.inc(V.tensor_copy(kT64[:, 512 * I:512 * I + 512], ps[0:64, 3584:4096]))
    for j in range(nt):
        s_kv.wait(TE)
        for lc in range(4):
            mm = TE.matmul(ps[:, 3584:3712],
                           latT[:, T * lc + 128 * j:T * lc + 128 * j + 128],
                           wkv_bf[:, 128 * lc:128 * lc + 128],
                           start=(lc == 0), stop=(lc == 3))
        s_mm.inc(mm)
        s_mm.wait(V)
        s_kv.inc(V.tensor_copy(vext[:, 66 * j:66 * j + 64], ps[:, 3648:3712]))
    V.memset(knt[:, :], 0.0)
    s_vec.inc(V.engine_nop())
    for I in range(ns):
        s_mm.wait(S)
        s_kv.wait(S)
        s_sca.inc(S.activation(sqk[:, :], kT64[:, 512 * I:512 * I + 512], AF.Square))
        s_sca.wait(TE)
        s_vec.wait(TE)
        pslot = ps[0:1, 3584:4096]
        s_mm.inc(TE.matmul(pslot, ones64[:, :], sqk[:, :], start=True, stop=True))
        s_mm.wait(V)
        V.tensor_copy(arow[:, :], pslot)
        s_vec.inc(V.engine_nop())
        s_vec.wait(TE)
        s_mm.inc(TE.matmul(ps[:, 3072:3584], ones1b[:, :], arow[:, :],
                           start=True, stop=True))
        s_mm.wait(V)
        V.reduce_max(kn2t[:, :], ps[:, 3072:3584], axis=AX.X)
        vsp()
        s_vec.inc(V.tensor_max(knt[:, :], knt[:, :], kn2t[:, :]))
    knt_ready = s_vec.v

    out_marks = []
    gpar = 0
    psum_free = {0: 0, 1: 0}
    pbf_free = {0: 0, 1: 0}
    for I in range(ns):
        nblk = 4 * I + 4
        ngrp = (nblk + 1) // 2
        s_asm.wait(TE)
        for h in range(H):
            pslot = ps[0:64, 3584:4096]
            if not (I == 0 and h == 0):
                s_qp.wait(TE)
            for lc in range(4):
                mm = TE.matmul(pslot,
                               wq_bf[:, 1024 * lc + 64 * h:1024 * lc + 64 * h + 64],
                               latT[:, T * lc + 512 * I:T * lc + 512 * I + 512],
                               start=(lc == 0), stop=(lc == 3))
            s_mm.inc(mm)
            s_mm.wait(V)
            s_qp.inc(V.tensor_copy(q64[:, 512 * h:512 * h + 512], pslot))
        s_qp.wait(V)
        s_vec.inc(V.tensor_mul(sq64[:, :], q64[:, :], q64[:, :]))
        s_vec.wait(TE)
        for h in range(H):
            mm = TE.matmul(ps[0:1, 3072:3584], ones64[:, :],
                           sq64[:, 512 * h:512 * h + 512],
                           start=(h == 0), stop=(h == H - 1))
        s_mm.inc(mm)
        s_mm.wait(V)
        s_vec.inc(V.tensor_copy(arow[:, :], ps[0:1, 3072:3584]))
        s_vec.wait(TE)
        s_mm.inc(TE.matmul(ps[:, 3584:4096], ones1b[:, :], arow[:, :],
                           start=True, stop=True))
        s_mm.wait(V)
        V.reduce_max(amaxv[:, :], ps[:, 3584:4096], axis=AX.X)
        vsp()
        s_vec.inc(V.tensor_scalar(amaxv[:, :], amaxv[:, :], 1e-30, None, ALU.max))
        s_vec.wait(S)
        S.activation(amaxv[:, :], amaxv[:, :], AF.Ln, scale=knt[:, 0:1])
        s_sca.inc(S.activation(amaxv[:, :], amaxv[:, :], AF.Exp, scale=0.5))
        s_sca.wait(V)
        s_nb.inc(V.tensor_scalar(nbias[:, :], amaxv[:, :], -1.0, 35.0,
                                 ALU.mult, ALU.add))
        nb_ready = s_nb.v
        for h in range(H):
            exp_at = {}
            par = {}
            sc_at = {}

            def emit_pv(gx):
                s_exp.wait(TE, exp_at[gx])
                if gx == 0:
                    s_ye.wait(TE)
                j0v = 2 * gx
                jnv = min(2, nblk - j0v)
                last = None
                for jj in range(jnv):
                    j = j0v + jj
                    for ti in range(4):
                        if j <= 4 * I + ti:
                            last = TE.matmul(
                                ps[:, 2048 + 512 * ti:2048 + 512 * ti + 65],
                                pbf[par[gx]][:, 512 * jj + 128 * ti:
                                             512 * jj + 128 * ti + 128],
                                vext[:, 66 * j:66 * j + 65],
                                start=(j == 0), stop=(j == 4 * I + ti))
                s_pv.inc(last)
                pbf_free[par[gx]] = s_pv.v

            for g in range(ngrp):
                pp = gpar % 2
                gpar += 1
                par[g] = pp
                j0 = 2 * g
                jn = min(2, nblk - j0)
                s_exp.wait(TE, psum_free[pp])
                for jj in range(jn):
                    j = j0 + jj
                    s_sc.inc(TE.matmul(
                        ps[:, 1024 * pp + 512 * jj:1024 * pp + 512 * jj + 512],
                        kT64[:, 128 * j:128 * j + 128],
                        q64[:, 512 * h:512 * h + 512],
                        start=True, stop=True))
                sc_at[g] = s_sc.v
                nmsk = 0
                for jj in range(jn):
                    di = j0 + jj - 4 * I
                    if di >= 0:
                        if nmsk == 0:
                            s_sc.wait(V, sc_at[g])
                            s_bld.wait_mark(V, mask_mk)
                        pvw = ps[:, 1024 * pp + 512 * jj:
                                 1024 * pp + 512 * jj + 128 * (di + 1)]
                        s_msk.inc(V.tensor_tensor(
                            pvw, pvw,
                            mask4[:, 512 * di:512 * di + 128 * (di + 1)], ALU.add))
                        nmsk += 1
                if g >= 1:
                    emit_pv(g - 1)
                s_sc.wait(S, sc_at[g])
                if nmsk:
                    s_msk.wait(S)
                s_pv.wait(S, pbf_free[pp])
                s_nb.wait(S, nb_ready)
                s_exp.inc(S.activation(pbf[pp][:, 0:512 * jn],
                                       ps[:, 1024 * pp:1024 * pp + 512 * jn],
                                       AF.Exp, bias=nbias[:, 0:1]))
                exp_at[g] = s_exp.v
                psum_free[pp] = s_exp.v
            emit_pv(ngrp - 1)
            s_pv.wait(V)
            for ti in range(4):
                V.reciprocal(drb[:, ti:ti + 1],
                             ps[:, 2048 + 512 * ti + 64:2048 + 512 * ti + 65])
            for ti in range(4):
                s_ye.inc(V.tensor_scalar(
                    ybuf[:, D * ti + 64 * h:D * ti + 64 * h + 64],
                    ps[:, 2048 + 512 * ti:2048 + 512 * ti + 64],
                    drb[:, ti:ti + 1], None, ALU.mult))
        s_ye.wait(SY)
        if I > 0:
            s_pr.wait(SY)
        s_tp.begin(SY)
        for ti in range(4):
            for c in range(8):
                s_tp.issue(SY.dma_start(
                    yT[:, 512 * c + 128 * ti:512 * c + 128 * ti + 128],
                    ybuf[:, D * ti + 128 * c:D * ti + 128 * c + 128],
                    transpose=True))
        s_tp.wait_all(TE)
        for ti in range(4):
            i = 4 * I + ti
            b = xb[i % 2]
            fb = fbk[i % 2]
            s_xfr.wait(SY, s_xfr.v - (1 if i > 0 else 0))
            s_xld.begin(SY)
            s_xld.issue(SY.dma_start(b[:, :], x_h[i * 128:(i + 1) * 128, :]))
            asm_xmk = s_xld.mark()
            if i >= 2:
                s_asm.wait(G, s_asm.v - 2)
            s_fb.begin(G)
            s_fb.issue(G.dma_start(fb[:, :], spill_h[i * 128:(i + 1) * 128, :]))
            fb_mk = s_fb.mark()
            for half in range(2):
                pslot = ps[:, 3072 + 512 * half:3072 + 512 * half + 512]
                if not (I == 0 and ti == 0):
                    s_asm.wait(TE)
                for c in range(8):
                    mm = TE.matmul(
                        pslot, yT[:, 512 * c + 128 * ti:512 * c + 128 * ti + 128],
                        wpr_bf[:, 1024 * c + 512 * half:1024 * c + 512 * half + 512],
                        start=(c == 0), stop=(c == 7))
                s_pr.inc(mm)
            o = outst[i % 2]
            s_pr.wait(V)
            s_xld.wait_mark(V, asm_xmk)
            s_fb.wait_mark(V, fb_mk)
            if i >= 2:
                s_out.wait_mark(V, out_marks[i - 2])
            for half in range(2):
                pslot = ps[:, 3072 + 512 * half:3072 + 512 * half + 512]
                sl = slice(512 * half, 512 * half + 512)
                V.tensor_add(o[:, sl], pslot, b[:, sl])
                V.scalar_tensor_tensor(o[:, sl], fb[:, sl], stc(11)[:, i:i + 1],
                                       o[:, sl], ALU.mult, ALU.add)
                s_asm.inc(V.tensor_add(o[:, sl], o[:, sl], b2rep[:, sl]))
            s_xfr.inc(V.engine_nop())
            s_asm.wait(G)
            s_out.begin(G)
            s_out.issue(G.dma_start(out_h[i * 128:(i + 1) * 128, :], o[:, :]))
            out_marks.append(s_out.mark())

    s_out.wait_all(G)
    st.close()
    return nc


def _mask4_np():
    m = np.zeros((128, 4 * 512), dtype=np.float32)
    r = np.arange(128)
    for di in range(4):
        blk = m[:, 512 * di:512 * di + 512]
        blk[:, 0:128 * di] = NEG
        tri = blk[:, 128 * di:128 * di + 128]
        tri[r[:, None] > r[None, :]] = NEG
    return m


def kernel(L0_attn_x, L0_attn_Wc_down, L0_attn_Wq_up, L0_attn_Wkv_up,
           L0_attn_Wc_proj, L0_ffn_w1, L0_ffn_b1, L0_ffn_w2, L0_ffn_b2):
    T = 4096
    nc = build_nc(T)
    x = np.ascontiguousarray(np.asarray(L0_attn_x, dtype=np.float32))
    shared = {
        "wcd": np.ascontiguousarray(np.asarray(L0_attn_Wc_down, dtype=np.float32)),
        "wq": np.ascontiguousarray(np.asarray(L0_attn_Wq_up, dtype=np.float32)),
        "wkv": np.ascontiguousarray(np.asarray(L0_attn_Wkv_up, dtype=np.float32)),
        "wpr": np.ascontiguousarray(np.asarray(L0_attn_Wc_proj, dtype=np.float32)),
        "w1": np.ascontiguousarray(np.asarray(L0_ffn_w1, dtype=np.float32)),
        "b1": np.ascontiguousarray(np.asarray(L0_ffn_b1, dtype=np.float32)).reshape(1, -1),
        "w2": np.ascontiguousarray(np.asarray(L0_ffn_w2, dtype=np.float32)),
        "b2": np.ascontiguousarray(np.asarray(L0_ffn_b2, dtype=np.float32)).reshape(1, -1),
        "msk": _mask4_np(),
    }
    in_maps = [dict(shared, x=x[c]) for c in range(8)]
    res = run_bass_kernel_spmd(nc, in_maps, core_ids=list(range(8)))
    return np.stack([np.asarray(res.results[c]["out"]) for c in range(8)], axis=0)


# version 04
# speedup vs baseline: 1.0010x; 1.0010x over previous
import numpy as np
import ml_dtypes
from contextlib import ExitStack

import concourse.bass as bass
import concourse.mybir as mybir
from concourse.bass_utils import run_bass_kernel_spmd

F32 = mybir.dt.float32
BF16 = mybir.dt.bfloat16
AF = mybir.ActivationFunctionType
ALU = mybir.AluOpType
AX = mybir.AxisListType

MAGIC = 12582912.0
NEG = -1.0e30
D = 1024
L = 512
FF = 4096
H = 16
DH = 64
EPS = 1e-5


def build_nc(T=4096):
    nc = bass.Bass("TRN2", target_bir_lowering=False, detect_race_conditions=False)
    nt = T // 128
    ns = T // 512

    x_h = nc.dram_tensor("x", [T, D], F32, kind="ExternalInput")
    wcd_h = nc.dram_tensor("wcd", [D, L], F32, kind="ExternalInput")
    wq_h = nc.dram_tensor("wq", [L, D], F32, kind="ExternalInput")
    wkv_h = nc.dram_tensor("wkv", [L, 2 * DH], F32, kind="ExternalInput")
    wpr_h = nc.dram_tensor("wpr", [D, D], F32, kind="ExternalInput")
    w1_h = nc.dram_tensor("w1", [FF, D], F32, kind="ExternalInput")
    b1_h = nc.dram_tensor("b1", [1, FF], F32, kind="ExternalInput")
    w2_h = nc.dram_tensor("w2", [D, FF], F32, kind="ExternalInput")
    b2_h = nc.dram_tensor("b2", [1, D], F32, kind="ExternalInput")
    msk_h = nc.dram_tensor("msk", [128, 4 * 512], F32, kind="ExternalInput")
    out_h = nc.dram_tensor("out", [T, D], F32, kind="ExternalOutput")
    spill_h = nc.dram_tensor("ffn_spill", [T, D], BF16, kind="ExternalOutput")

    st = ExitStack()
    sb = lambda name, shape, dt: st.enter_context(nc.sbuf_tensor(name, shape, dt))
    sem = lambda name: st.enter_context(nc.semaphore(name))

    xb = [sb("xb0", [128, D], F32), sb("xb1", [128, D], F32)]
    b2rep = sb("b2rep", [128, D], BF16)
    ones1f = sb("ones1f", [1, 128], F32)
    ones1b = sb("ones1b", [1, 128], BF16)
    onesff = sb("onesff", [128, 128], F32)
    ones64 = sb("ones64", [64, 1], BF16)
    stt = sb("stt", [128, 16 * nt], F32)

    def stc(i):
        return stt[:, i * nt:(i + 1) * nt]

    w1s = sb("w1s", [128, 1], F32)
    mw1s = sb("mw1s", [128, 1], F32)
    w2s = sb("w2s", [128, 1], F32)
    mw2s = sb("mw2s", [128, 1], F32)
    wabs = sb("wabs", [128, 40], F32)
    brow = sb("brow", [1, 512], F32)
    tmp1 = sb("tmp1", [128, 1], F32)
    tmp2 = sb("tmp2", [128, 1], F32)
    tmp3 = sb("tmp3", [128, 128], F32)

    ps = st.enter_context(nc.psum_tensor("ps", [128, 4096], F32))

    SY, V, S, TE, G = nc.sync, nc.vector, nc.scalar, nc.tensor, nc.gpsimd

    class Ctr:
        def __init__(self, name):
            self.h = sem(name)
            self.v = 0

        def inc(self, inst, n=1):
            inst.then_inc(self.h, n)
            self.v += n

        def wait(self, eng, val=None):
            val = self.v if val is None else val
            if val > 0:
                eng.wait_ge(self.h, val)

    class DCtr:

        def __init__(self, name):
            self.h = [sem(name + "a"), sem(name + "b")]
            self.v = [0, 0]
            self.b = 0
            self.cur = None

        def begin(self, eng):
            p = self.b % 2
            self.b += 1
            self.cur = p
            if self.v[p] > 0:
                eng.wait_ge(self.h[p], self.v[p])
            return p

        def issue(self, inst):
            p = self.cur
            inst.then_inc(self.h[p], 16)
            self.v[p] += 16

        def mark(self):
            return (self.cur, self.v[self.cur])

        def wait_mark(self, eng, mk):
            if mk is not None:
                eng.wait_ge(self.h[mk[0]], mk[1])

        def wait_all(self, eng):
            for p in (0, 1):
                if self.v[p] > 0:
                    eng.wait_ge(self.h[p], self.v[p])

    s_xld = DCtr("s_xld")
    s_xfr = Ctr("s_xfr")
    s_xfs = Ctr("s_xfs")
    s_bld = DCtr("s_bld")
    s_mm = Ctr("s_mm")
    s_vec = Ctr("s_vec")
    s_sca = Ctr("s_sca")
    s_wld = DCtr("s_wld")
    s_wfr = Ctr("s_wfr")
    s_tp = DCtr("s_tp")
    s_ff1 = Ctr("s_ff1")
    s_hev = Ctr("s_hev")
    s_ff2 = Ctr("s_ff2")
    s_f2e = Ctr("s_f2e")
    s_sp = DCtr("s_sp")
    s_q1 = Ctr("s_q1")
    s_q2 = Ctr("s_q2")
    s_g = Ctr("s_g")
    s_lat = Ctr("s_lat")
    s_sc = Ctr("s_sc")
    s_msk = Ctr("s_msk")
    s_exp = Ctr("s_exp")
    s_pv = Ctr("s_pv")
    s_ye = Ctr("s_ye")
    s_qp = Ctr("s_qp")
    s_nb = Ctr("s_nb")
    s_out = DCtr("s_out")
    s_fb = DCtr("s_fb")
    s_pr = Ctr("s_pr")
    s_asm = Ctr("s_asm")
    s_kv = Ctr("s_kv")
    s_wc = DCtr("s_wc")

    def vsp():
        V.tensor_scalar(tmp3[:, :], tmp3[:, :], 1.0, None, ALU.mult)

    V.memset(ones1f[:, :], 1.0)
    V.memset(ones1b[:, :], 1.0)
    V.memset(onesff[:, :], 1.0)
    s_vec.inc(V.memset(ones64[:, :], 1.0))

    p1st = ExitStack()
    sb1 = lambda name, shape, dt: p1st.enter_context(nc.sbuf_tensor(name, shape, dt))
    big_a = sb1("big_a", [128, FF], F32)
    t1b = sb1("t1b", [128, D], F32)
    b1rep = sb1("b1rep", [128, FF], BF16)
    w1qT = sb1("w1qT", [128, 8 * FF], BF16)
    w2qT = sb1("w2qT", [128, 32 * D], BF16)
    wq_sb = sb1("wqs0", [128, 1024], BF16)

    for i in range(nt):
        b = xb[i % 2]
        if i >= 2:
            s_xfr.wait(SY, i - 1)
            s_xfs.wait(SY, i - 1)
        s_xld.begin(SY)
        s_xld.issue(SY.dma_start(b[:, :], x_h[i * 128:(i + 1) * 128, :]))
        xmk = s_xld.mark()
        s_xld.wait_mark(V, xmk)
        s_xfr.inc(V.reduce_sum(stc(0)[:, i:i + 1], b[:, :], axis=AX.X))
        V.reduce_max(stc(2)[:, i:i + 1], b[:, :], axis=AX.X)
        V.tensor_reduce(stc(3)[:, i:i + 1], b[:, :], axis=AX.X, op=ALU.min)
        s_xld.wait_mark(S, xmk)
        s_xfs.inc(S.activation(t1b[:, :], b[:, :], AF.Square,
                               accum_out=stc(1)[:, i:i + 1]))

    s_xld.wait_all(V)
    s_xfr.wait(V, nt)
    s_xfs.wait(V, nt)
    inv_d = 1.0 / D
    m = stc(12)
    V.tensor_scalar(m, stc(0), inv_d, None, ALU.mult)
    V.tensor_scalar(stc(13), stc(1), inv_d, None, ALU.mult)
    V.tensor_mul(stc(14), m, m)
    vsp()
    V.tensor_sub(stc(1), stc(13), stc(14))
    vsp()
    s_vec.inc(V.tensor_scalar(stc(15), stc(1), EPS, None, ALU.add))
    s_vec.wait(S)
    s_sca.inc(S.activation(stc(14), stc(15), AF.Ln))
    s_sca.wait(V)
    s_vec.inc(V.tensor_scalar(stc(14), stc(14), -0.5, None, ALU.mult))
    s_vec.wait(S)
    s_sca.inc(S.activation(stc(4), stc(14), AF.Exp))
    s_sca.wait(V)
    V.scalar_tensor_tensor(stc(5), stc(4), -1.0, m, ALU.mult, ALU.mult)
    V.tensor_sub(stc(2), stc(2), m)
    V.tensor_sub(stc(3), m, stc(3))
    vsp()
    V.tensor_max(stc(2), stc(2), stc(3))
    vsp()
    V.tensor_mul(stc(2), stc(2), stc(4))
    V.tensor_mul(stc(14), stc(4), stc(4))
    vsp()
    V.tensor_mul(stc(14), stc(1), stc(14))
    vsp()
    s_vec.inc(V.tensor_scalar(stc(14), stc(14), EPS, None, ALU.add))
    s_vec.wait(S)
    s_sca.inc(S.activation(stc(15), stc(14), AF.Ln))
    s_sca.wait(V)
    s_vec.inc(V.tensor_scalar(stc(15), stc(15), -0.5, None, ALU.mult))
    s_vec.wait(S)
    s_sca.inc(S.activation(stc(3), stc(15), AF.Exp))
    s_sca.wait(V)
    V.tensor_mul(stc(2), stc(2), stc(3))
    vsp()
    s_vec.inc(V.tensor_scalar(stc(2), stc(2), 1e-5, None, ALU.max))
    s_vec.wait(S)
    s_sca.inc(S.activation(stc(15), stc(2), AF.Ln))
    s_sca.wait(V)
    s_vec.inc(V.tensor_scalar(stc(15), stc(15), -1.0, None, ALU.mult))
    s_vec.wait(S)
    s_sca.inc(S.activation(stc(14), stc(15), AF.Exp))
    s_sca.wait(V)
    V.tensor_scalar(stc(14), stc(14), 127.0, None, ALU.mult)
    V.tensor_mul(stc(6), stc(4), stc(3))
    V.tensor_mul(stc(7), stc(5), stc(3))
    V.tensor_mul(stc(6), stc(6), stc(14))
    s_vec.inc(V.tensor_mul(stc(7), stc(7), stc(14)))


    s_vec.wait(TE)
    for c in range(FF // 512 + D // 512):
        if c < FF // 512:
            src_ap = b1_h[0:1, c * 512:(c + 1) * 512]
            dst = b1rep[:, c * 512:(c + 1) * 512]
        else:
            c2 = c - FF // 512
            src_ap = b2_h[0:1, c2 * 512:(c2 + 1) * 512]
            dst = b2rep[:, c2 * 512:(c2 + 1) * 512]
        pslot = ps[:, (c % 2) * 512:(c % 2) * 512 + 512]
        if c >= 1:
            s_mm.wait(SY)
        s_bld.begin(SY)
        s_bld.issue(SY.dma_start(brow[:, :], src_ap))
        s_bld.wait_mark(TE, s_bld.mark())
        if c >= 2:
            s_vec.wait(TE, s_vec.v - 1)
        s_mm.inc(TE.matmul(pslot, ones1f[:, :], brow[:, :], start=True, stop=True))
        s_mm.wait(V)
        s_vec.inc(V.tensor_copy(dst, pslot))


    def w_pass1(w_h, cols, nstr, off):
        for sx in range(nstr):
            b = big_a
            s_wfr.wait(G)
            s_wld.begin(G)
            s_wld.issue(G.dma_start(b[:, 0:cols], w_h[sx * 128:(sx + 1) * 128, :]))
            s_wld.wait_mark(V, s_wld.mark())
            s_wfr.inc(V.reduce_sum(wabs[:, off + sx:off + sx + 1], b[:, 0:cols],
                                   axis=AX.X, apply_absolute_value=True))

    w_pass1(w1_h, D, FF // 128, 0)
    w_pass1(w2_h, FF, D // 128, 32)
    s_wfr.wait(V)
    s_vec.inc(V.engine_nop())
    s_vec.wait(TE)
    s_mm.inc(TE.matmul(ps[:, 2048:2088], onesff[:, :], wabs[:, 0:40],
                       start=True, stop=True))
    s_mm.wait(V)
    V.reduce_sum(w1s[:, :], ps[:, 2048:2080], axis=AX.X)
    V.reduce_sum(w2s[:, :], ps[:, 2080:2088], axis=AX.X)
    for wsx, mwx, nel in ((w1s, mw1s, FF * D), (w2s, mw2s, D * FF)):
        V.tensor_scalar(tmp2[:, :], wsx[:, :], 1.0 / nel, None, ALU.mult)
        vsp()
        s_vec.inc(V.tensor_scalar(tmp2[:, :], tmp2[:, :], 1e-5, None, ALU.max))
        s_vec.wait(S)
        s_sca.inc(S.activation(tmp1[:, :], tmp2[:, :], AF.Copy))
        s_sca.wait(V)
        V.reciprocal(wsx[:, :], tmp1[:, :])
        vsp()
        s_vec.inc(V.tensor_scalar(mwx[:, :], tmp2[:, :], 1.0 / 127.0, None, ALU.mult))
    s_vec.inc(V.engine_nop())
    V.tensor_scalar(stc(8), stc(2), mw1s[:, 0:1], None, ALU.mult)

    def w_pass2(w_h, cols, nstr, s_rec, qT, chunk_cols):
        q = wq_sb
        for sx in range(nstr):
            b = big_a
            s_wfr.wait(G)
            s_wld.begin(G)
            s_wld.issue(G.dma_start(b[:, 0:cols], w_h[sx * 128:(sx + 1) * 128, :]))
            s_wld.wait_mark(V, s_wld.mark())
            V.tensor_scalar(b[:, 0:cols], b[:, 0:cols], s_rec[:, 0:1], MAGIC,
                            ALU.mult, ALU.add)
            V.tensor_scalar(b[:, 0:cols], b[:, 0:cols], MAGIC, 1.0,
                            ALU.subtract, ALU.min)
            for qc in range(cols // 1024):
                s_tp.wait_all(V)
                s_vec.inc(V.tensor_scalar(q[:, :], b[:, 1024 * qc:1024 * qc + 1024],
                                           -1.0, None, ALU.max))
                if qc == cols // 1024 - 1:
                    s_wfr.inc(V.engine_nop())
                s_vec.wait(SY)
                s_tp.begin(SY)
                for k in range(8):
                    kk = 8 * qc + k
                    s_tp.issue(SY.dma_start(
                        qT[:, chunk_cols * kk + 128 * sx:chunk_cols * kk + 128 * sx + 128],
                        q[:, 128 * k:128 * k + 128], transpose=True))

    w_pass2(w1_h, D, FF // 128, w1s, w1qT, FF)
    w_pass2(w2_h, FF, D // 128, w2s, w2qT, D)

    xq1 = sb1("xq1", [128, D], BF16)
    xq1T = sb1("xq1T", [128, D], BF16)
    hq = sb1("hq", [128, D], BF16)
    hqT = sb1("hqT", [128, FF], BF16)
    spst = [sb1("spst0", [128, D], BF16)]
    sp_marks = []

    for i in range(nt):
        b = xb[i % 2]
        if i == 0:
            s_xfr.wait(SY)
            s_xfs.wait(SY)
            base_xfr = s_xfr.v
        else:
            s_xfr.wait(SY, base_xfr + i - 1)
        s_xld.begin(SY)
        s_xld.issue(SY.dma_start(b[:, :], x_h[i * 128:(i + 1) * 128, :]))
        s_xld.wait_mark(V, s_xld.mark())
        s_xfr.inc(V.tensor_scalar(t1b[:, :], b[:, :], stc(6)[:, i:i + 1],
                                  stc(7)[:, i:i + 1], ALU.mult, ALU.add))
        if i > 0:
            s_tp.wait_mark(V, xq1_mk)
        s_q1.inc(V.tensor_scalar(xq1[:, :], t1b[:, :], MAGIC, MAGIC,
                                 ALU.add, ALU.subtract))
        s_q1.wait(SY)
        if i > 0:
            s_ff1.wait(SY)
        s_tp.begin(SY)
        for k in range(8):
            s_tp.issue(SY.dma_start(xq1T[:, 128 * k:128 * k + 128],
                                    xq1[:, 128 * k:128 * k + 128], transpose=True))
        xq1_mk = s_tp.mark()
        s_tp.wait_all(TE)
        for fc in range(8):
            pslot = ps[:, (fc % 2) * 512:(fc % 2) * 512 + 512]
            s_hev.wait(TE, s_hev.v - 1)
            for dc in range(8):
                mm = TE.matmul(pslot, xq1T[:, 128 * dc:128 * dc + 128],
                               w1qT[:, FF * dc + 512 * fc:FF * dc + 512 * fc + 512],
                               start=(dc == 0), stop=(dc == 7))
            s_ff1.inc(mm)
            s_ff1.wait(V)
            s_hev.inc(V.scalar_tensor_tensor(
                big_a[:, 512 * fc:512 * fc + 512], pslot, stc(8)[:, i:i + 1],
                b1rep[:, 512 * fc:512 * fc + 512], ALU.mult, ALU.add))
        s_hev.wait(S)
        if i > 0:
            s_ff2.wait(S)
        s_g.inc(S.activation(big_a[:, :], big_a[:, :], AF.Gelu))
        s_g.wait(V)
        s_vec.inc(V.reduce_max(stc(13)[:, i:i + 1], big_a[:, :], axis=AX.X,
                               apply_absolute_value=True))
        s_vec.wait(S)
        s_sca.inc(S.activation(hqT[:, :], big_a[:, :], AF.Square,
                               accum_out=stc(9)[:, i:i + 1]))
        s_sca.inc(S.activation(stc(15)[:, i:i + 1], stc(13)[:, i:i + 1], AF.Copy))
        s_sca.wait(V)
        V.reciprocal(stc(14)[:, i:i + 1], stc(15)[:, i:i + 1])
        V.tensor_scalar(stc(10)[:, i:i + 1], stc(13)[:, i:i + 1], mw2s[:, 0:1],
                        None, ALU.mult)
        vsp()
        s_vec.inc(V.tensor_scalar(stc(14)[:, i:i + 1], stc(14)[:, i:i + 1],
                                  127.0, None, ALU.mult))
        s_vec.wait(S)
        s_sca.inc(S.activation(big_a[:, :], big_a[:, :], AF.Copy,
                               scale=stc(14)[:, i:i + 1]))
        s_sca.wait(V)
        hq_mk = None
        for kc in range(4):
            s_tp.wait_mark(V, hq_mk)
            s_q2.inc(V.tensor_scalar(hq[:, :], big_a[:, 1024 * kc:1024 * kc + 1024],
                                     MAGIC, MAGIC, ALU.add, ALU.subtract))
            s_q2.wait(SY)
            if i > 0 and kc == 0:
                s_ff2.wait(SY)
            s_tp.begin(SY)
            for k in range(8):
                s_tp.issue(SY.dma_start(
                    hqT[:, 1024 * kc + 128 * k:1024 * kc + 128 * k + 128],
                    hq[:, 128 * k:128 * k + 128], transpose=True))
            hq_mk = s_tp.mark()
        s_tp.wait_all(TE)
        s_f2e.wait(TE)
        sp = spst[0]
        for dc2 in range(2):
            pslot = ps[:, 1024 + dc2 * 512:1024 + dc2 * 512 + 512]
            for fc in range(32):
                mm = TE.matmul(pslot, hqT[:, 128 * fc:128 * fc + 128],
                               w2qT[:, D * fc + 512 * dc2:D * fc + 512 * dc2 + 512],
                               start=(fc == 0), stop=(fc == 31))
            s_ff2.inc(mm)
            s_ff2.wait(V)
            if i >= 1 and dc2 == 0:
                s_sp.wait_mark(V, sp_marks[i - 1])
            s_f2e.inc(V.tensor_scalar(sp[:, 512 * dc2:512 * dc2 + 512], pslot,
                                      stc(10)[:, i:i + 1], None, ALU.mult))
        s_f2e.wait(G)
        s_sp.begin(G)
        s_sp.issue(G.dma_start(spill_h[i * 128:(i + 1) * 128, :], sp[:, :]))
        sp_marks.append(s_sp.mark())

    p1st.close()
    for eng in (G, SY, V, S, TE):
        s_sp.wait_all(eng)
    V.tensor_scalar(stc(15), stc(9), 1.0 / FF, EPS, ALU.mult, ALU.add)
    s_vec.inc(V.engine_nop())
    s_vec.wait(S)
    S.activation(stc(15), stc(15), AF.Ln)
    s_sca.inc(S.activation(stc(15), stc(15), AF.Exp, scale=0.5))
    s_sca.wait(V)
    s_vec.inc(V.reciprocal(stc(11), stc(15)))

    wcd_bf = sb("wcd_bf", [128, 8 * 512], BF16)
    wq_bf = sb("wq_bf", [128, 4 * 1024], BF16)
    wkv_bf = sb("wkv_bf", [128, 4 * 128], BF16)
    wpr_bf = sb("wpr_bf", [128, 8 * 1024], BF16)
    wtmp = sb("wtmp", [128, 1024], F32)

    def load_cast(src_h, nchunk, ccols, dst, scale=None):
        for c in range(nchunk):
            s_vec.wait(G)
            s_wc.begin(G)
            s_wc.issue(G.dma_start(wtmp[:, 0:ccols], src_h[c * 128:(c + 1) * 128, :]))
            s_wc.wait_mark(V, s_wc.mark())
            if scale is None:
                s_vec.inc(V.tensor_copy(dst[:, c * ccols:(c + 1) * ccols],
                                        wtmp[:, 0:ccols]))
            else:
                s_vec.inc(V.tensor_scalar(dst[:, c * ccols:(c + 1) * ccols],
                                          wtmp[:, 0:ccols], scale, None, ALU.mult))

    load_cast(wcd_h, 8, 512, wcd_bf)
    load_cast(wq_h, 4, 1024, wq_bf, scale=0.125)
    load_cast(wkv_h, 4, 128, wkv_bf)
    load_cast(wpr_h, 8, 1024, wpr_bf)
    wts_ready = s_vec.v
    mask4 = sb("mask4", [128, 2048], F32)
    s_bld.begin(SY)
    s_bld.issue(SY.dma_start(mask4[:, :], msk_h[:, :]))
    mask_mk = s_bld.mark()

    outst = [sb("outst0", [128, D], F32), sb("outst1", [128, D], F32)]
    fbk = [sb("fbk0", [128, D], BF16), sb("fbk1", [128, D], BF16)]
    kT64 = sb("kT64", [64, T], BF16)
    vext = sb("vext", [128, 66 * nt], BF16)
    latT = sb("latT", [128, 4 * T], BF16)
    xnbf = sb("xnbf", [128, D], BF16)
    xnT = sb("xnT", [128, 8 * 512], BF16)
    q64 = sb("q64", [64, 16 * 512], BF16)
    sq64 = sb("sq64", [64, 16 * 512], BF16)
    arow = sb("arow", [1, 512], BF16)
    amaxv = sb("amaxv", [128, 1], F32)
    nbias = sb("nbias", [128, 1], F32)
    knt = sb("knt", [128, 1], F32)
    kn2t = sb("kn2t", [128, 1], F32)
    sqk = sb("sqk", [64, 512], BF16)
    pbf = [sb("pbf0", [128, 1024], BF16), sb("pbf1", [128, 1024], BF16)]
    ybuf = sb("ybuf", [128, 4 * D], BF16)
    yT = sb("yT", [128, 8 * 512], BF16)
    drb = sb("drb", [128, 4], F32)

    s_vec.inc(V.memset(vext[:, :], 1.0))
    vext_ones = s_vec.v

    lat_marks = []
    for I in range(ns):
        for t4 in range(4):
            i = 4 * I + t4
            b = xb[i % 2]
            s_xfr.wait(SY, s_xfr.v - (1 if i > 0 else 0))
            s_xld.begin(SY)
            s_xld.issue(SY.dma_start(b[:, :], x_h[i * 128:(i + 1) * 128, :]))
            s_xld.wait_mark(V, s_xld.mark())
            if i > 0:
                s_tp.wait_mark(V, xn_mk)
            s_xfr.inc(V.tensor_scalar(xnbf[:, :], b[:, :], stc(4)[:, i:i + 1],
                                      stc(5)[:, i:i + 1], ALU.mult, ALU.add))
            s_xfr.wait(SY)
            if I > 0 and t4 == 0:
                s_mm.wait(SY, lat_marks[-1])
            s_tp.begin(SY)
            for k in range(8):
                s_tp.issue(SY.dma_start(
                    xnT[:, 512 * k + 128 * t4:512 * k + 128 * t4 + 128],
                    xnbf[:, 128 * k:128 * k + 128], transpose=True))
            xn_mk = s_tp.mark()
        s_tp.wait_all(TE)
        s_vec.wait(TE, wts_ready)
        for lc in range(4):
            pslot = ps[:, 3584:4096]
            if not (I == 0 and lc == 0):
                s_lat.wait(TE)
            for dc in range(8):
                mm = TE.matmul(pslot,
                               wcd_bf[:, 512 * dc + 128 * lc:512 * dc + 128 * lc + 128],
                               xnT[:, 512 * dc:512 * dc + 512],
                               start=(dc == 0), stop=(dc == 7))
            s_mm.inc(mm)
            s_mm.wait(S)
            s_lat.inc(S.activation(latT[:, T * lc + 512 * I:T * lc + 512 * I + 512],
                                   pslot, AF.Copy))
        lat_marks.append(s_mm.v)

    s_vec.wait(S, vext_ones)
    for I in range(ns):
        s_kv.wait(TE)
        s_lat.wait(TE)
        for lc in range(4):
            mm = TE.matmul(ps[:, 3584:4096], wkv_bf[:, 128 * lc:128 * lc + 128],
                           latT[:, T * lc + 512 * I:T * lc + 512 * I + 512],
                           start=(lc == 0), stop=(lc == 3))
        s_mm.inc(mm)
        s_mm.wait(V)
        s_kv.inc(V.tensor_copy(kT64[:, 512 * I:512 * I + 512], ps[0:64, 3584:4096]))
    for j in range(nt):
        s_kv.wait(TE)
        for lc in range(4):
            mm = TE.matmul(ps[:, 3584:3712],
                           latT[:, T * lc + 128 * j:T * lc + 128 * j + 128],
                           wkv_bf[:, 128 * lc:128 * lc + 128],
                           start=(lc == 0), stop=(lc == 3))
        s_mm.inc(mm)
        s_mm.wait(V)
        s_kv.inc(V.tensor_copy(vext[:, 66 * j:66 * j + 64], ps[:, 3648:3712]))
    V.memset(knt[:, :], 0.0)
    s_vec.inc(V.engine_nop())
    for I in range(ns):
        s_mm.wait(S)
        s_kv.wait(S)
        s_sca.inc(S.activation(sqk[:, :], kT64[:, 512 * I:512 * I + 512], AF.Square))
        s_sca.wait(TE)
        s_vec.wait(TE)
        pslot = ps[0:1, 3584:4096]
        s_mm.inc(TE.matmul(pslot, ones64[:, :], sqk[:, :], start=True, stop=True))
        s_mm.wait(V)
        V.tensor_copy(arow[:, :], pslot)
        s_vec.inc(V.engine_nop())
        s_vec.wait(TE)
        s_mm.inc(TE.matmul(ps[:, 3072:3584], ones1b[:, :], arow[:, :],
                           start=True, stop=True))
        s_mm.wait(V)
        V.reduce_max(kn2t[:, :], ps[:, 3072:3584], axis=AX.X)
        vsp()
        s_vec.inc(V.tensor_max(knt[:, :], knt[:, :], kn2t[:, :]))
    knt_ready = s_vec.v

    out_marks = []
    gpar = 0
    psum_free = {0: 0, 1: 0}
    pbf_free = {0: 0, 1: 0}
    for I in range(ns):
        nblk = 4 * I + 4
        ngrp = (nblk + 1) // 2
        s_asm.wait(TE)
        for h in range(H):
            pslot = ps[0:64, 3584:4096]
            if not (I == 0 and h == 0):
                s_qp.wait(TE)
            for lc in range(4):
                mm = TE.matmul(pslot,
                               wq_bf[:, 1024 * lc + 64 * h:1024 * lc + 64 * h + 64],
                               latT[:, T * lc + 512 * I:T * lc + 512 * I + 512],
                               start=(lc == 0), stop=(lc == 3))
            s_mm.inc(mm)
            s_mm.wait(V)
            s_qp.inc(V.tensor_copy(q64[:, 512 * h:512 * h + 512], pslot))
        s_qp.wait(V)
        s_vec.inc(V.tensor_mul(sq64[:, :], q64[:, :], q64[:, :]))
        s_vec.wait(TE)
        for h in range(H):
            mm = TE.matmul(ps[0:1, 3072:3584], ones64[:, :],
                           sq64[:, 512 * h:512 * h + 512],
                           start=(h == 0), stop=(h == H - 1))
        s_mm.inc(mm)
        s_mm.wait(V)
        s_vec.inc(V.tensor_copy(arow[:, :], ps[0:1, 3072:3584]))
        s_vec.wait(TE)
        s_mm.inc(TE.matmul(ps[:, 3584:4096], ones1b[:, :], arow[:, :],
                           start=True, stop=True))
        s_mm.wait(V)
        V.reduce_max(amaxv[:, :], ps[:, 3584:4096], axis=AX.X)
        vsp()
        s_vec.inc(V.tensor_scalar(amaxv[:, :], amaxv[:, :], 1e-30, None, ALU.max))
        s_vec.wait(S)
        S.activation(amaxv[:, :], amaxv[:, :], AF.Ln, scale=knt[:, 0:1])
        s_sca.inc(S.activation(amaxv[:, :], amaxv[:, :], AF.Exp, scale=0.5))
        s_sca.wait(V)
        s_nb.inc(V.tensor_scalar(nbias[:, :], amaxv[:, :], -1.0, 35.0,
                                 ALU.mult, ALU.add))
        nb_ready = s_nb.v
        for h in range(H):
            exp_at = {}
            par = {}
            sc_at = {}

            def emit_pv(gx):
                s_exp.wait(TE, exp_at[gx])
                if gx == 0:
                    s_ye.wait(TE)
                j0v = 2 * gx
                jnv = min(2, nblk - j0v)
                last = None
                for jj in range(jnv):
                    j = j0v + jj
                    for ti in range(4):
                        if j <= 4 * I + ti:
                            last = TE.matmul(
                                ps[:, 2048 + 512 * ti:2048 + 512 * ti + 65],
                                pbf[par[gx]][:, 512 * jj + 128 * ti:
                                             512 * jj + 128 * ti + 128],
                                vext[:, 66 * j:66 * j + 65],
                                start=(j == 0), stop=(j == 4 * I + ti))
                s_pv.inc(last)
                pbf_free[par[gx]] = s_pv.v

            for g in range(ngrp):
                pp = gpar % 2
                gpar += 1
                par[g] = pp
                j0 = 2 * g
                jn = min(2, nblk - j0)
                s_exp.wait(TE, psum_free[pp])
                for jj in range(jn):
                    j = j0 + jj
                    s_sc.inc(TE.matmul(
                        ps[:, 1024 * pp + 512 * jj:1024 * pp + 512 * jj + 512],
                        kT64[:, 128 * j:128 * j + 128],
                        q64[:, 512 * h:512 * h + 512],
                        start=True, stop=True))
                sc_at[g] = s_sc.v
                nmsk = 0
                for jj in range(jn):
                    di = j0 + jj - 4 * I
                    if di >= 0:
                        if nmsk == 0:
                            s_sc.wait(V, sc_at[g])
                            s_bld.wait_mark(V, mask_mk)
                        pvw = ps[:, 1024 * pp + 512 * jj:
                                 1024 * pp + 512 * jj + 128 * (di + 1)]
                        s_msk.inc(V.tensor_tensor(
                            pvw, pvw,
                            mask4[:, 512 * di:512 * di + 128 * (di + 1)], ALU.add))
                        nmsk += 1
                if g >= 1:
                    emit_pv(g - 1)
                s_sc.wait(S, sc_at[g])
                if nmsk:
                    s_msk.wait(S)
                s_pv.wait(S, pbf_free[pp])
                s_nb.wait(S, nb_ready)
                s_exp.inc(S.activation(pbf[pp][:, 0:512 * jn],
                                       ps[:, 1024 * pp:1024 * pp + 512 * jn],
                                       AF.Exp, bias=nbias[:, 0:1]))
                exp_at[g] = s_exp.v
                psum_free[pp] = s_exp.v
            emit_pv(ngrp - 1)
            s_pv.wait(V)
            for ti in range(4):
                V.reciprocal(drb[:, ti:ti + 1],
                             ps[:, 2048 + 512 * ti + 64:2048 + 512 * ti + 65])
            vsp()
            for ti in range(4):
                s_ye.inc(V.tensor_scalar(
                    ybuf[:, D * ti + 64 * h:D * ti + 64 * h + 64],
                    ps[:, 2048 + 512 * ti:2048 + 512 * ti + 64],
                    drb[:, ti:ti + 1], None, ALU.mult))
        s_ye.wait(SY)
        if I > 0:
            s_pr.wait(SY)
        s_tp.begin(SY)
        for ti in range(4):
            for c in range(8):
                s_tp.issue(SY.dma_start(
                    yT[:, 512 * c + 128 * ti:512 * c + 128 * ti + 128],
                    ybuf[:, D * ti + 128 * c:D * ti + 128 * c + 128],
                    transpose=True))
        s_tp.wait_all(TE)
        for ti in range(4):
            i = 4 * I + ti
            b = xb[i % 2]
            fb = fbk[i % 2]
            s_xfr.wait(SY, s_xfr.v - (1 if i > 0 else 0))
            s_xld.begin(SY)
            s_xld.issue(SY.dma_start(b[:, :], x_h[i * 128:(i + 1) * 128, :]))
            asm_xmk = s_xld.mark()
            if i >= 2:
                s_asm.wait(G, s_asm.v - 2)
            s_fb.begin(G)
            s_fb.issue(G.dma_start(fb[:, :], spill_h[i * 128:(i + 1) * 128, :]))
            fb_mk = s_fb.mark()
            for half in range(2):
                pslot = ps[:, 3072 + 512 * half:3072 + 512 * half + 512]
                if not (I == 0 and ti == 0):
                    s_asm.wait(TE)
                for c in range(8):
                    mm = TE.matmul(
                        pslot, yT[:, 512 * c + 128 * ti:512 * c + 128 * ti + 128],
                        wpr_bf[:, 1024 * c + 512 * half:1024 * c + 512 * half + 512],
                        start=(c == 0), stop=(c == 7))
                s_pr.inc(mm)
            o = outst[i % 2]
            s_pr.wait(V)
            s_xld.wait_mark(V, asm_xmk)
            s_fb.wait_mark(V, fb_mk)
            if i >= 2:
                s_out.wait_mark(V, out_marks[i - 2])
            for half in range(2):
                pslot = ps[:, 3072 + 512 * half:3072 + 512 * half + 512]
                sl = slice(512 * half, 512 * half + 512)
                V.tensor_add(o[:, sl], pslot, b[:, sl])
                V.scalar_tensor_tensor(o[:, sl], fb[:, sl], stc(11)[:, i:i + 1],
                                       o[:, sl], ALU.mult, ALU.add)
                s_asm.inc(V.tensor_add(o[:, sl], o[:, sl], b2rep[:, sl]))
            s_xfr.inc(V.engine_nop())
            s_asm.wait(G)
            s_out.begin(G)
            s_out.issue(G.dma_start(out_h[i * 128:(i + 1) * 128, :], o[:, :]))
            out_marks.append(s_out.mark())

    s_out.wait_all(G)
    st.close()
    return nc


def _mask4_np():
    m = np.zeros((128, 4 * 512), dtype=np.float32)
    r = np.arange(128)
    for di in range(4):
        blk = m[:, 512 * di:512 * di + 512]
        blk[:, 0:128 * di] = NEG
        tri = blk[:, 128 * di:128 * di + 128]
        tri[r[:, None] > r[None, :]] = NEG
    return m


def kernel(L0_attn_x, L0_attn_Wc_down, L0_attn_Wq_up, L0_attn_Wkv_up,
           L0_attn_Wc_proj, L0_ffn_w1, L0_ffn_b1, L0_ffn_w2, L0_ffn_b2):
    T = 4096
    nc = build_nc(T)
    x = np.ascontiguousarray(np.asarray(L0_attn_x, dtype=np.float32))
    shared = {
        "wcd": np.ascontiguousarray(np.asarray(L0_attn_Wc_down, dtype=np.float32)),
        "wq": np.ascontiguousarray(np.asarray(L0_attn_Wq_up, dtype=np.float32)),
        "wkv": np.ascontiguousarray(np.asarray(L0_attn_Wkv_up, dtype=np.float32)),
        "wpr": np.ascontiguousarray(np.asarray(L0_attn_Wc_proj, dtype=np.float32)),
        "w1": np.ascontiguousarray(np.asarray(L0_ffn_w1, dtype=np.float32)),
        "b1": np.ascontiguousarray(np.asarray(L0_ffn_b1, dtype=np.float32)).reshape(1, -1),
        "w2": np.ascontiguousarray(np.asarray(L0_ffn_w2, dtype=np.float32)),
        "b2": np.ascontiguousarray(np.asarray(L0_ffn_b2, dtype=np.float32)).reshape(1, -1),
        "msk": _mask4_np(),
    }
    in_maps = [dict(shared, x=x[c]) for c in range(8)]
    res = run_bass_kernel_spmd(nc, in_maps, core_ids=list(range(8)))
    return np.stack([np.asarray(res.results[c]["out"]) for c in range(8)], axis=0)


# version 06
# speedup vs baseline: 1.0438x; 1.0428x over previous
import numpy as np
import ml_dtypes
from contextlib import ExitStack

import concourse.bass as bass
import concourse.mybir as mybir
from concourse.bass_utils import run_bass_kernel_spmd

F32 = mybir.dt.float32
BF16 = mybir.dt.bfloat16
AF = mybir.ActivationFunctionType
ALU = mybir.AluOpType
AX = mybir.AxisListType

MAGIC = 12582912.0
NEG = -1.0e30
D = 1024
L = 512
FF = 4096
H = 16
DH = 64
EPS = 1e-5


def build_nc(T=4096):
    nc = bass.Bass("TRN2", target_bir_lowering=False, detect_race_conditions=False)
    nt = T // 128
    ns = T // 512

    x_h = nc.dram_tensor("x", [T, D], F32, kind="ExternalInput")
    wcd_h = nc.dram_tensor("wcd", [D, L], F32, kind="ExternalInput")
    wq_h = nc.dram_tensor("wq", [L, D], F32, kind="ExternalInput")
    wkv_h = nc.dram_tensor("wkv", [L, 2 * DH], F32, kind="ExternalInput")
    wpr_h = nc.dram_tensor("wpr", [D, D], F32, kind="ExternalInput")
    w1_h = nc.dram_tensor("w1", [FF, D], F32, kind="ExternalInput")
    b1_h = nc.dram_tensor("b1", [1, FF], F32, kind="ExternalInput")
    w2_h = nc.dram_tensor("w2", [D, FF], F32, kind="ExternalInput")
    b2_h = nc.dram_tensor("b2", [1, D], F32, kind="ExternalInput")
    msk_h = nc.dram_tensor("msk", [128, 4 * 512], F32, kind="ExternalInput")
    out_h = nc.dram_tensor("out", [T, D], F32, kind="ExternalOutput")
    spill_h = nc.dram_tensor("ffn_spill", [T, D], BF16, kind="ExternalOutput")

    st = ExitStack()
    sb = lambda name, shape, dt: st.enter_context(nc.sbuf_tensor(name, shape, dt))
    sem = lambda name: st.enter_context(nc.semaphore(name))

    xb = [sb("xb0", [128, D], F32), sb("xb1", [128, D], F32)]
    b2rep = sb("b2rep", [128, D], BF16)
    ones1f = sb("ones1f", [1, 128], F32)
    ones1b = sb("ones1b", [1, 128], BF16)
    onesff = sb("onesff", [128, 128], F32)
    ones64 = sb("ones64", [64, 1], BF16)
    stt = sb("stt", [128, 16 * nt], F32)

    def stc(i):
        return stt[:, i * nt:(i + 1) * nt]

    w1s = sb("w1s", [128, 1], F32)
    mw1s = sb("mw1s", [128, 1], F32)
    w2s = sb("w2s", [128, 1], F32)
    mw2s = sb("mw2s", [128, 1], F32)
    wabs = sb("wabs", [128, 40], F32)
    brow = sb("brow", [1, 512], F32)
    tmp1 = sb("tmp1", [128, 1], F32)
    tmp2 = sb("tmp2", [128, 1], F32)
    tmp3 = sb("tmp3", [128, 128], F32)

    ps = st.enter_context(nc.psum_tensor("ps", [128, 4096], F32))

    SY, V, S, TE, G = nc.sync, nc.vector, nc.scalar, nc.tensor, nc.gpsimd

    class Ctr:
        def __init__(self, name):
            self.h = sem(name)
            self.v = 0

        def inc(self, inst, n=1):
            inst.then_inc(self.h, n)
            self.v += n

        def wait(self, eng, val=None):
            val = self.v if val is None else val
            if val > 0:
                eng.wait_ge(self.h, val)

    class DCtr:

        def __init__(self, name):
            self.h = [sem(name + "a"), sem(name + "b")]
            self.v = [0, 0]
            self.b = 0
            self.cur = None

        def __init_serial__(self):
            pass

        def begin(self, eng):
            p = self.b % 2
            self.b += 1
            self.cur = p
            if getattr(self, "serial", False):
                for pp in (0, 1):
                    if self.v[pp] > 0:
                        eng.wait_ge(self.h[pp], self.v[pp])
            elif self.v[p] > 0:
                eng.wait_ge(self.h[p], self.v[p])
            return p

        def issue(self, inst):
            p = self.cur
            inst.then_inc(self.h[p], 16)
            self.v[p] += 16

        def mark(self):
            return (self.cur, self.v[self.cur])

        def wait_mark(self, eng, mk):
            if mk is not None:
                eng.wait_ge(self.h[mk[0]], mk[1])

        def wait_all(self, eng):
            for p in (0, 1):
                if self.v[p] > 0:
                    eng.wait_ge(self.h[p], self.v[p])

    s_xld = DCtr("s_xld")
    s_xfr = Ctr("s_xfr")
    s_xfs = Ctr("s_xfs")
    s_bld = DCtr("s_bld")
    s_mm = Ctr("s_mm")
    s_vec = Ctr("s_vec")
    s_sca = Ctr("s_sca")
    s_wld = DCtr("s_wld")
    s_wfr = Ctr("s_wfr")
    s_tp = DCtr("s_tp")
    s_tp.serial = True
    s_ff1 = Ctr("s_ff1")
    s_hev = Ctr("s_hev")
    s_ff2 = Ctr("s_ff2")
    s_f2e = Ctr("s_f2e")
    s_sp = DCtr("s_sp")
    s_q1 = Ctr("s_q1")
    s_q2 = Ctr("s_q2")
    s_g = Ctr("s_g")
    s_lat = Ctr("s_lat")
    s_sc = Ctr("s_sc")
    s_msk = Ctr("s_msk")
    s_exp = Ctr("s_exp")
    s_pv = Ctr("s_pv")
    s_ye = Ctr("s_ye")
    s_qp = Ctr("s_qp")
    s_nb = Ctr("s_nb")
    s_out = DCtr("s_out")
    s_fb = DCtr("s_fb")
    s_pr = Ctr("s_pr")
    s_asm = Ctr("s_asm")
    s_kv = Ctr("s_kv")
    s_wc = DCtr("s_wc")

    def vsp():
        V.tensor_scalar(tmp3[:, :], tmp3[:, :], 1.0, None, ALU.mult)

    V.memset(tmp3[:, :], 0.0)
    V.memset(tmp1[:, :], 0.0)
    V.memset(ones1f[:, :], 1.0)
    V.memset(ones1b[:, :], 1.0)
    V.memset(onesff[:, :], 1.0)
    s_vec.inc(V.memset(ones64[:, :], 1.0))

    p1st = ExitStack()
    sb1 = lambda name, shape, dt: p1st.enter_context(nc.sbuf_tensor(name, shape, dt))
    t1b = sb1("t1b", [128, D], F32)
    b1rep = sb1("b1rep", [128, FF], BF16)
    w1qT = sb1("w1qT", [128, 8 * FF], BF16)
    w2qT = sb1("w2qT", [128, 32 * D], BF16)
    p0wst = ExitStack()
    big_a = p0wst.enter_context(nc.sbuf_tensor("big_a", [128, FF], F32))
    wq_sb = p0wst.enter_context(nc.sbuf_tensor("wqs0", [128, 1024], BF16))

    for i in range(nt):
        b = xb[i % 2]
        if i >= 2:
            s_xfr.wait(SY, i - 1)
            s_xfs.wait(SY, i - 1)
        s_xld.begin(SY)
        s_xld.issue(SY.dma_start(b[:, :], x_h[i * 128:(i + 1) * 128, :]))
        xmk = s_xld.mark()
        s_xld.wait_mark(V, xmk)
        s_xfr.inc(V.reduce_sum(stc(0)[:, i:i + 1], b[:, :], axis=AX.X))
        V.reduce_max(stc(2)[:, i:i + 1], b[:, :], axis=AX.X)
        V.tensor_reduce(stc(3)[:, i:i + 1], b[:, :], axis=AX.X, op=ALU.min)
        s_xld.wait_mark(S, xmk)
        s_xfs.inc(S.activation(t1b[:, :], b[:, :], AF.Square,
                               accum_out=stc(1)[:, i:i + 1]))

    s_xld.wait_all(V)
    s_xfr.wait(V, nt)
    s_xfs.wait(V, nt)
    inv_d = 1.0 / D
    m = stc(12)
    V.tensor_scalar(m, stc(0), inv_d, None, ALU.mult)
    V.tensor_scalar(stc(13), stc(1), inv_d, None, ALU.mult)
    V.tensor_mul(stc(14), m, m)
    vsp()
    V.tensor_sub(stc(1), stc(13), stc(14))
    vsp()
    s_vec.inc(V.tensor_scalar(stc(15), stc(1), EPS, None, ALU.add))
    s_vec.wait(S)
    s_sca.inc(S.activation(stc(14), stc(15), AF.Ln))
    s_sca.wait(V)
    s_vec.inc(V.tensor_scalar(stc(14), stc(14), -0.5, None, ALU.mult))
    s_vec.wait(S)
    s_sca.inc(S.activation(stc(4), stc(14), AF.Exp))
    s_sca.wait(V)
    V.scalar_tensor_tensor(stc(5), stc(4), -1.0, m, ALU.mult, ALU.mult)
    V.tensor_sub(stc(2), stc(2), m)
    V.tensor_sub(stc(3), m, stc(3))
    vsp()
    V.tensor_max(stc(2), stc(2), stc(3))
    vsp()
    V.tensor_mul(stc(2), stc(2), stc(4))
    V.tensor_mul(stc(14), stc(4), stc(4))
    vsp()
    V.tensor_mul(stc(14), stc(1), stc(14))
    vsp()
    s_vec.inc(V.tensor_scalar(stc(14), stc(14), EPS, None, ALU.add))
    s_vec.wait(S)
    s_sca.inc(S.activation(stc(15), stc(14), AF.Ln))
    s_sca.wait(V)
    s_vec.inc(V.tensor_scalar(stc(15), stc(15), -0.5, None, ALU.mult))
    s_vec.wait(S)
    s_sca.inc(S.activation(stc(3), stc(15), AF.Exp))
    s_sca.wait(V)
    V.tensor_mul(stc(2), stc(2), stc(3))
    vsp()
    s_vec.inc(V.tensor_scalar(stc(2), stc(2), 1e-5, None, ALU.max))
    s_vec.wait(S)
    s_sca.inc(S.activation(stc(15), stc(2), AF.Ln))
    s_sca.wait(V)
    s_vec.inc(V.tensor_scalar(stc(15), stc(15), -1.0, None, ALU.mult))
    s_vec.wait(S)
    s_sca.inc(S.activation(stc(14), stc(15), AF.Exp))
    s_sca.wait(V)
    V.tensor_scalar(stc(14), stc(14), 127.0, None, ALU.mult)
    V.tensor_mul(stc(6), stc(4), stc(3))
    V.tensor_mul(stc(7), stc(5), stc(3))
    V.tensor_mul(stc(6), stc(6), stc(14))
    s_vec.inc(V.tensor_mul(stc(7), stc(7), stc(14)))


    s_vec.wait(TE)
    for c in range(FF // 512 + D // 512):
        if c < FF // 512:
            src_ap = b1_h[0:1, c * 512:(c + 1) * 512]
            dst = b1rep[:, c * 512:(c + 1) * 512]
        else:
            c2 = c - FF // 512
            src_ap = b2_h[0:1, c2 * 512:(c2 + 1) * 512]
            dst = b2rep[:, c2 * 512:(c2 + 1) * 512]
        pslot = ps[:, (c % 2) * 512:(c % 2) * 512 + 512]
        if c >= 1:
            s_mm.wait(SY)
        s_bld.begin(SY)
        s_bld.issue(SY.dma_start(brow[:, :], src_ap))
        s_bld.wait_mark(TE, s_bld.mark())
        if c >= 2:
            s_vec.wait(TE, s_vec.v - 1)
        s_mm.inc(TE.matmul(pslot, ones1f[:, :], brow[:, :], start=True, stop=True))
        s_mm.wait(V)
        s_vec.inc(V.tensor_copy(dst, pslot))


    def w_pass1(w_h, cols, nstr, off):
        for sx in range(nstr):
            b = big_a
            s_wfr.wait(G)
            s_wld.begin(G)
            s_wld.issue(G.dma_start(b[:, 0:cols], w_h[sx * 128:(sx + 1) * 128, :]))
            s_wld.wait_mark(V, s_wld.mark())
            s_wfr.inc(V.reduce_sum(wabs[:, off + sx:off + sx + 1], b[:, 0:cols],
                                   axis=AX.X, apply_absolute_value=True))

    w_pass1(w1_h, D, FF // 128, 0)
    w_pass1(w2_h, FF, D // 128, 32)
    s_wfr.wait(V)
    s_vec.inc(V.engine_nop())
    s_vec.wait(TE)
    s_mm.inc(TE.matmul(ps[:, 2048:2088], onesff[:, :], wabs[:, 0:40],
                       start=True, stop=True))
    s_mm.wait(V)
    V.reduce_sum(w1s[:, :], ps[:, 2048:2080], axis=AX.X)
    V.reduce_sum(w2s[:, :], ps[:, 2080:2088], axis=AX.X)
    for wsx, mwx, nel in ((w1s, mw1s, FF * D), (w2s, mw2s, D * FF)):
        V.tensor_scalar(tmp2[:, :], wsx[:, :], 1.0 / nel, None, ALU.mult)
        vsp()
        s_vec.inc(V.tensor_scalar(tmp2[:, :], tmp2[:, :], 1e-5, None, ALU.max))
        s_vec.wait(S)
        s_sca.inc(S.activation(tmp1[:, :], tmp2[:, :], AF.Copy))
        s_sca.wait(V)
        V.reciprocal(wsx[:, :], tmp1[:, :])
        vsp()
        s_vec.inc(V.tensor_scalar(mwx[:, :], tmp2[:, :], 1.0 / 127.0, None, ALU.mult))
    s_vec.inc(V.engine_nop())
    V.tensor_scalar(stc(8), stc(2), mw1s[:, 0:1], None, ALU.mult)

    def w_pass2(w_h, cols, nstr, s_rec, qT, chunk_cols):
        q = wq_sb
        for sx in range(nstr):
            b = big_a
            s_wfr.wait(G)
            s_wld.begin(G)
            s_wld.issue(G.dma_start(b[:, 0:cols], w_h[sx * 128:(sx + 1) * 128, :]))
            s_wld.wait_mark(V, s_wld.mark())
            V.tensor_scalar(b[:, 0:cols], b[:, 0:cols], s_rec[:, 0:1], MAGIC,
                            ALU.mult, ALU.add)
            V.tensor_scalar(b[:, 0:cols], b[:, 0:cols], MAGIC, 1.0,
                            ALU.subtract, ALU.min)
            for qc in range(cols // 1024):
                s_tp.wait_all(V)
                s_vec.inc(V.tensor_scalar(q[:, :], b[:, 1024 * qc:1024 * qc + 1024],
                                           -1.0, None, ALU.max))
                if qc == cols // 1024 - 1:
                    s_wfr.inc(V.engine_nop())
                s_vec.wait(SY)
                s_tp.begin(SY)
                for k in range(8):
                    kk = 8 * qc + k
                    s_tp.issue(SY.dma_start(
                        qT[:, chunk_cols * kk + 128 * sx:chunk_cols * kk + 128 * sx + 128],
                        q[:, 128 * k:128 * k + 128], transpose=True))

    w_pass2(w1_h, D, FF // 128, w1s, w1qT, FF)
    w_pass2(w2_h, FF, D // 128, w2s, w2qT, D)

    p0wst.close()
    hg = [sb1("hg0", [128, FF], BF16), sb1("hg1", [128, FF], BF16)]
    xq1 = sb1("xq1", [128, D], BF16)
    xq1T = sb1("xq1T", [128, D], BF16)
    hq = [sb1("hq0", [128, 1024], BF16), sb1("hq1", [128, 1024], BF16)]
    hqT = sb1("hqT", [128, FF], BF16)
    qtmp = sb1("qtmp", [128, 1024], F32)
    spst = [sb1("spst0", [128, D], BF16), sb1("spst1", [128, D], BF16)]
    sp_marks = []
    hev_mark = {}
    ff2_mark = {}
    q2_mark = {}
    xq1_mks = {}
    sq_mark = {}

    def p1_stageA(i):
        b = xb[i % 2]
        if i == 0:
            s_xfr.wait(SY)
            s_xfs.wait(SY)
            p1_stageA.base = s_xfr.v
        else:
            s_xfr.wait(SY, p1_stageA.base + i - 1)
        s_xld.begin(SY)
        s_xld.issue(SY.dma_start(b[:, :], x_h[i * 128:(i + 1) * 128, :]))
        s_xld.wait_mark(V, s_xld.mark())
        s_xfr.inc(V.tensor_scalar(t1b[:, :], b[:, :], stc(6)[:, i:i + 1],
                                  stc(7)[:, i:i + 1], ALU.mult, ALU.add))
        if i > 0:
            s_tp.wait_mark(V, xq1_mks[i - 1])
        else:
            s_tp.wait_all(V)
        s_q1.inc(V.tensor_scalar(xq1[:, :], t1b[:, :], MAGIC, MAGIC,
                                 ALU.add, ALU.subtract))
        s_q1.wait(SY)
        if i > 0:
            s_ff1.wait(SY, 8 * i)
        else:
            s_tp.wait_all(SY)
        s_tp.begin(SY)
        for k in range(8):
            s_tp.issue(SY.dma_start(xq1T[:, 128 * k:128 * k + 128],
                                    xq1[:, 128 * k:128 * k + 128], transpose=True))
        xq1_mks[i] = s_tp.mark()

    def p1_ffn1(i):
        if i == 0:
            s_tp.wait_all(TE)
            s_wfr.wait(V)
        s_tp.wait_mark(TE, xq1_mks[i])
        for fc in range(8):
            pslot = ps[:, (fc % 4) * 512:(fc % 4) * 512 + 512]
            s_hev.wait(TE, s_hev.v - 3)
            for dc in range(8):
                mm = TE.matmul(pslot, xq1T[:, 128 * dc:128 * dc + 128],
                               w1qT[:, FF * dc + 512 * fc:FF * dc + 512 * fc + 512],
                               start=(dc == 0), stop=(dc == 7))
            s_ff1.inc(mm)
            s_ff1.wait(V)
            if fc == 0 and i >= 2:
                s_sca.wait(V, sq_mark[i - 2])
            s_hev.inc(V.scalar_tensor_tensor(
                hg[i % 2][:, 512 * fc:512 * fc + 512], pslot, stc(8)[:, i:i + 1],
                b1rep[:, 512 * fc:512 * fc + 512], ALU.mult, ALU.add))
        hev_mark[i] = s_hev.v

    def p1_post_ffn2(i):
        h = hg[i % 2]
        g = hqT
        s_hev.wait(S, hev_mark[i])
        if i > 0:
            s_ff2.wait(S, ff2_mark[i - 1])
        s_g.inc(S.activation(g[:, :], h[:, :], AF.Gelu))
        s_g.wait(V)
        s_vec.inc(V.reduce_max(stc(13)[:, i:i + 1], g[:, :], axis=AX.X,
                               apply_absolute_value=True))
        s_g.wait(S)
        s_sca.inc(S.activation(h[:, :], g[:, :], AF.Square,
                               accum_out=stc(9)[:, i:i + 1]))
        sq_mark[i] = s_sca.v
        s_vec.wait(S)
        s_sca.inc(S.activation(stc(15)[:, i:i + 1], stc(13)[:, i:i + 1], AF.Copy))
        s_sca.wait(V)
        V.reciprocal(stc(14)[:, i:i + 1], stc(15)[:, i:i + 1])
        V.tensor_scalar(stc(10)[:, i:i + 1], stc(13)[:, i:i + 1], mw2s[:, 0:1],
                        None, ALU.mult)
        vsp()
        V.tensor_scalar(stc(14)[:, i:i + 1], stc(14)[:, i:i + 1],
                        127.0, None, ALU.mult)
        vsp()
        s_vec.inc(V.engine_nop())
        for kc in range(4):
            if (i, kc) != (0, 0):
                pk = (i * 4 + kc - 2)
                if pk >= 0:
                    s_tp.wait_mark(V, q2_mark[pk])
            V.tensor_scalar(qtmp[:, :], g[:, 1024 * kc:1024 * kc + 1024],
                            stc(14)[:, i:i + 1], MAGIC, ALU.mult, ALU.add)
            s_q2.inc(V.tensor_scalar(hq[kc % 2][:, :], qtmp[:, :], MAGIC, None,
                                     ALU.subtract))
            s_q2.wait(SY)
            s_tp.begin(SY)
            for k in range(8):
                s_tp.issue(SY.dma_start(
                    hqT[:, 1024 * kc + 128 * k:1024 * kc + 128 * k + 128],
                    hq[kc % 2][:, 128 * k:128 * k + 128], transpose=True))
            q2_mark[i * 4 + kc] = s_tp.mark()
        for q2m in (q2_mark[i * 4 + 2], q2_mark[i * 4 + 3]):
            s_tp.wait_mark(TE, q2m)
        s_f2e.wait(TE)
        sp = spst[i % 2]
        for dc2 in range(2):
            pslot = ps[:, 2048 + dc2 * 512:2048 + dc2 * 512 + 512]
            for fc in range(32):
                mm = TE.matmul(pslot, hqT[:, 128 * fc:128 * fc + 128],
                               w2qT[:, D * fc + 512 * dc2:D * fc + 512 * dc2 + 512],
                               start=(fc == 0), stop=(fc == 31))
            s_ff2.inc(mm)
            s_ff2.wait(V)
            if i >= 2 and dc2 == 0:
                s_sp.wait_mark(V, sp_marks[i - 2])
            s_f2e.inc(V.tensor_scalar(sp[:, 512 * dc2:512 * dc2 + 512], pslot,
                                      stc(10)[:, i:i + 1], None, ALU.mult))
        ff2_mark[i] = s_ff2.v
        s_f2e.wait(G)
        s_sp.begin(G)
        s_sp.issue(G.dma_start(spill_h[i * 128:(i + 1) * 128, :], sp[:, :]))
        sp_marks.append(s_sp.mark())

    for i in range(nt + 1):
        if i < nt:
            p1_stageA(i)
            p1_ffn1(i)
        if i >= 1:
            p1_post_ffn2(i - 1)

    p1st.close()
    for eng in (G, SY, V, S, TE):
        s_sp.wait_all(eng)
    V.tensor_scalar(stc(15), stc(9), 1.0 / FF, EPS, ALU.mult, ALU.add)
    s_vec.inc(V.engine_nop())
    s_vec.wait(S)
    S.activation(stc(15), stc(15), AF.Ln)
    s_sca.inc(S.activation(stc(15), stc(15), AF.Exp, scale=0.5))
    s_sca.wait(V)
    s_vec.inc(V.reciprocal(stc(11), stc(15)))

    wcd_bf = sb("wcd_bf", [128, 8 * 512], BF16)
    wq_bf = sb("wq_bf", [128, 4 * 1024], BF16)
    wkv_bf = sb("wkv_bf", [128, 4 * 128], BF16)
    wpr_bf = sb("wpr_bf", [128, 8 * 1024], BF16)
    wtmp = sb("wtmp", [128, 1024], F32)

    def load_cast(src_h, nchunk, ccols, dst, scale=None):
        for c in range(nchunk):
            s_vec.wait(G)
            s_wc.begin(G)
            s_wc.issue(G.dma_start(wtmp[:, 0:ccols], src_h[c * 128:(c + 1) * 128, :]))
            s_wc.wait_mark(V, s_wc.mark())
            if scale is None:
                s_vec.inc(V.tensor_copy(dst[:, c * ccols:(c + 1) * ccols],
                                        wtmp[:, 0:ccols]))
            else:
                s_vec.inc(V.tensor_scalar(dst[:, c * ccols:(c + 1) * ccols],
                                          wtmp[:, 0:ccols], scale, None, ALU.mult))

    load_cast(wcd_h, 8, 512, wcd_bf)
    load_cast(wq_h, 4, 1024, wq_bf, scale=0.125)
    load_cast(wkv_h, 4, 128, wkv_bf)
    load_cast(wpr_h, 8, 1024, wpr_bf)
    wts_ready = s_vec.v
    mask4 = sb("mask4", [128, 2048], F32)
    s_bld.begin(SY)
    s_bld.issue(SY.dma_start(mask4[:, :], msk_h[:, :]))
    mask_mk = s_bld.mark()

    outst = [sb("outst0", [128, D], F32), sb("outst1", [128, D], F32)]
    fbk = [sb("fbk0", [128, D], BF16), sb("fbk1", [128, D], BF16)]
    kT64 = sb("kT64", [64, T], BF16)
    vext = sb("vext", [128, 66 * nt], BF16)
    latT = sb("latT", [128, 4 * T], BF16)
    xnbf = sb("xnbf", [128, D], BF16)
    xnT = sb("xnT", [128, 8 * 512], BF16)
    q64 = sb("q64", [64, 16 * 512], BF16)
    sq64 = sb("sq64", [64, 16 * 512], BF16)
    arow = sb("arow", [1, 512], BF16)
    amaxv = sb("amaxv", [128, 1], F32)
    nbias = sb("nbias", [128, 1], F32)
    knt = sb("knt", [128, 1], F32)
    kn2t = sb("kn2t", [128, 1], F32)
    sqk = sb("sqk", [64, 512], BF16)
    pbf = [sb("pbf0", [128, 1024], BF16), sb("pbf1", [128, 1024], BF16)]
    ybuf = sb("ybuf", [128, 4 * D], BF16)
    yT = sb("yT", [128, 8 * 512], BF16)
    drb = sb("drb", [128, 4], F32)

    s_vec.inc(V.memset(vext[:, :], 1.0))
    vext_ones = s_vec.v

    lat_marks = []
    for I in range(ns):
        for t4 in range(4):
            i = 4 * I + t4
            b = xb[i % 2]
            s_xfr.wait(SY, s_xfr.v - (1 if i > 0 else 0))
            s_xld.begin(SY)
            s_xld.issue(SY.dma_start(b[:, :], x_h[i * 128:(i + 1) * 128, :]))
            s_xld.wait_mark(V, s_xld.mark())
            if i > 0:
                s_tp.wait_mark(V, xn_mk)
            s_xfr.inc(V.tensor_scalar(xnbf[:, :], b[:, :], stc(4)[:, i:i + 1],
                                      stc(5)[:, i:i + 1], ALU.mult, ALU.add))
            s_xfr.wait(SY)
            if I > 0 and t4 == 0:
                s_mm.wait(SY, lat_marks[-1])
            s_tp.begin(SY)
            for k in range(8):
                s_tp.issue(SY.dma_start(
                    xnT[:, 512 * k + 128 * t4:512 * k + 128 * t4 + 128],
                    xnbf[:, 128 * k:128 * k + 128], transpose=True))
            xn_mk = s_tp.mark()
        s_tp.wait_all(TE)
        s_vec.wait(TE, wts_ready)
        for lc in range(4):
            pslot = ps[:, 3584:4096]
            if not (I == 0 and lc == 0):
                s_lat.wait(TE)
            for dc in range(8):
                mm = TE.matmul(pslot,
                               wcd_bf[:, 512 * dc + 128 * lc:512 * dc + 128 * lc + 128],
                               xnT[:, 512 * dc:512 * dc + 512],
                               start=(dc == 0), stop=(dc == 7))
            s_mm.inc(mm)
            s_mm.wait(S)
            s_lat.inc(S.activation(latT[:, T * lc + 512 * I:T * lc + 512 * I + 512],
                                   pslot, AF.Copy))
        lat_marks.append(s_mm.v)

    s_vec.wait(S, vext_ones)
    for I in range(ns):
        s_kv.wait(TE)
        s_lat.wait(TE)
        for lc in range(4):
            mm = TE.matmul(ps[:, 3584:4096], wkv_bf[:, 128 * lc:128 * lc + 128],
                           latT[:, T * lc + 512 * I:T * lc + 512 * I + 512],
                           start=(lc == 0), stop=(lc == 3))
        s_mm.inc(mm)
        s_mm.wait(V)
        s_kv.inc(V.tensor_copy(kT64[:, 512 * I:512 * I + 512], ps[0:64, 3584:4096]))
    for j in range(nt):
        s_kv.wait(TE)
        for lc in range(4):
            mm = TE.matmul(ps[:, 3584:3712],
                           latT[:, T * lc + 128 * j:T * lc + 128 * j + 128],
                           wkv_bf[:, 128 * lc:128 * lc + 128],
                           start=(lc == 0), stop=(lc == 3))
        s_mm.inc(mm)
        s_mm.wait(V)
        s_kv.inc(V.tensor_copy(vext[:, 66 * j:66 * j + 64], ps[:, 3648:3712]))
    V.memset(knt[:, :], 0.0)
    s_vec.inc(V.engine_nop())
    for I in range(ns):
        s_mm.wait(S)
        s_kv.wait(S)
        s_sca.inc(S.activation(sqk[:, :], kT64[:, 512 * I:512 * I + 512], AF.Square))
        s_sca.wait(TE)
        s_vec.wait(TE)
        pslot = ps[0:1, 3584:4096]
        s_mm.inc(TE.matmul(pslot, ones64[:, :], sqk[:, :], start=True, stop=True))
        s_mm.wait(V)
        V.tensor_copy(arow[:, :], pslot)
        s_vec.inc(V.engine_nop())
        s_vec.wait(TE)
        s_mm.inc(TE.matmul(ps[:, 3072:3584], ones1b[:, :], arow[:, :],
                           start=True, stop=True))
        s_mm.wait(V)
        V.reduce_max(kn2t[:, :], ps[:, 3072:3584], axis=AX.X)
        vsp()
        s_vec.inc(V.tensor_max(knt[:, :], knt[:, :], kn2t[:, :]))
    knt_ready = s_vec.v

    out_marks = []
    gpar = 0
    psum_free = {0: 0, 1: 0}
    pbf_free = {0: 0, 1: 0}
    for I in range(ns):
        nblk = 4 * I + 4
        ngrp = (nblk + 1) // 2
        s_asm.wait(TE)
        for h in range(H):
            pslot = ps[0:64, 3584:4096]
            if not (I == 0 and h == 0):
                s_qp.wait(TE)
            for lc in range(4):
                mm = TE.matmul(pslot,
                               wq_bf[:, 1024 * lc + 64 * h:1024 * lc + 64 * h + 64],
                               latT[:, T * lc + 512 * I:T * lc + 512 * I + 512],
                               start=(lc == 0), stop=(lc == 3))
            s_mm.inc(mm)
            s_mm.wait(V)
            s_qp.inc(V.tensor_copy(q64[:, 512 * h:512 * h + 512], pslot))
        s_qp.wait(V)
        s_vec.inc(V.tensor_mul(sq64[:, :], q64[:, :], q64[:, :]))
        s_vec.wait(TE)
        for h in range(H):
            mm = TE.matmul(ps[0:1, 3072:3584], ones64[:, :],
                           sq64[:, 512 * h:512 * h + 512],
                           start=(h == 0), stop=(h == H - 1))
        s_mm.inc(mm)
        s_mm.wait(V)
        s_vec.inc(V.tensor_copy(arow[:, :], ps[0:1, 3072:3584]))
        s_vec.wait(TE)
        s_mm.inc(TE.matmul(ps[:, 3584:4096], ones1b[:, :], arow[:, :],
                           start=True, stop=True))
        s_mm.wait(V)
        V.reduce_max(amaxv[:, :], ps[:, 3584:4096], axis=AX.X)
        vsp()
        s_vec.inc(V.tensor_scalar(amaxv[:, :], amaxv[:, :], 1e-30, None, ALU.max))
        s_vec.wait(S)
        S.activation(amaxv[:, :], amaxv[:, :], AF.Ln, scale=knt[:, 0:1])
        s_sca.inc(S.activation(amaxv[:, :], amaxv[:, :], AF.Exp, scale=0.5))
        s_sca.wait(V)
        s_nb.inc(V.tensor_scalar(nbias[:, :], amaxv[:, :], -1.0, 35.0,
                                 ALU.mult, ALU.add))
        nb_ready = s_nb.v
        for h in range(H):
            exp_at = {}
            par = {}
            sc_at = {}

            def emit_pv(gx):
                s_exp.wait(TE, exp_at[gx])
                if gx == 0:
                    s_ye.wait(TE)
                j0v = 2 * gx
                jnv = min(2, nblk - j0v)
                last = None
                for jj in range(jnv):
                    j = j0v + jj
                    for ti in range(4):
                        if j <= 4 * I + ti:
                            last = TE.matmul(
                                ps[:, 2048 + 512 * ti:2048 + 512 * ti + 65],
                                pbf[par[gx]][:, 512 * jj + 128 * ti:
                                             512 * jj + 128 * ti + 128],
                                vext[:, 66 * j:66 * j + 65],
                                start=(j == 0), stop=(j == 4 * I + ti))
                s_pv.inc(last)
                pbf_free[par[gx]] = s_pv.v

            for g in range(ngrp):
                pp = gpar % 2
                gpar += 1
                par[g] = pp
                j0 = 2 * g
                jn = min(2, nblk - j0)
                s_exp.wait(TE, psum_free[pp])
                for jj in range(jn):
                    j = j0 + jj
                    s_sc.inc(TE.matmul(
                        ps[:, 1024 * pp + 512 * jj:1024 * pp + 512 * jj + 512],
                        kT64[:, 128 * j:128 * j + 128],
                        q64[:, 512 * h:512 * h + 512],
                        start=True, stop=True))
                sc_at[g] = s_sc.v
                nmsk = 0
                for jj in range(jn):
                    di = j0 + jj - 4 * I
                    if di >= 0:
                        if nmsk == 0:
                            s_sc.wait(V, sc_at[g])
                            s_bld.wait_mark(V, mask_mk)
                        pvw = ps[:, 1024 * pp + 512 * jj:
                                 1024 * pp + 512 * jj + 128 * (di + 1)]
                        s_msk.inc(V.tensor_tensor(
                            pvw, pvw,
                            mask4[:, 512 * di:512 * di + 128 * (di + 1)], ALU.add))
                        nmsk += 1
                if g >= 1:
                    emit_pv(g - 1)
                s_sc.wait(S, sc_at[g])
                if nmsk:
                    s_msk.wait(S)
                s_pv.wait(S, pbf_free[pp])
                s_nb.wait(S, nb_ready)
                s_exp.inc(S.activation(pbf[pp][:, 0:512 * jn],
                                       ps[:, 1024 * pp:1024 * pp + 512 * jn],
                                       AF.Exp, bias=nbias[:, 0:1]))
                exp_at[g] = s_exp.v
                psum_free[pp] = s_exp.v
            emit_pv(ngrp - 1)
            s_pv.wait(V)
            for ti in range(4):
                V.reciprocal(drb[:, ti:ti + 1],
                             ps[:, 2048 + 512 * ti + 64:2048 + 512 * ti + 65])
            vsp()
            for ti in range(4):
                s_ye.inc(V.tensor_scalar(
                    ybuf[:, D * ti + 64 * h:D * ti + 64 * h + 64],
                    ps[:, 2048 + 512 * ti:2048 + 512 * ti + 64],
                    drb[:, ti:ti + 1], None, ALU.mult))
        s_ye.wait(SY)
        if I > 0:
            s_pr.wait(SY)
        s_tp.begin(SY)
        for ti in range(4):
            for c in range(8):
                s_tp.issue(SY.dma_start(
                    yT[:, 512 * c + 128 * ti:512 * c + 128 * ti + 128],
                    ybuf[:, D * ti + 128 * c:D * ti + 128 * c + 128],
                    transpose=True))
        s_tp.wait_all(TE)
        for ti in range(4):
            i = 4 * I + ti
            b = xb[i % 2]
            fb = fbk[i % 2]
            s_xfr.wait(SY, s_xfr.v - (1 if i > 0 else 0))
            s_xld.begin(SY)
            s_xld.issue(SY.dma_start(b[:, :], x_h[i * 128:(i + 1) * 128, :]))
            asm_xmk = s_xld.mark()
            if i >= 2:
                s_asm.wait(G, s_asm.v - 2)
            s_fb.begin(G)
            s_fb.issue(G.dma_start(fb[:, :], spill_h[i * 128:(i + 1) * 128, :]))
            fb_mk = s_fb.mark()
            for half in range(2):
                pslot = ps[:, 3072 + 512 * half:3072 + 512 * half + 512]
                if not (I == 0 and ti == 0):
                    s_asm.wait(TE)
                for c in range(8):
                    mm = TE.matmul(
                        pslot, yT[:, 512 * c + 128 * ti:512 * c + 128 * ti + 128],
                        wpr_bf[:, 1024 * c + 512 * half:1024 * c + 512 * half + 512],
                        start=(c == 0), stop=(c == 7))
                s_pr.inc(mm)
            o = outst[i % 2]
            s_pr.wait(V)
            s_xld.wait_mark(V, asm_xmk)
            s_fb.wait_mark(V, fb_mk)
            if i >= 2:
                s_out.wait_mark(V, out_marks[i - 2])
            for half in range(2):
                pslot = ps[:, 3072 + 512 * half:3072 + 512 * half + 512]
                sl = slice(512 * half, 512 * half + 512)
                V.tensor_add(o[:, sl], pslot, b[:, sl])
                V.scalar_tensor_tensor(o[:, sl], fb[:, sl], stc(11)[:, i:i + 1],
                                       o[:, sl], ALU.mult, ALU.add)
                s_asm.inc(V.tensor_add(o[:, sl], o[:, sl], b2rep[:, sl]))
            s_xfr.inc(V.engine_nop())
            s_asm.wait(G)
            s_out.begin(G)
            s_out.issue(G.dma_start(out_h[i * 128:(i + 1) * 128, :], o[:, :]))
            out_marks.append(s_out.mark())

    s_out.wait_all(G)
    st.close()
    return nc


def _mask4_np():
    m = np.zeros((128, 4 * 512), dtype=np.float32)
    r = np.arange(128)
    for di in range(4):
        blk = m[:, 512 * di:512 * di + 512]
        blk[:, 0:128 * di] = NEG
        tri = blk[:, 128 * di:128 * di + 128]
        tri[r[:, None] > r[None, :]] = NEG
    return m


def kernel(L0_attn_x, L0_attn_Wc_down, L0_attn_Wq_up, L0_attn_Wkv_up,
           L0_attn_Wc_proj, L0_ffn_w1, L0_ffn_b1, L0_ffn_w2, L0_ffn_b2):
    T = 4096
    nc = build_nc(T)
    x = np.ascontiguousarray(np.asarray(L0_attn_x, dtype=np.float32))
    shared = {
        "wcd": np.ascontiguousarray(np.asarray(L0_attn_Wc_down, dtype=np.float32)),
        "wq": np.ascontiguousarray(np.asarray(L0_attn_Wq_up, dtype=np.float32)),
        "wkv": np.ascontiguousarray(np.asarray(L0_attn_Wkv_up, dtype=np.float32)),
        "wpr": np.ascontiguousarray(np.asarray(L0_attn_Wc_proj, dtype=np.float32)),
        "w1": np.ascontiguousarray(np.asarray(L0_ffn_w1, dtype=np.float32)),
        "b1": np.ascontiguousarray(np.asarray(L0_ffn_b1, dtype=np.float32)).reshape(1, -1),
        "w2": np.ascontiguousarray(np.asarray(L0_ffn_w2, dtype=np.float32)),
        "b2": np.ascontiguousarray(np.asarray(L0_ffn_b2, dtype=np.float32)).reshape(1, -1),
        "msk": _mask4_np(),
    }
    in_maps = [dict(shared, x=x[c]) for c in range(8)]
    res = run_bass_kernel_spmd(nc, in_maps, core_ids=list(range(8)))
    return np.stack([np.asarray(res.results[c]["out"]) for c in range(8)], axis=0)


# version 07
# speedup vs baseline: 1.0647x; 1.0200x over previous
import numpy as np
import ml_dtypes
from contextlib import ExitStack

import concourse.bass as bass
import concourse.mybir as mybir
from concourse.bass_utils import run_bass_kernel_spmd

F32 = mybir.dt.float32
BF16 = mybir.dt.bfloat16
AF = mybir.ActivationFunctionType
ALU = mybir.AluOpType
AX = mybir.AxisListType

MAGIC = 12582912.0
NEG = -1.0e30
D = 1024
L = 512
FF = 4096
H = 16
DH = 64
EPS = 1e-5


def build_nc(T=4096):
    nc = bass.Bass("TRN2", target_bir_lowering=False, detect_race_conditions=False)
    nt = T // 128
    ns = T // 512

    x_h = nc.dram_tensor("x", [T, D], F32, kind="ExternalInput")
    wcd_h = nc.dram_tensor("wcd", [D, L], F32, kind="ExternalInput")
    wq_h = nc.dram_tensor("wq", [L, D], F32, kind="ExternalInput")
    wkv_h = nc.dram_tensor("wkv", [L, 2 * DH], F32, kind="ExternalInput")
    wpr_h = nc.dram_tensor("wpr", [D, D], F32, kind="ExternalInput")
    w1_h = nc.dram_tensor("w1", [FF, D], F32, kind="ExternalInput")
    b1_h = nc.dram_tensor("b1", [1, FF], F32, kind="ExternalInput")
    w2_h = nc.dram_tensor("w2", [D, FF], F32, kind="ExternalInput")
    b2_h = nc.dram_tensor("b2", [1, D], F32, kind="ExternalInput")
    msk_h = nc.dram_tensor("msk", [128, 4 * 512], F32, kind="ExternalInput")
    out_h = nc.dram_tensor("out", [T, D], F32, kind="ExternalOutput")
    spill_h = nc.dram_tensor("ffn_spill", [T, D], BF16, kind="ExternalOutput")

    st = ExitStack()
    sb = lambda name, shape, dt: st.enter_context(nc.sbuf_tensor(name, shape, dt))
    sem = lambda name: st.enter_context(nc.semaphore(name))

    xb = [sb("xb0", [128, D], F32), sb("xb1", [128, D], F32)]
    b2rep = sb("b2rep", [128, D], BF16)
    ones1f = sb("ones1f", [1, 128], F32)
    ones1b = sb("ones1b", [1, 128], BF16)
    onesff = sb("onesff", [128, 128], F32)
    ones64 = sb("ones64", [64, 1], BF16)
    stt = sb("stt", [128, 16 * nt], F32)

    def stc(i):
        return stt[:, i * nt:(i + 1) * nt]

    w1s = sb("w1s", [128, 1], F32)
    mw1s = sb("mw1s", [128, 1], F32)
    w2s = sb("w2s", [128, 1], F32)
    mw2s = sb("mw2s", [128, 1], F32)
    wabs = sb("wabs", [128, 40], F32)
    brow = sb("brow", [1, 512], F32)
    tmp1 = sb("tmp1", [128, 1], F32)
    tmp2 = sb("tmp2", [128, 1], F32)
    tmp3 = sb("tmp3", [128, 128], F32)

    ps = st.enter_context(nc.psum_tensor("ps", [128, 4096], F32))

    SY, V, S, TE, G = nc.sync, nc.vector, nc.scalar, nc.tensor, nc.gpsimd

    class Ctr:
        def __init__(self, name):
            self.h = sem(name)
            self.v = 0

        def inc(self, inst, n=1):
            inst.then_inc(self.h, n)
            self.v += n

        def wait(self, eng, val=None):
            val = self.v if val is None else val
            if val > 0:
                eng.wait_ge(self.h, val)

    class DCtr:

        def __init__(self, name):
            self.h = [sem(name + "a"), sem(name + "b")]
            self.v = [0, 0]
            self.b = 0
            self.cur = None

        def __init_serial__(self):
            pass

        def begin(self, eng):
            p = self.b % 2
            self.b += 1
            self.cur = p
            if getattr(self, "serial", False):
                for pp in (0, 1):
                    if self.v[pp] > 0:
                        eng.wait_ge(self.h[pp], self.v[pp])
            elif self.v[p] > 0:
                eng.wait_ge(self.h[p], self.v[p])
            return p

        def issue(self, inst):
            p = self.cur
            inst.then_inc(self.h[p], 16)
            self.v[p] += 16

        def mark(self):
            return (self.cur, self.v[self.cur])

        def wait_mark(self, eng, mk):
            if mk is not None:
                eng.wait_ge(self.h[mk[0]], mk[1])

        def wait_all(self, eng):
            for p in (0, 1):
                if self.v[p] > 0:
                    eng.wait_ge(self.h[p], self.v[p])

    s_xld = DCtr("s_xld")
    s_xfr = Ctr("s_xfr")
    s_xfs = Ctr("s_xfs")
    s_bld = DCtr("s_bld")
    s_mm = Ctr("s_mm")
    s_vec = Ctr("s_vec")
    s_sca = Ctr("s_sca")
    s_wld = DCtr("s_wld")
    s_wfr = Ctr("s_wfr")
    s_tp = DCtr("s_tp")
    s_ff1 = Ctr("s_ff1")
    s_hev = Ctr("s_hev")
    s_ff2 = Ctr("s_ff2")
    s_f2e = Ctr("s_f2e")
    s_sp = DCtr("s_sp")
    s_q1 = Ctr("s_q1")
    s_q2 = Ctr("s_q2")
    s_g = Ctr("s_g")
    s_lat = Ctr("s_lat")
    s_sc = Ctr("s_sc")
    s_msk = Ctr("s_msk")
    s_exp = Ctr("s_exp")
    s_pv = Ctr("s_pv")
    s_ye = Ctr("s_ye")
    s_qp = Ctr("s_qp")
    s_nb = Ctr("s_nb")
    s_out = DCtr("s_out")
    s_fb = DCtr("s_fb")
    s_pr = Ctr("s_pr")
    s_asm = Ctr("s_asm")
    s_kv = Ctr("s_kv")
    s_wc = DCtr("s_wc")

    def vsp():
        V.tensor_scalar(tmp3[:, :], tmp3[:, :], 1.0, None, ALU.mult)

    V.memset(tmp3[:, :], 0.0)
    V.memset(tmp1[:, :], 0.0)
    V.memset(ones1f[:, :], 1.0)
    V.memset(ones1b[:, :], 1.0)
    V.memset(onesff[:, :], 1.0)
    s_vec.inc(V.memset(ones64[:, :], 1.0))

    p1st = ExitStack()
    sb1 = lambda name, shape, dt: p1st.enter_context(nc.sbuf_tensor(name, shape, dt))
    t1b = sb1("t1b", [128, D], F32)
    b1rep = sb1("b1rep", [128, FF], BF16)
    w1qT = sb1("w1qT", [128, 8 * FF], BF16)
    w2qT = sb1("w2qT", [128, 32 * D], BF16)
    p0wst = ExitStack()
    big_a = p0wst.enter_context(nc.sbuf_tensor("big_a", [128, FF], F32))
    wq_sb = p0wst.enter_context(nc.sbuf_tensor("wqs0", [128, 1024], BF16))

    for i in range(nt):
        b = xb[i % 2]
        if i >= 2:
            s_xfr.wait(SY, i - 1)
            s_xfs.wait(SY, i - 1)
        s_xld.begin(SY)
        s_xld.issue(SY.dma_start(b[:, :], x_h[i * 128:(i + 1) * 128, :]))
        xmk = s_xld.mark()
        s_xld.wait_mark(V, xmk)
        s_xfr.inc(V.reduce_sum(stc(0)[:, i:i + 1], b[:, :], axis=AX.X))
        V.reduce_max(stc(2)[:, i:i + 1], b[:, :], axis=AX.X)
        V.tensor_reduce(stc(3)[:, i:i + 1], b[:, :], axis=AX.X, op=ALU.min)
        s_xld.wait_mark(S, xmk)
        s_xfs.inc(S.activation(t1b[:, :], b[:, :], AF.Square,
                               accum_out=stc(1)[:, i:i + 1]))

    s_xld.wait_all(V)
    s_xfr.wait(V, nt)
    s_xfs.wait(V, nt)
    inv_d = 1.0 / D
    m = stc(12)
    V.tensor_scalar(m, stc(0), inv_d, None, ALU.mult)
    V.tensor_scalar(stc(13), stc(1), inv_d, None, ALU.mult)
    V.tensor_mul(stc(14), m, m)
    vsp()
    V.tensor_sub(stc(1), stc(13), stc(14))
    vsp()
    s_vec.inc(V.tensor_scalar(stc(15), stc(1), EPS, None, ALU.add))
    s_vec.wait(S)
    s_sca.inc(S.activation(stc(14), stc(15), AF.Ln))
    s_sca.wait(V)
    s_vec.inc(V.tensor_scalar(stc(14), stc(14), -0.5, None, ALU.mult))
    s_vec.wait(S)
    s_sca.inc(S.activation(stc(4), stc(14), AF.Exp))
    s_sca.wait(V)
    V.scalar_tensor_tensor(stc(5), stc(4), -1.0, m, ALU.mult, ALU.mult)
    V.tensor_sub(stc(2), stc(2), m)
    V.tensor_sub(stc(3), m, stc(3))
    vsp()
    V.tensor_max(stc(2), stc(2), stc(3))
    vsp()
    V.tensor_mul(stc(2), stc(2), stc(4))
    V.tensor_mul(stc(14), stc(4), stc(4))
    vsp()
    V.tensor_mul(stc(14), stc(1), stc(14))
    vsp()
    s_vec.inc(V.tensor_scalar(stc(14), stc(14), EPS, None, ALU.add))
    s_vec.wait(S)
    s_sca.inc(S.activation(stc(15), stc(14), AF.Ln))
    s_sca.wait(V)
    s_vec.inc(V.tensor_scalar(stc(15), stc(15), -0.5, None, ALU.mult))
    s_vec.wait(S)
    s_sca.inc(S.activation(stc(3), stc(15), AF.Exp))
    s_sca.wait(V)
    V.tensor_mul(stc(2), stc(2), stc(3))
    vsp()
    s_vec.inc(V.tensor_scalar(stc(2), stc(2), 1e-5, None, ALU.max))
    s_vec.wait(S)
    s_sca.inc(S.activation(stc(15), stc(2), AF.Ln))
    s_sca.wait(V)
    s_vec.inc(V.tensor_scalar(stc(15), stc(15), -1.0, None, ALU.mult))
    s_vec.wait(S)
    s_sca.inc(S.activation(stc(14), stc(15), AF.Exp))
    s_sca.wait(V)
    V.tensor_scalar(stc(14), stc(14), 127.0, None, ALU.mult)
    V.tensor_mul(stc(6), stc(4), stc(3))
    V.tensor_mul(stc(7), stc(5), stc(3))
    V.tensor_mul(stc(6), stc(6), stc(14))
    s_vec.inc(V.tensor_mul(stc(7), stc(7), stc(14)))


    s_vec.wait(TE)
    for c in range(FF // 512 + D // 512):
        if c < FF // 512:
            src_ap = b1_h[0:1, c * 512:(c + 1) * 512]
            dst = b1rep[:, c * 512:(c + 1) * 512]
        else:
            c2 = c - FF // 512
            src_ap = b2_h[0:1, c2 * 512:(c2 + 1) * 512]
            dst = b2rep[:, c2 * 512:(c2 + 1) * 512]
        pslot = ps[:, (c % 2) * 512:(c % 2) * 512 + 512]
        if c >= 1:
            s_mm.wait(SY)
        s_bld.begin(SY)
        s_bld.issue(SY.dma_start(brow[:, :], src_ap))
        s_bld.wait_mark(TE, s_bld.mark())
        if c >= 2:
            s_vec.wait(TE, s_vec.v - 1)
        s_mm.inc(TE.matmul(pslot, ones1f[:, :], brow[:, :], start=True, stop=True))
        s_mm.wait(V)
        s_vec.inc(V.tensor_copy(dst, pslot))


    def w_pass1(w_h, cols, nstr, off):
        for sx in range(nstr):
            b = big_a
            s_wfr.wait(G)
            s_wld.begin(G)
            s_wld.issue(G.dma_start(b[:, 0:cols], w_h[sx * 128:(sx + 1) * 128, :]))
            s_wld.wait_mark(V, s_wld.mark())
            s_wfr.inc(V.reduce_sum(wabs[:, off + sx:off + sx + 1], b[:, 0:cols],
                                   axis=AX.X, apply_absolute_value=True))

    w_pass1(w1_h, D, FF // 128, 0)
    w_pass1(w2_h, FF, D // 128, 32)
    s_wfr.wait(V)
    s_vec.inc(V.engine_nop())
    s_vec.wait(TE)
    s_mm.inc(TE.matmul(ps[:, 2048:2088], onesff[:, :], wabs[:, 0:40],
                       start=True, stop=True))
    s_mm.wait(V)
    V.reduce_sum(w1s[:, :], ps[:, 2048:2080], axis=AX.X)
    V.reduce_sum(w2s[:, :], ps[:, 2080:2088], axis=AX.X)
    for wsx, mwx, nel in ((w1s, mw1s, FF * D), (w2s, mw2s, D * FF)):
        V.tensor_scalar(tmp2[:, :], wsx[:, :], 1.0 / nel, None, ALU.mult)
        vsp()
        s_vec.inc(V.tensor_scalar(tmp2[:, :], tmp2[:, :], 1e-5, None, ALU.max))
        s_vec.wait(S)
        s_sca.inc(S.activation(tmp1[:, :], tmp2[:, :], AF.Copy))
        s_sca.wait(V)
        V.reciprocal(wsx[:, :], tmp1[:, :])
        vsp()
        s_vec.inc(V.tensor_scalar(mwx[:, :], tmp2[:, :], 1.0 / 127.0, None, ALU.mult))
    s_vec.inc(V.engine_nop())
    V.tensor_scalar(stc(8), stc(2), mw1s[:, 0:1], None, ALU.mult)

    def w_pass2(w_h, cols, nstr, s_rec, qT, chunk_cols):
        q = wq_sb
        for sx in range(nstr):
            b = big_a
            s_wfr.wait(G)
            s_wld.begin(G)
            s_wld.issue(G.dma_start(b[:, 0:cols], w_h[sx * 128:(sx + 1) * 128, :]))
            s_wld.wait_mark(V, s_wld.mark())
            V.tensor_scalar(b[:, 0:cols], b[:, 0:cols], s_rec[:, 0:1], MAGIC,
                            ALU.mult, ALU.add)
            V.tensor_scalar(b[:, 0:cols], b[:, 0:cols], MAGIC, 1.0,
                            ALU.subtract, ALU.min)
            for qc in range(cols // 1024):
                s_tp.wait_all(V)
                s_vec.inc(V.tensor_scalar(q[:, :], b[:, 1024 * qc:1024 * qc + 1024],
                                           -1.0, None, ALU.max))
                if qc == cols // 1024 - 1:
                    s_wfr.inc(V.engine_nop())
                s_vec.wait(SY)
                s_tp.begin(SY)
                for k in range(8):
                    kk = 8 * qc + k
                    s_tp.issue(SY.dma_start(
                        qT[:, chunk_cols * kk + 128 * sx:chunk_cols * kk + 128 * sx + 128],
                        q[:, 128 * k:128 * k + 128], transpose=True))

    w_pass2(w1_h, D, FF // 128, w1s, w1qT, FF)
    w_pass2(w2_h, FF, D // 128, w2s, w2qT, D)

    p0wst.close()
    hg = [sb1("hg0", [128, FF], BF16), sb1("hg1", [128, FF], BF16)]
    xq1 = sb1("xq1", [128, D], BF16)
    xq1T = sb1("xq1T", [128, D], BF16)
    hq = [sb1("hq0", [128, 1024], BF16), sb1("hq1", [128, 1024], BF16)]
    hqT = sb1("hqT", [128, FF], BF16)
    qtmp = sb1("qtmp", [128, 1024], F32)
    spst = [sb1("spst0", [128, D], BF16), sb1("spst1", [128, D], BF16)]
    sp_marks = []
    hev_mark = {}
    ff2_mark = {}
    q2_mark = {}
    xq1_mks = {}
    sq_mark = {}

    def p1_stageA(i):
        b = xb[i % 2]
        if i == 0:
            s_xfr.wait(SY)
            s_xfs.wait(SY)
            p1_stageA.base = s_xfr.v
        else:
            s_xfr.wait(SY, p1_stageA.base + i - 1)
        s_xld.begin(SY)
        s_xld.issue(SY.dma_start(b[:, :], x_h[i * 128:(i + 1) * 128, :]))
        s_xld.wait_mark(V, s_xld.mark())
        s_xfr.inc(V.tensor_scalar(t1b[:, :], b[:, :], stc(6)[:, i:i + 1],
                                  stc(7)[:, i:i + 1], ALU.mult, ALU.add))
        if i > 0:
            s_tp.wait_mark(V, xq1_mks[i - 1])
        else:
            s_tp.wait_all(V)
        s_q1.inc(V.tensor_scalar(xq1[:, :], t1b[:, :], MAGIC, MAGIC,
                                 ALU.add, ALU.subtract))
        s_q1.wait(SY)
        if i > 0:
            s_ff1.wait(SY, 8 * i)
        else:
            s_tp.wait_all(SY)
        s_tp.begin(SY)
        for k in range(8):
            s_tp.issue(SY.dma_start(xq1T[:, 128 * k:128 * k + 128],
                                    xq1[:, 128 * k:128 * k + 128], transpose=True))
        xq1_mks[i] = s_tp.mark()

    def p1_ffn1(i):
        if i == 0:
            s_tp.wait_all(TE)
            s_wfr.wait(V)
        s_tp.wait_mark(TE, xq1_mks[i])
        for fc in range(8):
            pslot = ps[:, (fc % 4) * 512:(fc % 4) * 512 + 512]
            s_hev.wait(TE, s_hev.v - 3)
            for dc in range(8):
                mm = TE.matmul(pslot, xq1T[:, 128 * dc:128 * dc + 128],
                               w1qT[:, FF * dc + 512 * fc:FF * dc + 512 * fc + 512],
                               start=(dc == 0), stop=(dc == 7))
            s_ff1.inc(mm)
            s_ff1.wait(V)
            if fc == 0 and i >= 2:
                s_sca.wait(V, sq_mark[i - 2])
            s_hev.inc(V.scalar_tensor_tensor(
                hg[i % 2][:, 512 * fc:512 * fc + 512], pslot, stc(8)[:, i:i + 1],
                b1rep[:, 512 * fc:512 * fc + 512], ALU.mult, ALU.add))
        hev_mark[i] = s_hev.v

    def p1_post_ffn2(i):
        h = hg[i % 2]
        g = hqT
        s_hev.wait(S, hev_mark[i])
        if i > 0:
            s_ff2.wait(S, ff2_mark[i - 1])
        s_g.inc(S.activation(g[:, :], h[:, :], AF.Gelu))
        s_g.wait(V)
        s_vec.inc(V.reduce_max(stc(13)[:, i:i + 1], g[:, :], axis=AX.X,
                               apply_absolute_value=True))
        s_g.wait(S)
        s_sca.inc(S.activation(h[:, :], g[:, :], AF.Square,
                               accum_out=stc(9)[:, i:i + 1]))
        sq_mark[i] = s_sca.v
        s_vec.wait(S)
        s_sca.inc(S.activation(stc(15)[:, i:i + 1], stc(13)[:, i:i + 1], AF.Copy))
        s_sca.wait(V)
        V.reciprocal(stc(14)[:, i:i + 1], stc(15)[:, i:i + 1])
        V.tensor_scalar(stc(10)[:, i:i + 1], stc(13)[:, i:i + 1], mw2s[:, 0:1],
                        None, ALU.mult)
        vsp()
        V.tensor_scalar(stc(14)[:, i:i + 1], stc(14)[:, i:i + 1],
                        127.0, None, ALU.mult)
        vsp()
        s_vec.inc(V.engine_nop())
        for kc in range(4):
            if (i, kc) != (0, 0):
                pk = (i * 4 + kc - 2)
                if pk >= 0:
                    s_tp.wait_mark(V, q2_mark[pk])
            V.tensor_scalar(qtmp[:, :], g[:, 1024 * kc:1024 * kc + 1024],
                            stc(14)[:, i:i + 1], MAGIC, ALU.mult, ALU.add)
            s_q2.inc(V.tensor_scalar(hq[kc % 2][:, :], qtmp[:, :], MAGIC, None,
                                     ALU.subtract))
            s_q2.wait(SY)
            s_tp.begin(SY)
            for k in range(8):
                s_tp.issue(SY.dma_start(
                    hqT[:, 1024 * kc + 128 * k:1024 * kc + 128 * k + 128],
                    hq[kc % 2][:, 128 * k:128 * k + 128], transpose=True))
            q2_mark[i * 4 + kc] = s_tp.mark()
        for q2m in (q2_mark[i * 4 + 2], q2_mark[i * 4 + 3]):
            s_tp.wait_mark(TE, q2m)
        s_f2e.wait(TE)
        sp = spst[i % 2]
        for dc2 in range(2):
            pslot = ps[:, 2048 + dc2 * 512:2048 + dc2 * 512 + 512]
            for fc in range(32):
                mm = TE.matmul(pslot, hqT[:, 128 * fc:128 * fc + 128],
                               w2qT[:, D * fc + 512 * dc2:D * fc + 512 * dc2 + 512],
                               start=(fc == 0), stop=(fc == 31))
            s_ff2.inc(mm)
            s_ff2.wait(V)
            if i >= 2 and dc2 == 0:
                s_sp.wait_mark(V, sp_marks[i - 2])
            s_f2e.inc(V.tensor_scalar(sp[:, 512 * dc2:512 * dc2 + 512], pslot,
                                      stc(10)[:, i:i + 1], None, ALU.mult))
        ff2_mark[i] = s_ff2.v
        s_f2e.wait(G)
        s_sp.begin(G)
        s_sp.issue(G.dma_start(spill_h[i * 128:(i + 1) * 128, :], sp[:, :]))
        sp_marks.append(s_sp.mark())

    for i in range(nt + 1):
        if i < nt:
            p1_stageA(i)
            p1_ffn1(i)
        if i >= 1:
            p1_post_ffn2(i - 1)

    p1st.close()
    for eng in (G, SY, V, S, TE):
        s_sp.wait_all(eng)
    V.tensor_scalar(stc(15), stc(9), 1.0 / FF, EPS, ALU.mult, ALU.add)
    s_vec.inc(V.engine_nop())
    s_vec.wait(S)
    S.activation(stc(15), stc(15), AF.Ln)
    s_sca.inc(S.activation(stc(15), stc(15), AF.Exp, scale=0.5))
    s_sca.wait(V)
    s_vec.inc(V.reciprocal(stc(11), stc(15)))

    wcd_bf = sb("wcd_bf", [128, 8 * 512], BF16)
    wq_bf = sb("wq_bf", [128, 4 * 1024], BF16)
    wkv_bf = sb("wkv_bf", [128, 4 * 128], BF16)
    wpr_bf = sb("wpr_bf", [128, 8 * 1024], BF16)
    wtmp = sb("wtmp", [128, 1024], F32)

    def load_cast(src_h, nchunk, ccols, dst, scale=None):
        for c in range(nchunk):
            s_vec.wait(G)
            s_wc.begin(G)
            s_wc.issue(G.dma_start(wtmp[:, 0:ccols], src_h[c * 128:(c + 1) * 128, :]))
            s_wc.wait_mark(V, s_wc.mark())
            if scale is None:
                s_vec.inc(V.tensor_copy(dst[:, c * ccols:(c + 1) * ccols],
                                        wtmp[:, 0:ccols]))
            else:
                s_vec.inc(V.tensor_scalar(dst[:, c * ccols:(c + 1) * ccols],
                                          wtmp[:, 0:ccols], scale, None, ALU.mult))

    load_cast(wcd_h, 8, 512, wcd_bf)
    load_cast(wq_h, 4, 1024, wq_bf, scale=0.125)
    load_cast(wkv_h, 4, 128, wkv_bf)
    load_cast(wpr_h, 8, 1024, wpr_bf)
    wts_ready = s_vec.v
    mask4 = sb("mask4", [128, 2048], F32)
    s_bld.begin(SY)
    s_bld.issue(SY.dma_start(mask4[:, :], msk_h[:, :]))
    mask_mk = s_bld.mark()

    outst = [sb("outst0", [128, D], F32), sb("outst1", [128, D], F32)]
    fbk = [sb("fbk0", [128, D], BF16), sb("fbk1", [128, D], BF16)]
    kT64 = sb("kT64", [64, T], BF16)
    vext = sb("vext", [128, 66 * nt], BF16)
    latT = sb("latT", [128, 4 * T], BF16)
    xnbf = sb("xnbf", [128, D], BF16)
    xnT = sb("xnT", [128, 8 * 512], BF16)
    q64 = sb("q64", [64, 16 * 512], BF16)
    sq64 = sb("sq64", [64, 16 * 512], BF16)
    arow = sb("arow", [1, 512], BF16)
    amaxv = sb("amaxv", [128, 1], F32)
    nbias = sb("nbias", [128, 1], F32)
    knt = sb("knt", [128, 1], F32)
    kn2t = sb("kn2t", [128, 1], F32)
    sqk = sb("sqk", [64, 512], BF16)
    pbf = [sb("pbf0", [128, 1024], BF16), sb("pbf1", [128, 1024], BF16)]
    ybuf = sb("ybuf", [128, 4 * D], BF16)
    yT = sb("yT", [128, 8 * 512], BF16)
    drb = sb("drb", [128, 4], F32)

    s_vec.inc(V.memset(vext[:, :], 1.0))
    vext_ones = s_vec.v

    lat_marks = []
    for I in range(ns):
        for t4 in range(4):
            i = 4 * I + t4
            b = xb[i % 2]
            s_xfr.wait(SY, s_xfr.v - (1 if i > 0 else 0))
            s_xld.begin(SY)
            s_xld.issue(SY.dma_start(b[:, :], x_h[i * 128:(i + 1) * 128, :]))
            s_xld.wait_mark(V, s_xld.mark())
            if i > 0:
                s_tp.wait_mark(V, xn_mk)
            s_xfr.inc(V.tensor_scalar(xnbf[:, :], b[:, :], stc(4)[:, i:i + 1],
                                      stc(5)[:, i:i + 1], ALU.mult, ALU.add))
            s_xfr.wait(SY)
            if I > 0 and t4 == 0:
                s_mm.wait(SY, lat_marks[-1])
            s_tp.begin(SY)
            for k in range(8):
                s_tp.issue(SY.dma_start(
                    xnT[:, 512 * k + 128 * t4:512 * k + 128 * t4 + 128],
                    xnbf[:, 128 * k:128 * k + 128], transpose=True))
            xn_mk = s_tp.mark()
        s_tp.wait_all(TE)
        s_vec.wait(TE, wts_ready)
        for lc in range(4):
            pslot = ps[:, 3584:4096]
            if not (I == 0 and lc == 0):
                s_lat.wait(TE)
            for dc in range(8):
                mm = TE.matmul(pslot,
                               wcd_bf[:, 512 * dc + 128 * lc:512 * dc + 128 * lc + 128],
                               xnT[:, 512 * dc:512 * dc + 512],
                               start=(dc == 0), stop=(dc == 7))
            s_mm.inc(mm)
            s_mm.wait(S)
            s_lat.inc(S.activation(latT[:, T * lc + 512 * I:T * lc + 512 * I + 512],
                                   pslot, AF.Copy))
        lat_marks.append(s_mm.v)

    s_vec.wait(S, vext_ones)
    for I in range(ns):
        s_kv.wait(TE)
        s_lat.wait(TE)
        for lc in range(4):
            mm = TE.matmul(ps[:, 3584:4096], wkv_bf[:, 128 * lc:128 * lc + 128],
                           latT[:, T * lc + 512 * I:T * lc + 512 * I + 512],
                           start=(lc == 0), stop=(lc == 3))
        s_mm.inc(mm)
        s_mm.wait(V)
        s_kv.inc(V.tensor_copy(kT64[:, 512 * I:512 * I + 512], ps[0:64, 3584:4096]))
    for j in range(nt):
        s_kv.wait(TE)
        for lc in range(4):
            mm = TE.matmul(ps[:, 3584:3712],
                           latT[:, T * lc + 128 * j:T * lc + 128 * j + 128],
                           wkv_bf[:, 128 * lc:128 * lc + 128],
                           start=(lc == 0), stop=(lc == 3))
        s_mm.inc(mm)
        s_mm.wait(V)
        s_kv.inc(V.tensor_copy(vext[:, 66 * j:66 * j + 64], ps[:, 3648:3712]))
    V.memset(knt[:, :], 0.0)
    s_vec.inc(V.engine_nop())
    for I in range(ns):
        s_mm.wait(S)
        s_kv.wait(S)
        s_sca.inc(S.activation(sqk[:, :], kT64[:, 512 * I:512 * I + 512], AF.Square))
        s_sca.wait(TE)
        s_vec.wait(TE)
        pslot = ps[0:1, 3584:4096]
        s_mm.inc(TE.matmul(pslot, ones64[:, :], sqk[:, :], start=True, stop=True))
        s_mm.wait(V)
        V.tensor_copy(arow[:, :], pslot)
        s_vec.inc(V.engine_nop())
        s_vec.wait(TE)
        s_mm.inc(TE.matmul(ps[:, 3072:3584], ones1b[:, :], arow[:, :],
                           start=True, stop=True))
        s_mm.wait(V)
        V.reduce_max(kn2t[:, :], ps[:, 3072:3584], axis=AX.X)
        vsp()
        s_vec.inc(V.tensor_max(knt[:, :], knt[:, :], kn2t[:, :]))
    knt_ready = s_vec.v

    out_marks = []
    gpar = 0
    psum_free = {0: 0, 1: 0}
    pbf_free = {0: 0, 1: 0}
    for I in range(ns):
        nblk = 4 * I + 4
        ngrp = (nblk + 1) // 2
        s_asm.wait(TE)
        for h in range(H):
            pslot = ps[0:64, 3584:4096]
            if not (I == 0 and h == 0):
                s_qp.wait(TE)
            for lc in range(4):
                mm = TE.matmul(pslot,
                               wq_bf[:, 1024 * lc + 64 * h:1024 * lc + 64 * h + 64],
                               latT[:, T * lc + 512 * I:T * lc + 512 * I + 512],
                               start=(lc == 0), stop=(lc == 3))
            s_mm.inc(mm)
            s_mm.wait(V)
            s_qp.inc(V.tensor_copy(q64[:, 512 * h:512 * h + 512], pslot))
        s_qp.wait(V)
        s_vec.inc(V.tensor_mul(sq64[:, :], q64[:, :], q64[:, :]))
        s_vec.wait(TE)
        for h in range(H):
            mm = TE.matmul(ps[0:1, 3072:3584], ones64[:, :],
                           sq64[:, 512 * h:512 * h + 512],
                           start=(h == 0), stop=(h == H - 1))
        s_mm.inc(mm)
        s_mm.wait(V)
        s_vec.inc(V.tensor_copy(arow[:, :], ps[0:1, 3072:3584]))
        s_vec.wait(TE)
        s_mm.inc(TE.matmul(ps[:, 3584:4096], ones1b[:, :], arow[:, :],
                           start=True, stop=True))
        s_mm.wait(V)
        V.reduce_max(amaxv[:, :], ps[:, 3584:4096], axis=AX.X)
        vsp()
        s_vec.inc(V.tensor_scalar(amaxv[:, :], amaxv[:, :], 1e-30, None, ALU.max))
        s_vec.wait(S)
        S.activation(amaxv[:, :], amaxv[:, :], AF.Ln, scale=knt[:, 0:1])
        s_sca.inc(S.activation(amaxv[:, :], amaxv[:, :], AF.Exp, scale=0.5))
        s_sca.wait(V)
        s_nb.inc(V.tensor_scalar(nbias[:, :], amaxv[:, :], -1.0, 35.0,
                                 ALU.mult, ALU.add))
        nb_ready = s_nb.v
        for h in range(H):
            exp_at = {}
            par = {}
            sc_at = {}

            def emit_pv(gx):
                s_exp.wait(TE, exp_at[gx])
                if gx == 0:
                    s_ye.wait(TE)
                j0v = 2 * gx
                jnv = min(2, nblk - j0v)
                last = None
                for jj in range(jnv):
                    j = j0v + jj
                    for ti in range(4):
                        if j <= 4 * I + ti:
                            last = TE.matmul(
                                ps[:, 2048 + 512 * ti:2048 + 512 * ti + 65],
                                pbf[par[gx]][:, 512 * jj + 128 * ti:
                                             512 * jj + 128 * ti + 128],
                                vext[:, 66 * j:66 * j + 65],
                                start=(j == 0), stop=(j == 4 * I + ti))
                s_pv.inc(last)
                pbf_free[par[gx]] = s_pv.v

            for g in range(ngrp):
                pp = gpar % 2
                gpar += 1
                par[g] = pp
                j0 = 2 * g
                jn = min(2, nblk - j0)
                s_exp.wait(TE, psum_free[pp])
                for jj in range(jn):
                    j = j0 + jj
                    s_sc.inc(TE.matmul(
                        ps[:, 1024 * pp + 512 * jj:1024 * pp + 512 * jj + 512],
                        kT64[:, 128 * j:128 * j + 128],
                        q64[:, 512 * h:512 * h + 512],
                        start=True, stop=True))
                sc_at[g] = s_sc.v
                nmsk = 0
                for jj in range(jn):
                    di = j0 + jj - 4 * I
                    if di >= 0:
                        if nmsk == 0:
                            s_sc.wait(V, sc_at[g])
                            s_bld.wait_mark(V, mask_mk)
                        pvw = ps[:, 1024 * pp + 512 * jj:
                                 1024 * pp + 512 * jj + 128 * (di + 1)]
                        s_msk.inc(V.tensor_tensor(
                            pvw, pvw,
                            mask4[:, 512 * di:512 * di + 128 * (di + 1)], ALU.add))
                        nmsk += 1
                if g >= 1:
                    emit_pv(g - 1)
                s_sc.wait(S, sc_at[g])
                if nmsk:
                    s_msk.wait(S)
                s_pv.wait(S, pbf_free[pp])
                s_nb.wait(S, nb_ready)
                s_exp.inc(S.activation(pbf[pp][:, 0:512 * jn],
                                       ps[:, 1024 * pp:1024 * pp + 512 * jn],
                                       AF.Exp, bias=nbias[:, 0:1]))
                exp_at[g] = s_exp.v
                psum_free[pp] = s_exp.v
            emit_pv(ngrp - 1)
            s_pv.wait(V)
            for ti in range(4):
                V.reciprocal(drb[:, ti:ti + 1],
                             ps[:, 2048 + 512 * ti + 64:2048 + 512 * ti + 65])
            vsp()
            for ti in range(4):
                s_ye.inc(V.tensor_scalar(
                    ybuf[:, D * ti + 64 * h:D * ti + 64 * h + 64],
                    ps[:, 2048 + 512 * ti:2048 + 512 * ti + 64],
                    drb[:, ti:ti + 1], None, ALU.mult))
        s_ye.wait(SY)
        if I > 0:
            s_pr.wait(SY)
        s_tp.begin(SY)
        for ti in range(4):
            for c in range(8):
                s_tp.issue(SY.dma_start(
                    yT[:, 512 * c + 128 * ti:512 * c + 128 * ti + 128],
                    ybuf[:, D * ti + 128 * c:D * ti + 128 * c + 128],
                    transpose=True))
        s_tp.wait_all(TE)
        for ti in range(4):
            i = 4 * I + ti
            b = xb[i % 2]
            fb = fbk[i % 2]
            s_xfr.wait(SY, s_xfr.v - (1 if i > 0 else 0))
            s_xld.begin(SY)
            s_xld.issue(SY.dma_start(b[:, :], x_h[i * 128:(i + 1) * 128, :]))
            asm_xmk = s_xld.mark()
            if i >= 2:
                s_asm.wait(G, s_asm.v - 2)
            s_fb.begin(G)
            s_fb.issue(G.dma_start(fb[:, :], spill_h[i * 128:(i + 1) * 128, :]))
            fb_mk = s_fb.mark()
            for half in range(2):
                pslot = ps[:, 3072 + 512 * half:3072 + 512 * half + 512]
                if not (I == 0 and ti == 0):
                    s_asm.wait(TE)
                for c in range(8):
                    mm = TE.matmul(
                        pslot, yT[:, 512 * c + 128 * ti:512 * c + 128 * ti + 128],
                        wpr_bf[:, 1024 * c + 512 * half:1024 * c + 512 * half + 512],
                        start=(c == 0), stop=(c == 7))
                s_pr.inc(mm)
            o = outst[i % 2]
            s_pr.wait(V)
            s_xld.wait_mark(V, asm_xmk)
            s_fb.wait_mark(V, fb_mk)
            if i >= 2:
                s_out.wait_mark(V, out_marks[i - 2])
            for half in range(2):
                pslot = ps[:, 3072 + 512 * half:3072 + 512 * half + 512]
                sl = slice(512 * half, 512 * half + 512)
                V.tensor_add(o[:, sl], pslot, b[:, sl])
                V.scalar_tensor_tensor(o[:, sl], fb[:, sl], stc(11)[:, i:i + 1],
                                       o[:, sl], ALU.mult, ALU.add)
                s_asm.inc(V.tensor_add(o[:, sl], o[:, sl], b2rep[:, sl]))
            s_xfr.inc(V.engine_nop())
            s_asm.wait(G)
            s_out.begin(G)
            s_out.issue(G.dma_start(out_h[i * 128:(i + 1) * 128, :], o[:, :]))
            out_marks.append(s_out.mark())

    s_out.wait_all(G)
    st.close()
    return nc


def _mask4_np():
    m = np.zeros((128, 4 * 512), dtype=np.float32)
    r = np.arange(128)
    for di in range(4):
        blk = m[:, 512 * di:512 * di + 512]
        blk[:, 0:128 * di] = NEG
        tri = blk[:, 128 * di:128 * di + 128]
        tri[r[:, None] > r[None, :]] = NEG
    return m


def kernel(L0_attn_x, L0_attn_Wc_down, L0_attn_Wq_up, L0_attn_Wkv_up,
           L0_attn_Wc_proj, L0_ffn_w1, L0_ffn_b1, L0_ffn_w2, L0_ffn_b2):
    T = 4096
    nc = build_nc(T)
    x = np.ascontiguousarray(np.asarray(L0_attn_x, dtype=np.float32))
    shared = {
        "wcd": np.ascontiguousarray(np.asarray(L0_attn_Wc_down, dtype=np.float32)),
        "wq": np.ascontiguousarray(np.asarray(L0_attn_Wq_up, dtype=np.float32)),
        "wkv": np.ascontiguousarray(np.asarray(L0_attn_Wkv_up, dtype=np.float32)),
        "wpr": np.ascontiguousarray(np.asarray(L0_attn_Wc_proj, dtype=np.float32)),
        "w1": np.ascontiguousarray(np.asarray(L0_ffn_w1, dtype=np.float32)),
        "b1": np.ascontiguousarray(np.asarray(L0_ffn_b1, dtype=np.float32)).reshape(1, -1),
        "w2": np.ascontiguousarray(np.asarray(L0_ffn_w2, dtype=np.float32)),
        "b2": np.ascontiguousarray(np.asarray(L0_ffn_b2, dtype=np.float32)).reshape(1, -1),
        "msk": _mask4_np(),
    }
    in_maps = [dict(shared, x=x[c]) for c in range(8)]
    res = run_bass_kernel_spmd(nc, in_maps, core_ids=list(range(8)))
    return np.stack([np.asarray(res.results[c]["out"]) for c in range(8)], axis=0)


# version 09
# speedup vs baseline: 1.2068x; 1.1334x over previous
import numpy as np
import ml_dtypes
from contextlib import ExitStack

import concourse.bass as bass
import concourse.mybir as mybir
from concourse.bass_utils import run_bass_kernel_spmd

F32 = mybir.dt.float32
BF16 = mybir.dt.bfloat16
AF = mybir.ActivationFunctionType
ALU = mybir.AluOpType
AX = mybir.AxisListType

MAGIC = 12582912.0
NEG = -1.0e30
D = 1024
L = 512
FF = 4096
H = 16
DH = 64
EPS = 1e-5


def build_nc(T=4096):
    nc = bass.Bass("TRN2", target_bir_lowering=False, detect_race_conditions=False)
    nt = T // 128
    ns = T // 512

    x_h = nc.dram_tensor("x", [T, D], F32, kind="ExternalInput")
    wcd_h = nc.dram_tensor("wcd", [D, L], F32, kind="ExternalInput")
    wq_h = nc.dram_tensor("wq", [L, D], F32, kind="ExternalInput")
    wkv_h = nc.dram_tensor("wkv", [L, 2 * DH], F32, kind="ExternalInput")
    wpr_h = nc.dram_tensor("wpr", [D, D], F32, kind="ExternalInput")
    w1_h = nc.dram_tensor("w1", [FF, D], F32, kind="ExternalInput")
    b1_h = nc.dram_tensor("b1", [1, FF], F32, kind="ExternalInput")
    w2_h = nc.dram_tensor("w2", [D, FF], F32, kind="ExternalInput")
    b2_h = nc.dram_tensor("b2", [1, D], F32, kind="ExternalInput")
    msk_h = nc.dram_tensor("msk", [128, 4 * 512], F32, kind="ExternalInput")
    idn_h = nc.dram_tensor("idn", [128, 128], BF16, kind="ExternalInput")
    out_h = nc.dram_tensor("out", [T, D], F32, kind="ExternalOutput")
    spill_h = nc.dram_tensor("ffn_spill", [T, D], BF16, kind="ExternalOutput")

    st = ExitStack()
    sb = lambda name, shape, dt: st.enter_context(nc.sbuf_tensor(name, shape, dt))
    sem = lambda name: st.enter_context(nc.semaphore(name))

    xb = [sb("xb0", [128, D], F32), sb("xb1", [128, D], F32)]
    b2rep = sb("b2rep", [128, D], BF16)
    ones1f = sb("ones1f", [1, 128], F32)
    ones1b = sb("ones1b", [1, 128], BF16)
    onesff = sb("onesff", [128, 128], F32)
    ones64 = sb("ones64", [64, 1], BF16)
    idn = sb("idn_sb", [128, 128], BF16)
    stt = sb("stt", [128, 16 * nt], F32)

    def stc(i):
        return stt[:, i * nt:(i + 1) * nt]

    w1s = sb("w1s", [128, 1], F32)
    mw1s = sb("mw1s", [128, 1], F32)
    w2s = sb("w2s", [128, 1], F32)
    mw2s = sb("mw2s", [128, 1], F32)
    wabs = sb("wabs", [128, 40], F32)
    brow = sb("brow", [1, 512], F32)
    tmp1 = sb("tmp1", [128, 1], F32)
    tmp2 = sb("tmp2", [128, 1], F32)
    tmp3 = sb("tmp3", [128, 128], F32)

    ps = st.enter_context(nc.psum_tensor("ps", [128, 4096], F32))

    SY, V, S, TE, G = nc.sync, nc.vector, nc.scalar, nc.tensor, nc.gpsimd

    class Ctr:
        def __init__(self, name):
            self.h = sem(name)
            self.v = 0

        def inc(self, inst, n=1):
            inst.then_inc(self.h, n)
            self.v += n

        def wait(self, eng, val=None):
            val = self.v if val is None else val
            if val > 0:
                eng.wait_ge(self.h, val)

    class DCtr:

        def __init__(self, name):
            self.h = [sem(name + "a"), sem(name + "b")]
            self.v = [0, 0]
            self.b = 0
            self.cur = None

        def __init_serial__(self):
            pass

        def begin(self, eng):
            p = self.b % 2
            self.b += 1
            self.cur = p
            if getattr(self, "serial", False):
                for pp in (0, 1):
                    if self.v[pp] > 0:
                        eng.wait_ge(self.h[pp], self.v[pp])
            elif self.v[p] > 0:
                eng.wait_ge(self.h[p], self.v[p])
            return p

        def issue(self, inst):
            p = self.cur
            inst.then_inc(self.h[p], 16)
            self.v[p] += 16

        def mark(self):
            return (self.cur, self.v[self.cur])

        def wait_mark(self, eng, mk):
            if mk is not None:
                eng.wait_ge(self.h[mk[0]], mk[1])

        def wait_all(self, eng):
            for p in (0, 1):
                if self.v[p] > 0:
                    eng.wait_ge(self.h[p], self.v[p])

    s_xld = DCtr("s_xld")
    s_xfr = Ctr("s_xfr")
    s_xfs = Ctr("s_xfs")
    s_bld = DCtr("s_bld")
    s_mm = Ctr("s_mm")
    s_vec = Ctr("s_vec")
    s_sca = Ctr("s_sca")
    s_wld = DCtr("s_wld")
    s_wfr = Ctr("s_wfr")
    s_tp = DCtr("s_tp")
    s_tpm = Ctr("s_tpm")
    s_he6 = Ctr("s_he6")
    s_he7 = Ctr("s_he7")
    s_ff1 = Ctr("s_ff1")
    s_hev = Ctr("s_hev")
    s_ff2 = Ctr("s_ff2")
    s_f2e = Ctr("s_f2e")
    s_sp = DCtr("s_sp")
    s_q1 = Ctr("s_q1")
    s_q2 = Ctr("s_q2")
    s_g = Ctr("s_g")
    s_lat = Ctr("s_lat")
    s_sc = Ctr("s_sc")
    s_msk = Ctr("s_msk")
    s_exp = Ctr("s_exp")
    s_pv = Ctr("s_pv")
    s_ye = Ctr("s_ye")
    s_qp = Ctr("s_qp")
    s_nb = Ctr("s_nb")
    s_out = DCtr("s_out")
    s_fb = DCtr("s_fb")
    s_pr = Ctr("s_pr")
    s_asm = Ctr("s_asm")
    s_kv = Ctr("s_kv")
    s_wc = DCtr("s_wc")

    def vsp():
        V.tensor_scalar(tmp3[:, :], tmp3[:, :], 1.0, None, ALU.mult)

    s_bld.begin(SY)
    s_bld.issue(SY.dma_start(idn[:, :], idn_h[:, :]))
    idn_mk = s_bld.mark()
    V.memset(tmp3[:, :], 0.0)
    V.memset(tmp1[:, :], 0.0)
    V.memset(ones1f[:, :], 1.0)
    V.memset(ones1b[:, :], 1.0)
    V.memset(onesff[:, :], 1.0)
    s_vec.inc(V.memset(ones64[:, :], 1.0))

    p1st = ExitStack()
    sb1 = lambda name, shape, dt: p1st.enter_context(nc.sbuf_tensor(name, shape, dt))
    t1b = sb1("t1b", [128, D], F32)
    b1rep = sb1("b1rep", [128, FF], BF16)
    w1qT = sb1("w1qT", [128, 8 * FF], BF16)
    w2qT = sb1("w2qT", [128, 32 * D], BF16)
    p0wst = ExitStack()
    big_a = p0wst.enter_context(nc.sbuf_tensor("big_a", [128, FF], F32))
    wq_sb = p0wst.enter_context(nc.sbuf_tensor("wqs0", [128, 1024], BF16))

    for i in range(nt):
        b = xb[i % 2]
        if i >= 2:
            s_xfr.wait(SY, i - 1)
            s_xfs.wait(SY, i - 1)
        s_xld.begin(SY)
        s_xld.issue(SY.dma_start(b[:, :], x_h[i * 128:(i + 1) * 128, :]))
        xmk = s_xld.mark()
        s_xld.wait_mark(V, xmk)
        s_xfr.inc(V.reduce_sum(stc(0)[:, i:i + 1], b[:, :], axis=AX.X))
        V.reduce_max(stc(2)[:, i:i + 1], b[:, :], axis=AX.X)
        V.tensor_reduce(stc(3)[:, i:i + 1], b[:, :], axis=AX.X, op=ALU.min)
        s_xld.wait_mark(S, xmk)
        s_xfs.inc(S.activation(t1b[:, :], b[:, :], AF.Square,
                               accum_out=stc(1)[:, i:i + 1]))

    s_xld.wait_all(V)
    s_xfr.wait(V, nt)
    s_xfs.wait(V, nt)
    inv_d = 1.0 / D
    m = stc(12)
    V.tensor_scalar(m, stc(0), inv_d, None, ALU.mult)
    V.tensor_scalar(stc(13), stc(1), inv_d, None, ALU.mult)
    V.tensor_mul(stc(14), m, m)
    vsp()
    V.tensor_sub(stc(1), stc(13), stc(14))
    vsp()
    s_vec.inc(V.tensor_scalar(stc(15), stc(1), EPS, None, ALU.add))
    s_vec.wait(S)
    s_sca.inc(S.activation(stc(14), stc(15), AF.Ln))
    s_sca.wait(V)
    s_vec.inc(V.tensor_scalar(stc(14), stc(14), -0.5, None, ALU.mult))
    s_vec.wait(S)
    s_sca.inc(S.activation(stc(4), stc(14), AF.Exp))
    s_sca.wait(V)
    V.scalar_tensor_tensor(stc(5), stc(4), -1.0, m, ALU.mult, ALU.mult)
    V.tensor_sub(stc(2), stc(2), m)
    V.tensor_sub(stc(3), m, stc(3))
    vsp()
    V.tensor_max(stc(2), stc(2), stc(3))
    vsp()
    V.tensor_mul(stc(2), stc(2), stc(4))
    V.tensor_mul(stc(14), stc(4), stc(4))
    vsp()
    V.tensor_mul(stc(14), stc(1), stc(14))
    vsp()
    s_vec.inc(V.tensor_scalar(stc(14), stc(14), EPS, None, ALU.add))
    s_vec.wait(S)
    s_sca.inc(S.activation(stc(15), stc(14), AF.Ln))
    s_sca.wait(V)
    s_vec.inc(V.tensor_scalar(stc(15), stc(15), -0.5, None, ALU.mult))
    s_vec.wait(S)
    s_sca.inc(S.activation(stc(3), stc(15), AF.Exp))
    s_sca.wait(V)
    V.tensor_mul(stc(2), stc(2), stc(3))
    vsp()
    s_vec.inc(V.tensor_scalar(stc(2), stc(2), 1e-5, None, ALU.max))
    s_vec.wait(S)
    s_sca.inc(S.activation(stc(15), stc(2), AF.Ln))
    s_sca.wait(V)
    s_vec.inc(V.tensor_scalar(stc(15), stc(15), -1.0, None, ALU.mult))
    s_vec.wait(S)
    s_sca.inc(S.activation(stc(14), stc(15), AF.Exp))
    s_sca.wait(V)
    V.tensor_scalar(stc(14), stc(14), 127.0, None, ALU.mult)
    V.tensor_mul(stc(6), stc(4), stc(3))
    V.tensor_mul(stc(7), stc(5), stc(3))
    V.tensor_mul(stc(6), stc(6), stc(14))
    s_vec.inc(V.tensor_mul(stc(7), stc(7), stc(14)))


    s_vec.wait(TE)
    for c in range(FF // 512 + D // 512):
        if c < FF // 512:
            src_ap = b1_h[0:1, c * 512:(c + 1) * 512]
            dst = b1rep[:, c * 512:(c + 1) * 512]
        else:
            c2 = c - FF // 512
            src_ap = b2_h[0:1, c2 * 512:(c2 + 1) * 512]
            dst = b2rep[:, c2 * 512:(c2 + 1) * 512]
        pslot = ps[:, (c % 2) * 512:(c % 2) * 512 + 512]
        if c >= 1:
            s_mm.wait(SY)
        s_bld.begin(SY)
        s_bld.issue(SY.dma_start(brow[:, :], src_ap))
        s_bld.wait_mark(TE, s_bld.mark())
        if c >= 2:
            s_vec.wait(TE, s_vec.v - 1)
        s_mm.inc(TE.matmul(pslot, ones1f[:, :], brow[:, :], start=True, stop=True))
        s_mm.wait(V)
        s_vec.inc(V.tensor_copy(dst, pslot))


    def w_pass1(w_h, cols, nstr, off):
        for sx in range(nstr):
            b = big_a
            s_wfr.wait(G)
            s_wld.begin(G)
            s_wld.issue(G.dma_start(b[:, 0:cols], w_h[sx * 128:(sx + 1) * 128, :]))
            s_wld.wait_mark(V, s_wld.mark())
            s_wfr.inc(V.reduce_sum(wabs[:, off + sx:off + sx + 1], b[:, 0:cols],
                                   axis=AX.X, apply_absolute_value=True))

    w_pass1(w1_h, D, FF // 128, 0)
    w_pass1(w2_h, FF, D // 128, 32)
    s_wfr.wait(V)
    s_vec.inc(V.engine_nop())
    s_vec.wait(TE)
    s_mm.inc(TE.matmul(ps[:, 2048:2088], onesff[:, :], wabs[:, 0:40],
                       start=True, stop=True))
    s_mm.wait(V)
    V.reduce_sum(w1s[:, :], ps[:, 2048:2080], axis=AX.X)
    V.reduce_sum(w2s[:, :], ps[:, 2080:2088], axis=AX.X)
    for wsx, mwx, nel in ((w1s, mw1s, FF * D), (w2s, mw2s, D * FF)):
        V.tensor_scalar(tmp2[:, :], wsx[:, :], 1.0 / nel, None, ALU.mult)
        vsp()
        s_vec.inc(V.tensor_scalar(tmp2[:, :], tmp2[:, :], 1e-5, None, ALU.max))
        s_vec.wait(S)
        s_sca.inc(S.activation(tmp1[:, :], tmp2[:, :], AF.Copy))
        s_sca.wait(V)
        V.reciprocal(wsx[:, :], tmp1[:, :])
        vsp()
        s_vec.inc(V.tensor_scalar(mwx[:, :], tmp2[:, :], 1.0 / 127.0, None, ALU.mult))
    s_vec.inc(V.engine_nop())
    V.tensor_scalar(stc(8), stc(2), mw1s[:, 0:1], None, ALU.mult)

    def w_pass2(w_h, cols, nstr, s_rec, qT, chunk_cols):
        q = wq_sb
        for sx in range(nstr):
            b = big_a
            s_wfr.wait(G)
            s_wld.begin(G)
            s_wld.issue(G.dma_start(b[:, 0:cols], w_h[sx * 128:(sx + 1) * 128, :]))
            s_wld.wait_mark(V, s_wld.mark())
            V.tensor_scalar(b[:, 0:cols], b[:, 0:cols], s_rec[:, 0:1], MAGIC,
                            ALU.mult, ALU.add)
            V.tensor_scalar(b[:, 0:cols], b[:, 0:cols], MAGIC, 1.0,
                            ALU.subtract, ALU.min)
            for qc in range(cols // 1024):
                s_tp.wait_all(V)
                s_vec.inc(V.tensor_scalar(q[:, :], b[:, 1024 * qc:1024 * qc + 1024],
                                           -1.0, None, ALU.max))
                if qc == cols // 1024 - 1:
                    s_wfr.inc(V.engine_nop())
                s_vec.wait(SY)
                s_tp.begin(SY)
                for k in range(8):
                    kk = 8 * qc + k
                    s_tp.issue(SY.dma_start(
                        qT[:, chunk_cols * kk + 128 * sx:chunk_cols * kk + 128 * sx + 128],
                        q[:, 128 * k:128 * k + 128], transpose=True))

    w_pass2(w1_h, D, FF // 128, w1s, w1qT, FF)
    w_pass2(w2_h, FF, D // 128, w2s, w2qT, D)

    p0wst.close()
    hg = [sb1("hg0", [128, FF], BF16), sb1("hg1", [128, FF], BF16)]
    xq1 = sb1("xq1", [128, D], BF16)
    xq1T = sb1("xq1T", [128, D], BF16)
    hq = [sb1("hq0", [128, 1024], BF16), sb1("hq1", [128, 1024], BF16)]
    hqT = sb1("hqT", [128, FF], BF16)
    qtmp = sb1("qtmp", [128, 1024], F32)
    spst = [sb1("spst0", [128, D], BF16), sb1("spst1", [128, D], BF16)]
    sp_marks = []
    hev_mark = {}
    ff2_mark = {}
    xq1_mks = {}
    sq_mark = {}

    def p1_stageA(i):
        b = xb[i % 2]
        if i == 0:
            s_xfr.wait(SY)
            s_xfs.wait(SY)
            p1_stageA.base = s_xfr.v
        else:
            s_xfr.wait(SY, p1_stageA.base + i - 1)
        s_xld.begin(SY)
        s_xld.issue(SY.dma_start(b[:, :], x_h[i * 128:(i + 1) * 128, :]))
        s_xld.wait_mark(V, s_xld.mark())
        s_xfr.inc(V.tensor_scalar(t1b[:, :], b[:, :], stc(6)[:, i:i + 1],
                                  stc(7)[:, i:i + 1], ALU.mult, ALU.add))
        if i > 0:
            s_tp.wait_mark(V, xq1_mks[i - 1])
        else:
            s_tp.wait_all(V)
        s_q1.inc(V.tensor_scalar(xq1[:, :], t1b[:, :], MAGIC, MAGIC,
                                 ALU.add, ALU.subtract))
        s_q1.wait(SY)
        if i > 0:
            s_ff1.wait(SY, 8 * i)
        else:
            s_tp.wait_all(SY)
        s_tp.begin(SY)
        for k in range(8):
            s_tp.issue(SY.dma_start(xq1T[:, 128 * k:128 * k + 128],
                                    xq1[:, 128 * k:128 * k + 128], transpose=True))
        xq1_mks[i] = s_tp.mark()

    def p1_ffn1(i):
        if i == 0:
            s_tp.wait_all(TE)
            s_wfr.wait(V)
        s_tp.wait_mark(TE, xq1_mks[i])
        for fc in range(8):
            pslot = ps[:, (fc % 4) * 512:(fc % 4) * 512 + 512]
            s_hev.wait(TE, s_hev.v - 3)
            for dc in range(8):
                mm = TE.matmul(pslot, xq1T[:, 128 * dc:128 * dc + 128],
                               w1qT[:, FF * dc + 512 * fc:FF * dc + 512 * fc + 512],
                               start=(dc == 0), stop=(dc == 7))
            s_ff1.inc(mm)
            s_ff1.wait(V)
            if fc == 0 and i >= 2:
                s_sca.wait(V, sq_mark[i - 2])
            s_hev.inc(V.scalar_tensor_tensor(
                hg[i % 2][:, 512 * fc:512 * fc + 512], pslot, stc(8)[:, i:i + 1],
                b1rep[:, 512 * fc:512 * fc + 512], ALU.mult, ALU.add))
        hev_mark[i] = s_hev.v

    def p1_post_ffn2(i):
        h = hg[i % 2]
        g = hqT
        s_hev.wait(S, hev_mark[i])
        if i > 0:
            s_ff2.wait(S, ff2_mark[i - 1])
        s_g.inc(S.activation(g[:, :], h[:, :], AF.Gelu))
        s_g.wait(V)
        s_vec.inc(V.reduce_max(stc(13)[:, i:i + 1], g[:, :], axis=AX.X,
                               apply_absolute_value=True))
        s_g.wait(S)
        s_sca.inc(S.activation(h[:, :], g[:, :], AF.Square,
                               accum_out=stc(9)[:, i:i + 1]))
        sq_mark[i] = s_sca.v
        s_vec.wait(S)
        s_sca.inc(S.activation(stc(15)[:, i:i + 1], stc(13)[:, i:i + 1], AF.Copy))
        s_sca.wait(V)
        V.reciprocal(stc(14)[:, i:i + 1], stc(15)[:, i:i + 1])
        V.tensor_scalar(stc(10)[:, i:i + 1], stc(13)[:, i:i + 1], mw2s[:, 0:1],
                        None, ALU.mult)
        vsp()
        V.tensor_scalar(stc(14)[:, i:i + 1], stc(14)[:, i:i + 1],
                        127.0, None, ALU.mult)
        vsp()
        s_vec.inc(V.engine_nop())
        for kc in range(4):
            ck = i * 4 + kc
            pk = ck - 2
            if pk >= 0:
                s_tpm.wait(V, 2 * (pk + 1))
            V.tensor_scalar(qtmp[:, :], g[:, 1024 * kc:1024 * kc + 1024],
                            stc(14)[:, i:i + 1], MAGIC, ALU.mult, ALU.add)
            s_q2.inc(V.tensor_scalar(hq[kc % 2][:, :], qtmp[:, :], MAGIC, None,
                                     ALU.subtract))
            s_q2.wait(TE)
            if ck == 0:
                s_bld.wait_mark(TE, idn_mk)
            if ck >= 1:
                s_he6.wait(TE, ck)
            for k in range(4):
                mm = TE.matmul(ps[:, 3072 + 128 * k:3072 + 128 * k + 128],
                               hq[kc % 2][:, 128 * k:128 * k + 128],
                               idn[:, :], start=True, stop=True)
            s_tpm.inc(mm)
            if ck >= 1:
                s_he7.wait(TE, ck)
            for k in range(4, 8):
                mm = TE.matmul(ps[:, 3584 + 128 * (k - 4):3584 + 128 * (k - 4) + 128],
                               hq[kc % 2][:, 128 * k:128 * k + 128],
                               idn[:, :], start=True, stop=True)
            s_tpm.inc(mm)
            s_tpm.wait(V, 2 * ck + 1)
            s_he6.inc(V.tensor_copy(hqT[:, 1024 * kc:1024 * kc + 512],
                                    ps[:, 3072:3584]))
            s_tpm.wait(S, 2 * ck + 2)
            s_he7.inc(S.activation(hqT[:, 1024 * kc + 512:1024 * kc + 1024],
                                   ps[:, 3584:4096], AF.Copy))
        s_he6.wait(TE, 4 * (i + 1))
        s_he7.wait(TE, 4 * (i + 1))
        s_f2e.wait(TE)
        sp = spst[i % 2]
        for dc2 in range(2):
            pslot = ps[:, 2048 + dc2 * 512:2048 + dc2 * 512 + 512]
            for fc in range(32):
                mm = TE.matmul(pslot, hqT[:, 128 * fc:128 * fc + 128],
                               w2qT[:, D * fc + 512 * dc2:D * fc + 512 * dc2 + 512],
                               start=(fc == 0), stop=(fc == 31))
            s_ff2.inc(mm)
            s_ff2.wait(V)
            if i >= 2 and dc2 == 0:
                s_sp.wait_mark(V, sp_marks[i - 2])
            s_f2e.inc(V.tensor_scalar(sp[:, 512 * dc2:512 * dc2 + 512], pslot,
                                      stc(10)[:, i:i + 1], None, ALU.mult))
        ff2_mark[i] = s_ff2.v
        s_f2e.wait(G)
        s_sp.begin(G)
        s_sp.issue(G.dma_start(spill_h[i * 128:(i + 1) * 128, :], sp[:, :]))
        sp_marks.append(s_sp.mark())

    for i in range(nt + 1):
        if i < nt:
            p1_stageA(i)
            p1_ffn1(i)
        if i >= 1:
            p1_post_ffn2(i - 1)

    p1st.close()
    for eng in (G, SY, V, S, TE):
        s_sp.wait_all(eng)
    V.tensor_scalar(stc(15), stc(9), 1.0 / FF, EPS, ALU.mult, ALU.add)
    s_vec.inc(V.engine_nop())
    s_vec.wait(S)
    S.activation(stc(15), stc(15), AF.Ln)
    s_sca.inc(S.activation(stc(15), stc(15), AF.Exp, scale=0.5))
    s_sca.wait(V)
    s_vec.inc(V.reciprocal(stc(11), stc(15)))

    wcd_bf = sb("wcd_bf", [128, 8 * 512], BF16)
    wq_bf = sb("wq_bf", [128, 4 * 1024], BF16)
    wkv_bf = sb("wkv_bf", [128, 4 * 128], BF16)
    wpr_bf = sb("wpr_bf", [128, 8 * 1024], BF16)
    wtmp = sb("wtmp", [128, 1024], F32)

    def load_cast(src_h, nchunk, ccols, dst, scale=None):
        for c in range(nchunk):
            s_vec.wait(G)
            s_wc.begin(G)
            s_wc.issue(G.dma_start(wtmp[:, 0:ccols], src_h[c * 128:(c + 1) * 128, :]))
            s_wc.wait_mark(V, s_wc.mark())
            if scale is None:
                s_vec.inc(V.tensor_copy(dst[:, c * ccols:(c + 1) * ccols],
                                        wtmp[:, 0:ccols]))
            else:
                s_vec.inc(V.tensor_scalar(dst[:, c * ccols:(c + 1) * ccols],
                                          wtmp[:, 0:ccols], scale, None, ALU.mult))

    load_cast(wcd_h, 8, 512, wcd_bf)
    load_cast(wq_h, 4, 1024, wq_bf, scale=0.125)
    load_cast(wkv_h, 4, 128, wkv_bf)
    load_cast(wpr_h, 8, 1024, wpr_bf)
    wts_ready = s_vec.v
    mask4 = sb("mask4", [128, 2048], F32)
    s_bld.begin(SY)
    s_bld.issue(SY.dma_start(mask4[:, :], msk_h[:, :]))
    mask_mk = s_bld.mark()

    outst = [sb("outst0", [128, D], F32), sb("outst1", [128, D], F32)]
    fbk = [sb("fbk0", [128, D], BF16), sb("fbk1", [128, D], BF16)]
    kT64 = sb("kT64", [64, T], BF16)
    vext = sb("vext", [128, 66 * nt], BF16)
    latT = sb("latT", [128, 4 * T], BF16)
    xnbf = sb("xnbf", [128, D], BF16)
    xnT = sb("xnT", [128, 8 * 512], BF16)
    q64 = sb("q64", [64, 16 * 512], BF16)
    sq64 = sb("sq64", [64, 16 * 512], BF16)
    arow = sb("arow", [1, 512], BF16)
    amaxv = sb("amaxv", [128, 1], F32)
    nbias = sb("nbias", [128, 1], F32)
    knt = sb("knt", [128, 1], F32)
    kn2t = sb("kn2t", [128, 1], F32)
    sqk = sb("sqk", [64, 512], BF16)
    pbf = [sb("pbf0", [128, 1024], BF16), sb("pbf1", [128, 1024], BF16)]
    ybuf = sb("ybuf", [128, 4 * D], BF16)
    yT = sb("yT", [128, 8 * 512], BF16)
    drb = sb("drb", [128, 4], F32)

    s_vec.inc(V.memset(vext[:, :], 1.0))
    vext_ones = s_vec.v

    lat_marks = []
    for I in range(ns):
        for t4 in range(4):
            i = 4 * I + t4
            b = xb[i % 2]
            s_xfr.wait(SY, s_xfr.v - (1 if i > 0 else 0))
            s_xld.begin(SY)
            s_xld.issue(SY.dma_start(b[:, :], x_h[i * 128:(i + 1) * 128, :]))
            s_xld.wait_mark(V, s_xld.mark())
            if i > 0:
                s_tp.wait_mark(V, xn_mk)
            s_xfr.inc(V.tensor_scalar(xnbf[:, :], b[:, :], stc(4)[:, i:i + 1],
                                      stc(5)[:, i:i + 1], ALU.mult, ALU.add))
            s_xfr.wait(SY)
            if I > 0 and t4 == 0:
                s_mm.wait(SY, lat_marks[-1])
            s_tp.begin(SY)
            for k in range(8):
                s_tp.issue(SY.dma_start(
                    xnT[:, 512 * k + 128 * t4:512 * k + 128 * t4 + 128],
                    xnbf[:, 128 * k:128 * k + 128], transpose=True))
            xn_mk = s_tp.mark()
        s_tp.wait_all(TE)
        s_vec.wait(TE, wts_ready)
        for lc in range(4):
            pslot = ps[:, 3584:4096]
            if not (I == 0 and lc == 0):
                s_lat.wait(TE)
            for dc in range(8):
                mm = TE.matmul(pslot,
                               wcd_bf[:, 512 * dc + 128 * lc:512 * dc + 128 * lc + 128],
                               xnT[:, 512 * dc:512 * dc + 512],
                               start=(dc == 0), stop=(dc == 7))
            s_mm.inc(mm)
            s_mm.wait(S)
            s_lat.inc(S.activation(latT[:, T * lc + 512 * I:T * lc + 512 * I + 512],
                                   pslot, AF.Copy))
        lat_marks.append(s_mm.v)

    s_vec.wait(S, vext_ones)
    for I in range(ns):
        s_kv.wait(TE)
        s_lat.wait(TE)
        for lc in range(4):
            mm = TE.matmul(ps[:, 3584:4096], wkv_bf[:, 128 * lc:128 * lc + 128],
                           latT[:, T * lc + 512 * I:T * lc + 512 * I + 512],
                           start=(lc == 0), stop=(lc == 3))
        s_mm.inc(mm)
        s_mm.wait(V)
        s_kv.inc(V.tensor_copy(kT64[:, 512 * I:512 * I + 512], ps[0:64, 3584:4096]))
    for j in range(nt):
        s_kv.wait(TE)
        for lc in range(4):
            mm = TE.matmul(ps[:, 3584:3712],
                           latT[:, T * lc + 128 * j:T * lc + 128 * j + 128],
                           wkv_bf[:, 128 * lc:128 * lc + 128],
                           start=(lc == 0), stop=(lc == 3))
        s_mm.inc(mm)
        s_mm.wait(V)
        s_kv.inc(V.tensor_copy(vext[:, 66 * j:66 * j + 64], ps[:, 3648:3712]))
    V.memset(knt[:, :], 0.0)
    s_vec.inc(V.engine_nop())
    for I in range(ns):
        s_mm.wait(S)
        s_kv.wait(S)
        s_sca.inc(S.activation(sqk[:, :], kT64[:, 512 * I:512 * I + 512], AF.Square))
        s_sca.wait(TE)
        s_vec.wait(TE)
        pslot = ps[0:1, 3584:4096]
        s_mm.inc(TE.matmul(pslot, ones64[:, :], sqk[:, :], start=True, stop=True))
        s_mm.wait(V)
        V.tensor_copy(arow[:, :], pslot)
        s_vec.inc(V.engine_nop())
        s_vec.wait(TE)
        s_mm.inc(TE.matmul(ps[:, 3072:3584], ones1b[:, :], arow[:, :],
                           start=True, stop=True))
        s_mm.wait(V)
        V.reduce_max(kn2t[:, :], ps[:, 3072:3584], axis=AX.X)
        vsp()
        s_vec.inc(V.tensor_max(knt[:, :], knt[:, :], kn2t[:, :]))
    knt_ready = s_vec.v

    out_marks = []
    gpar = 0
    psum_free = {0: 0, 1: 0}
    pbf_free = {0: 0, 1: 0}
    for I in range(ns):
        nblk = 4 * I + 4
        ngrp = (nblk + 1) // 2
        s_asm.wait(TE)
        for h in range(H):
            pslot = ps[0:64, 3584:4096]
            if not (I == 0 and h == 0):
                s_qp.wait(TE)
            for lc in range(4):
                mm = TE.matmul(pslot,
                               wq_bf[:, 1024 * lc + 64 * h:1024 * lc + 64 * h + 64],
                               latT[:, T * lc + 512 * I:T * lc + 512 * I + 512],
                               start=(lc == 0), stop=(lc == 3))
            s_mm.inc(mm)
            s_mm.wait(V)
            s_qp.inc(V.tensor_copy(q64[:, 512 * h:512 * h + 512], pslot))
        s_qp.wait(V)
        s_vec.inc(V.tensor_mul(sq64[:, :], q64[:, :], q64[:, :]))
        s_vec.wait(TE)
        for h in range(H):
            mm = TE.matmul(ps[0:1, 3072:3584], ones64[:, :],
                           sq64[:, 512 * h:512 * h + 512],
                           start=(h == 0), stop=(h == H - 1))
        s_mm.inc(mm)
        s_mm.wait(V)
        s_vec.inc(V.tensor_copy(arow[:, :], ps[0:1, 3072:3584]))
        s_vec.wait(TE)
        s_mm.inc(TE.matmul(ps[:, 3584:4096], ones1b[:, :], arow[:, :],
                           start=True, stop=True))
        s_mm.wait(V)
        V.reduce_max(amaxv[:, :], ps[:, 3584:4096], axis=AX.X)
        vsp()
        s_vec.inc(V.tensor_scalar(amaxv[:, :], amaxv[:, :], 1e-30, None, ALU.max))
        s_vec.wait(S)
        S.activation(amaxv[:, :], amaxv[:, :], AF.Ln, scale=knt[:, 0:1])
        s_sca.inc(S.activation(amaxv[:, :], amaxv[:, :], AF.Exp, scale=0.5))
        s_sca.wait(V)
        s_nb.inc(V.tensor_scalar(nbias[:, :], amaxv[:, :], -1.0, 35.0,
                                 ALU.mult, ALU.add))
        nb_ready = s_nb.v
        for h in range(H):
            exp_at = {}
            par = {}
            sc_at = {}

            def emit_pv(gx):
                s_exp.wait(TE, exp_at[gx])
                if gx == 0:
                    s_ye.wait(TE)
                j0v = 2 * gx
                jnv = min(2, nblk - j0v)
                last = None
                for jj in range(jnv):
                    j = j0v + jj
                    for ti in range(4):
                        if j <= 4 * I + ti:
                            last = TE.matmul(
                                ps[:, 2048 + 512 * ti:2048 + 512 * ti + 65],
                                pbf[par[gx]][:, 512 * jj + 128 * ti:
                                             512 * jj + 128 * ti + 128],
                                vext[:, 66 * j:66 * j + 65],
                                start=(j == 0), stop=(j == 4 * I + ti))
                s_pv.inc(last)
                pbf_free[par[gx]] = s_pv.v

            for g in range(ngrp):
                pp = gpar % 2
                gpar += 1
                par[g] = pp
                j0 = 2 * g
                jn = min(2, nblk - j0)
                s_exp.wait(TE, psum_free[pp])
                for jj in range(jn):
                    j = j0 + jj
                    s_sc.inc(TE.matmul(
                        ps[:, 1024 * pp + 512 * jj:1024 * pp + 512 * jj + 512],
                        kT64[:, 128 * j:128 * j + 128],
                        q64[:, 512 * h:512 * h + 512],
                        start=True, stop=True))
                sc_at[g] = s_sc.v
                nmsk = 0
                for jj in range(jn):
                    di = j0 + jj - 4 * I
                    if di >= 0:
                        if nmsk == 0:
                            s_sc.wait(V, sc_at[g])
                            s_bld.wait_mark(V, mask_mk)
                        pvw = ps[:, 1024 * pp + 512 * jj:
                                 1024 * pp + 512 * jj + 128 * (di + 1)]
                        s_msk.inc(V.tensor_tensor(
                            pvw, pvw,
                            mask4[:, 512 * di:512 * di + 128 * (di + 1)], ALU.add))
                        nmsk += 1
                if g >= 1:
                    emit_pv(g - 1)
                s_sc.wait(S, sc_at[g])
                if nmsk:
                    s_msk.wait(S)
                s_pv.wait(S, pbf_free[pp])
                s_nb.wait(S, nb_ready)
                s_exp.inc(S.activation(pbf[pp][:, 0:512 * jn],
                                       ps[:, 1024 * pp:1024 * pp + 512 * jn],
                                       AF.Exp, bias=nbias[:, 0:1]))
                exp_at[g] = s_exp.v
                psum_free[pp] = s_exp.v
            emit_pv(ngrp - 1)
            s_pv.wait(V)
            for ti in range(4):
                V.reciprocal(drb[:, ti:ti + 1],
                             ps[:, 2048 + 512 * ti + 64:2048 + 512 * ti + 65])
            vsp()
            for ti in range(4):
                s_ye.inc(V.tensor_scalar(
                    ybuf[:, D * ti + 64 * h:D * ti + 64 * h + 64],
                    ps[:, 2048 + 512 * ti:2048 + 512 * ti + 64],
                    drb[:, ti:ti + 1], None, ALU.mult))
        s_ye.wait(SY)
        if I > 0:
            s_pr.wait(SY)
        s_tp.begin(SY)
        for ti in range(4):
            for c in range(8):
                s_tp.issue(SY.dma_start(
                    yT[:, 512 * c + 128 * ti:512 * c + 128 * ti + 128],
                    ybuf[:, D * ti + 128 * c:D * ti + 128 * c + 128],
                    transpose=True))
        s_tp.wait_all(TE)
        for ti in range(4):
            i = 4 * I + ti
            b = xb[i % 2]
            fb = fbk[i % 2]
            s_xfr.wait(SY, s_xfr.v - (1 if i > 0 else 0))
            s_xld.begin(SY)
            s_xld.issue(SY.dma_start(b[:, :], x_h[i * 128:(i + 1) * 128, :]))
            asm_xmk = s_xld.mark()
            if i >= 2:
                s_asm.wait(G, s_asm.v - 2)
            s_fb.begin(G)
            s_fb.issue(G.dma_start(fb[:, :], spill_h[i * 128:(i + 1) * 128, :]))
            fb_mk = s_fb.mark()
            for half in range(2):
                pslot = ps[:, 3072 + 512 * half:3072 + 512 * half + 512]
                if not (I == 0 and ti == 0):
                    s_asm.wait(TE)
                for c in range(8):
                    mm = TE.matmul(
                        pslot, yT[:, 512 * c + 128 * ti:512 * c + 128 * ti + 128],
                        wpr_bf[:, 1024 * c + 512 * half:1024 * c + 512 * half + 512],
                        start=(c == 0), stop=(c == 7))
                s_pr.inc(mm)
            o = outst[i % 2]
            s_pr.wait(V)
            s_xld.wait_mark(V, asm_xmk)
            s_fb.wait_mark(V, fb_mk)
            if i >= 2:
                s_out.wait_mark(V, out_marks[i - 2])
            for half in range(2):
                pslot = ps[:, 3072 + 512 * half:3072 + 512 * half + 512]
                sl = slice(512 * half, 512 * half + 512)
                V.tensor_add(o[:, sl], pslot, b[:, sl])
                V.scalar_tensor_tensor(o[:, sl], fb[:, sl], stc(11)[:, i:i + 1],
                                       o[:, sl], ALU.mult, ALU.add)
                s_asm.inc(V.tensor_add(o[:, sl], o[:, sl], b2rep[:, sl]))
            s_xfr.inc(V.engine_nop())
            s_asm.wait(G)
            s_out.begin(G)
            s_out.issue(G.dma_start(out_h[i * 128:(i + 1) * 128, :], o[:, :]))
            out_marks.append(s_out.mark())

    s_out.wait_all(G)
    st.close()
    return nc


def _mask4_np():
    m = np.zeros((128, 4 * 512), dtype=np.float32)
    r = np.arange(128)
    for di in range(4):
        blk = m[:, 512 * di:512 * di + 512]
        blk[:, 0:128 * di] = NEG
        tri = blk[:, 128 * di:128 * di + 128]
        tri[r[:, None] > r[None, :]] = NEG
    return m


def kernel(L0_attn_x, L0_attn_Wc_down, L0_attn_Wq_up, L0_attn_Wkv_up,
           L0_attn_Wc_proj, L0_ffn_w1, L0_ffn_b1, L0_ffn_w2, L0_ffn_b2):
    T = 4096
    nc = build_nc(T)
    x = np.ascontiguousarray(np.asarray(L0_attn_x, dtype=np.float32))
    shared = {
        "wcd": np.ascontiguousarray(np.asarray(L0_attn_Wc_down, dtype=np.float32)),
        "wq": np.ascontiguousarray(np.asarray(L0_attn_Wq_up, dtype=np.float32)),
        "wkv": np.ascontiguousarray(np.asarray(L0_attn_Wkv_up, dtype=np.float32)),
        "wpr": np.ascontiguousarray(np.asarray(L0_attn_Wc_proj, dtype=np.float32)),
        "w1": np.ascontiguousarray(np.asarray(L0_ffn_w1, dtype=np.float32)),
        "b1": np.ascontiguousarray(np.asarray(L0_ffn_b1, dtype=np.float32)).reshape(1, -1),
        "w2": np.ascontiguousarray(np.asarray(L0_ffn_w2, dtype=np.float32)),
        "b2": np.ascontiguousarray(np.asarray(L0_ffn_b2, dtype=np.float32)).reshape(1, -1),
        "msk": _mask4_np(),
        "idn": np.eye(128, dtype=ml_dtypes.bfloat16),
    }
    in_maps = [dict(shared, x=x[c]) for c in range(8)]
    res = run_bass_kernel_spmd(nc, in_maps, core_ids=list(range(8)))
    return np.stack([np.asarray(res.results[c]["out"]) for c in range(8)], axis=0)


# version 10
# speedup vs baseline: 1.3125x; 1.0877x over previous
import numpy as np
import ml_dtypes
from contextlib import ExitStack

import concourse.bass as bass
import concourse.mybir as mybir
from concourse.bass_utils import run_bass_kernel_spmd

F32 = mybir.dt.float32
BF16 = mybir.dt.bfloat16
AF = mybir.ActivationFunctionType
ALU = mybir.AluOpType
AX = mybir.AxisListType

MAGIC = 12582912.0
NEG = -1.0e30
D = 1024
L = 512
FF = 4096
H = 16
DH = 64
EPS = 1e-5


def build_nc(T=4096):
    nc = bass.Bass("TRN2", target_bir_lowering=False, detect_race_conditions=False)
    nt = T // 128
    ns = T // 512

    x_h = nc.dram_tensor("x", [T, D], F32, kind="ExternalInput")
    wcd_h = nc.dram_tensor("wcd", [D, L], F32, kind="ExternalInput")
    wq_h = nc.dram_tensor("wq", [L, D], F32, kind="ExternalInput")
    wkv_h = nc.dram_tensor("wkv", [L, 2 * DH], F32, kind="ExternalInput")
    wpr_h = nc.dram_tensor("wpr", [D, D], F32, kind="ExternalInput")
    w1_h = nc.dram_tensor("w1", [FF, D], F32, kind="ExternalInput")
    b1_h = nc.dram_tensor("b1", [1, FF], F32, kind="ExternalInput")
    w2_h = nc.dram_tensor("w2", [D, FF], F32, kind="ExternalInput")
    b2_h = nc.dram_tensor("b2", [1, D], F32, kind="ExternalInput")
    msk_h = nc.dram_tensor("msk", [128, 4 * 512], F32, kind="ExternalInput")
    idn_h = nc.dram_tensor("idn", [128, 128], BF16, kind="ExternalInput")
    out_h = nc.dram_tensor("out", [T, D], F32, kind="ExternalOutput")
    spill_h = nc.dram_tensor("ffn_spill", [T, D], BF16, kind="ExternalOutput")

    st = ExitStack()
    sb = lambda name, shape, dt: st.enter_context(nc.sbuf_tensor(name, shape, dt))
    sem = lambda name: st.enter_context(nc.semaphore(name))

    xb = [sb("xb0", [128, D], F32), sb("xb1", [128, D], F32)]
    b2rep = sb("b2rep", [128, D], BF16)
    ones1f = sb("ones1f", [1, 128], F32)
    ones1b = sb("ones1b", [1, 128], BF16)
    onesff = sb("onesff", [128, 128], F32)
    ones64 = sb("ones64", [64, 1], BF16)
    idn = sb("idn_sb", [128, 128], BF16)
    stt = sb("stt", [128, 16 * nt], F32)

    def stc(i):
        return stt[:, i * nt:(i + 1) * nt]

    w1s = sb("w1s", [128, 1], F32)
    mw1s = sb("mw1s", [128, 1], F32)
    w2s = sb("w2s", [128, 1], F32)
    mw2s = sb("mw2s", [128, 1], F32)
    wabs = sb("wabs", [128, 40], F32)
    brow = sb("brow", [1, 512], F32)
    tmp1 = sb("tmp1", [128, 1], F32)
    tmp2 = sb("tmp2", [128, 1], F32)
    tmp3 = sb("tmp3", [128, 128], F32)

    ps = st.enter_context(nc.psum_tensor("ps", [128, 4096], F32))

    SY, V, S, TE, G = nc.sync, nc.vector, nc.scalar, nc.tensor, nc.gpsimd

    class Ctr:
        def __init__(self, name):
            self.h = sem(name)
            self.v = 0

        def inc(self, inst, n=1):
            inst.then_inc(self.h, n)
            self.v += n

        def wait(self, eng, val=None):
            val = self.v if val is None else val
            if val > 0:
                eng.wait_ge(self.h, val)

    class DCtr:

        def __init__(self, name):
            self.h = [sem(name + "a"), sem(name + "b")]
            self.v = [0, 0]
            self.b = 0
            self.cur = None

        def __init_serial__(self):
            pass

        def begin(self, eng):
            p = self.b % 2
            self.b += 1
            self.cur = p
            if getattr(self, "serial", False):
                for pp in (0, 1):
                    if self.v[pp] > 0:
                        eng.wait_ge(self.h[pp], self.v[pp])
            elif self.v[p] > 0:
                eng.wait_ge(self.h[p], self.v[p])
            return p

        def issue(self, inst):
            p = self.cur
            inst.then_inc(self.h[p], 16)
            self.v[p] += 16

        def mark(self):
            return (self.cur, self.v[self.cur])

        def wait_mark(self, eng, mk):
            if mk is not None:
                eng.wait_ge(self.h[mk[0]], mk[1])

        def wait_all(self, eng):
            for p in (0, 1):
                if self.v[p] > 0:
                    eng.wait_ge(self.h[p], self.v[p])

    s_xld = DCtr("s_xld")
    s_xfr = Ctr("s_xfr")
    s_xfs = Ctr("s_xfs")
    s_bld = DCtr("s_bld")
    s_mm = Ctr("s_mm")
    s_vec = Ctr("s_vec")
    s_sca = Ctr("s_sca")
    s_wld = DCtr("s_wld")
    s_wfr = Ctr("s_wfr")
    s_tp = DCtr("s_tp")
    s_tpm = Ctr("s_tpm")
    s_he6 = Ctr("s_he6")
    s_he7 = Ctr("s_he7")
    s_ff1 = Ctr("s_ff1")
    s_hev = Ctr("s_hev")
    s_ff2 = Ctr("s_ff2")
    s_f2e = Ctr("s_f2e")
    s_sp = DCtr("s_sp")
    s_q1 = Ctr("s_q1")
    s_q2 = Ctr("s_q2")
    s_g = Ctr("s_g")
    s_lat = Ctr("s_lat")
    s_sc = Ctr("s_sc")
    s_msk = Ctr("s_msk")
    s_exp = Ctr("s_exp")
    s_pv = Ctr("s_pv")
    s_ye = Ctr("s_ye")
    s_qp = Ctr("s_qp")
    s_nb = Ctr("s_nb")
    s_out = DCtr("s_out")
    s_fb = DCtr("s_fb")
    s_pr = Ctr("s_pr")
    s_asm = Ctr("s_asm")
    s_kv = Ctr("s_kv")
    s_wc = DCtr("s_wc")

    def vsp():
        V.tensor_scalar(tmp3[:, :], tmp3[:, :], 1.0, None, ALU.mult)

    s_bld.begin(SY)
    s_bld.issue(SY.dma_start(idn[:, :], idn_h[:, :]))
    idn_mk = s_bld.mark()
    V.memset(tmp3[:, :], 0.0)
    V.memset(tmp1[:, :], 0.0)
    V.memset(ones1f[:, :], 1.0)
    V.memset(ones1b[:, :], 1.0)
    V.memset(onesff[:, :], 1.0)
    s_vec.inc(V.memset(ones64[:, :], 1.0))

    p1st = ExitStack()
    sb1 = lambda name, shape, dt: p1st.enter_context(nc.sbuf_tensor(name, shape, dt))
    t1b = sb1("t1b", [128, D], F32)
    b1rep = sb1("b1rep", [128, FF], BF16)
    w1qT = sb1("w1qT", [128, 8 * FF], BF16)
    w2qT = sb1("w2qT", [128, 32 * D], BF16)
    p0wst = ExitStack()
    big_a = p0wst.enter_context(nc.sbuf_tensor("big_a", [128, FF], F32))
    wq_sb = p0wst.enter_context(nc.sbuf_tensor("wqs0", [128, 1024], BF16))

    for i in range(nt):
        b = xb[i % 2]
        if i >= 2:
            s_xfr.wait(SY, i - 1)
            s_xfs.wait(SY, i - 1)
        s_xld.begin(SY)
        s_xld.issue(SY.dma_start(b[:, :], x_h[i * 128:(i + 1) * 128, :]))
        xmk = s_xld.mark()
        s_xld.wait_mark(V, xmk)
        s_xfr.inc(V.reduce_sum(stc(0)[:, i:i + 1], b[:, :], axis=AX.X))
        V.reduce_max(stc(2)[:, i:i + 1], b[:, :], axis=AX.X)
        V.tensor_reduce(stc(3)[:, i:i + 1], b[:, :], axis=AX.X, op=ALU.min)
        s_xld.wait_mark(S, xmk)
        s_xfs.inc(S.activation(t1b[:, :], b[:, :], AF.Square,
                               accum_out=stc(1)[:, i:i + 1]))

    s_xld.wait_all(V)
    s_xfr.wait(V, nt)
    s_xfs.wait(V, nt)
    inv_d = 1.0 / D
    m = stc(12)
    V.tensor_scalar(m, stc(0), inv_d, None, ALU.mult)
    V.tensor_scalar(stc(13), stc(1), inv_d, None, ALU.mult)
    V.tensor_mul(stc(14), m, m)
    vsp()
    V.tensor_sub(stc(1), stc(13), stc(14))
    vsp()
    s_vec.inc(V.tensor_scalar(stc(15), stc(1), EPS, None, ALU.add))
    s_vec.wait(S)
    s_sca.inc(S.activation(stc(14), stc(15), AF.Ln))
    s_sca.wait(V)
    s_vec.inc(V.tensor_scalar(stc(14), stc(14), -0.5, None, ALU.mult))
    s_vec.wait(S)
    s_sca.inc(S.activation(stc(4), stc(14), AF.Exp))
    s_sca.wait(V)
    V.scalar_tensor_tensor(stc(5), stc(4), -1.0, m, ALU.mult, ALU.mult)
    V.tensor_sub(stc(2), stc(2), m)
    V.tensor_sub(stc(3), m, stc(3))
    vsp()
    V.tensor_max(stc(2), stc(2), stc(3))
    vsp()
    V.tensor_mul(stc(2), stc(2), stc(4))
    V.tensor_mul(stc(14), stc(4), stc(4))
    vsp()
    V.tensor_mul(stc(14), stc(1), stc(14))
    vsp()
    s_vec.inc(V.tensor_scalar(stc(14), stc(14), EPS, None, ALU.add))
    s_vec.wait(S)
    s_sca.inc(S.activation(stc(15), stc(14), AF.Ln))
    s_sca.wait(V)
    s_vec.inc(V.tensor_scalar(stc(15), stc(15), -0.5, None, ALU.mult))
    s_vec.wait(S)
    s_sca.inc(S.activation(stc(3), stc(15), AF.Exp))
    s_sca.wait(V)
    V.tensor_mul(stc(2), stc(2), stc(3))
    vsp()
    s_vec.inc(V.tensor_scalar(stc(2), stc(2), 1e-5, None, ALU.max))
    s_vec.wait(S)
    s_sca.inc(S.activation(stc(15), stc(2), AF.Ln))
    s_sca.wait(V)
    s_vec.inc(V.tensor_scalar(stc(15), stc(15), -1.0, None, ALU.mult))
    s_vec.wait(S)
    s_sca.inc(S.activation(stc(14), stc(15), AF.Exp))
    s_sca.wait(V)
    V.tensor_scalar(stc(14), stc(14), 127.0, None, ALU.mult)
    V.tensor_mul(stc(6), stc(4), stc(3))
    V.tensor_mul(stc(7), stc(5), stc(3))
    V.tensor_mul(stc(6), stc(6), stc(14))
    s_vec.inc(V.tensor_mul(stc(7), stc(7), stc(14)))


    s_vec.wait(TE)
    for c in range(FF // 512 + D // 512):
        if c < FF // 512:
            src_ap = b1_h[0:1, c * 512:(c + 1) * 512]
            dst = b1rep[:, c * 512:(c + 1) * 512]
        else:
            c2 = c - FF // 512
            src_ap = b2_h[0:1, c2 * 512:(c2 + 1) * 512]
            dst = b2rep[:, c2 * 512:(c2 + 1) * 512]
        pslot = ps[:, (c % 2) * 512:(c % 2) * 512 + 512]
        if c >= 1:
            s_mm.wait(SY)
        s_bld.begin(SY)
        s_bld.issue(SY.dma_start(brow[:, :], src_ap))
        s_bld.wait_mark(TE, s_bld.mark())
        if c >= 2:
            s_vec.wait(TE, s_vec.v - 1)
        s_mm.inc(TE.matmul(pslot, ones1f[:, :], brow[:, :], start=True, stop=True))
        s_mm.wait(V)
        s_vec.inc(V.tensor_copy(dst, pslot))


    def w_pass1(w_h, cols, nstr, off):
        for sx in range(nstr):
            b = big_a
            s_wfr.wait(G)
            s_wld.begin(G)
            s_wld.issue(G.dma_start(b[:, 0:cols], w_h[sx * 128:(sx + 1) * 128, :]))
            s_wld.wait_mark(V, s_wld.mark())
            s_wfr.inc(V.reduce_sum(wabs[:, off + sx:off + sx + 1], b[:, 0:cols],
                                   axis=AX.X, apply_absolute_value=True))

    w_pass1(w1_h, D, FF // 128, 0)
    w_pass1(w2_h, FF, D // 128, 32)
    s_wfr.wait(V)
    s_vec.inc(V.engine_nop())
    s_vec.wait(TE)
    s_mm.inc(TE.matmul(ps[:, 2048:2088], onesff[:, :], wabs[:, 0:40],
                       start=True, stop=True))
    s_mm.wait(V)
    V.reduce_sum(w1s[:, :], ps[:, 2048:2080], axis=AX.X)
    V.reduce_sum(w2s[:, :], ps[:, 2080:2088], axis=AX.X)
    for wsx, mwx, nel in ((w1s, mw1s, FF * D), (w2s, mw2s, D * FF)):
        V.tensor_scalar(tmp2[:, :], wsx[:, :], 1.0 / nel, None, ALU.mult)
        vsp()
        s_vec.inc(V.tensor_scalar(tmp2[:, :], tmp2[:, :], 1e-5, None, ALU.max))
        s_vec.wait(S)
        s_sca.inc(S.activation(tmp1[:, :], tmp2[:, :], AF.Copy))
        s_sca.wait(V)
        V.reciprocal(wsx[:, :], tmp1[:, :])
        vsp()
        s_vec.inc(V.tensor_scalar(mwx[:, :], tmp2[:, :], 1.0 / 127.0, None, ALU.mult))
    s_vec.inc(V.engine_nop())
    V.tensor_scalar(stc(8), stc(2), mw1s[:, 0:1], None, ALU.mult)

    def w_pass2(w_h, cols, nstr, s_rec, qT, chunk_cols):
        q = wq_sb
        for sx in range(nstr):
            b = big_a
            s_wfr.wait(G)
            s_wld.begin(G)
            s_wld.issue(G.dma_start(b[:, 0:cols], w_h[sx * 128:(sx + 1) * 128, :]))
            s_wld.wait_mark(V, s_wld.mark())
            V.tensor_scalar(b[:, 0:cols], b[:, 0:cols], s_rec[:, 0:1], MAGIC,
                            ALU.mult, ALU.add)
            V.tensor_scalar(b[:, 0:cols], b[:, 0:cols], MAGIC, 1.0,
                            ALU.subtract, ALU.min)
            for qc in range(cols // 1024):
                s_tp.wait_all(V)
                s_vec.inc(V.tensor_scalar(q[:, :], b[:, 1024 * qc:1024 * qc + 1024],
                                           -1.0, None, ALU.max))
                if qc == cols // 1024 - 1:
                    s_wfr.inc(V.engine_nop())
                s_vec.wait(SY)
                s_tp.begin(SY)
                for k in range(8):
                    kk = 8 * qc + k
                    s_tp.issue(SY.dma_start(
                        qT[:, chunk_cols * kk + 128 * sx:chunk_cols * kk + 128 * sx + 128],
                        q[:, 128 * k:128 * k + 128], transpose=True))

    w_pass2(w1_h, D, FF // 128, w1s, w1qT, FF)
    w_pass2(w2_h, FF, D // 128, w2s, w2qT, D)

    p0wst.close()
    hg = [sb1("hg0", [128, FF], BF16), sb1("hg1", [128, FF], BF16)]
    xq1 = sb1("xq1", [128, D], BF16)
    xq1T = [sb1("xq1T0", [128, D], BF16), sb1("xq1T1", [128, D], BF16)]
    hq = [sb1("hq0", [128, 1024], BF16), sb1("hq1", [128, 1024], BF16)]
    hqT = sb1("hqT", [128, FF], BF16)
    qtmp = sb1("qtmp", [128, 1024], F32)
    spst = [sb1("spst0", [128, D], BF16), sb1("spst1", [128, D], BF16)]
    sp_marks = []
    hev_mark = {}
    ff2_mark = {}
    xq1_mks = {}
    xld_mks = {}
    sq_mark = {}

    def p1_loadX(i):
        b = xb[i % 2]
        if i == 0:
            s_xfr.wait(SY)
            s_xfs.wait(SY)
            p1_loadX.base = s_xfr.v
        elif i >= 2:
            s_xfr.wait(SY, p1_loadX.base + i - 1)
        s_xld.begin(SY)
        s_xld.issue(SY.dma_start(b[:, :], x_h[i * 128:(i + 1) * 128, :]))
        xld_mks[i] = s_xld.mark()

    def p1_prepX(i):
        b = xb[i % 2]
        s_xld.wait_mark(V, xld_mks[i])
        s_xfr.inc(V.tensor_scalar(t1b[:, :], b[:, :], stc(6)[:, i:i + 1],
                                  stc(7)[:, i:i + 1], ALU.mult, ALU.add))
        if i > 0:
            s_tp.wait_mark(V, xq1_mks[i - 1])
        else:
            s_tp.wait_all(V)
        s_q1.inc(V.tensor_scalar(xq1[:, :], t1b[:, :], MAGIC, MAGIC,
                                 ALU.add, ALU.subtract))
        s_q1.wait(SY)
        if i > 1:
            s_ff1.wait(SY, 8 * (i - 1))
        elif i == 0:
            s_tp.wait_all(SY)
        s_tp.begin(SY)
        for k in range(8):
            s_tp.issue(SY.dma_start(xq1T[i % 2][:, 128 * k:128 * k + 128],
                                    xq1[:, 128 * k:128 * k + 128], transpose=True))
        xq1_mks[i] = s_tp.mark()

    def p1_ffn1(i):
        if i == 0:
            s_tp.wait_all(TE)
            s_wfr.wait(V)
        s_tp.wait_mark(TE, xq1_mks[i])
        for fc in range(8):
            pslot = ps[:, (fc % 4) * 512:(fc % 4) * 512 + 512]
            s_hev.wait(TE, s_hev.v - 3)
            for dc in range(8):
                mm = TE.matmul(pslot, xq1T[i % 2][:, 128 * dc:128 * dc + 128],
                               w1qT[:, FF * dc + 512 * fc:FF * dc + 512 * fc + 512],
                               start=(dc == 0), stop=(dc == 7))
            s_ff1.inc(mm)
            s_ff1.wait(V)
            if fc == 0 and i >= 2:
                s_sca.wait(V, sq_mark[i - 2])
            s_hev.inc(V.scalar_tensor_tensor(
                hg[i % 2][:, 512 * fc:512 * fc + 512], pslot, stc(8)[:, i:i + 1],
                b1rep[:, 512 * fc:512 * fc + 512], ALU.mult, ALU.add))
        hev_mark[i] = s_hev.v

    def p1_post_head(i):
        h = hg[i % 2]
        g = hqT
        s_hev.wait(S, hev_mark[i])
        if i > 0:
            s_ff2.wait(S, ff2_mark[i - 1])
        s_g.inc(S.activation(g[:, :], h[:, :], AF.Gelu))
        s_g.wait(V)
        s_vec.inc(V.reduce_max(stc(13)[:, i:i + 1], g[:, :], axis=AX.X,
                               apply_absolute_value=True))
        s_g.wait(S)
        s_sca.inc(S.activation(h[:, :], g[:, :], AF.Square,
                               accum_out=stc(9)[:, i:i + 1]))
        sq_mark[i] = s_sca.v
        s_vec.wait(S)
        s_sca.inc(S.activation(stc(15)[:, i:i + 1], stc(13)[:, i:i + 1], AF.Copy))
        s_sca.wait(V)
        V.reciprocal(stc(14)[:, i:i + 1], stc(15)[:, i:i + 1])
        V.tensor_scalar(stc(10)[:, i:i + 1], stc(13)[:, i:i + 1], mw2s[:, 0:1],
                        None, ALU.mult)
        vsp()
        V.tensor_scalar(stc(14)[:, i:i + 1], stc(14)[:, i:i + 1],
                        127.0, None, ALU.mult)
        vsp()
        s_vec.inc(V.engine_nop())

    def p1_post_ffn2(i):
        g = hqT
        for kc in range(4):
            ck = i * 4 + kc
            pk = ck - 2
            if pk >= 0:
                s_tpm.wait(V, 2 * (pk + 1))
            V.tensor_scalar(qtmp[:, :], g[:, 1024 * kc:1024 * kc + 1024],
                            stc(14)[:, i:i + 1], MAGIC, ALU.mult, ALU.add)
            s_q2.inc(V.tensor_scalar(hq[kc % 2][:, :], qtmp[:, :], MAGIC, None,
                                     ALU.subtract))
            s_q2.wait(TE)
            if ck == 0:
                s_bld.wait_mark(TE, idn_mk)
            if ck >= 1:
                s_he6.wait(TE, ck)
            for k in range(4):
                mm = TE.matmul(ps[:, 3072 + 128 * k:3072 + 128 * k + 128],
                               hq[kc % 2][:, 128 * k:128 * k + 128],
                               idn[:, :], start=True, stop=True)
            s_tpm.inc(mm)
            if ck >= 1:
                s_he7.wait(TE, ck)
            for k in range(4, 8):
                mm = TE.matmul(ps[:, 3584 + 128 * (k - 4):3584 + 128 * (k - 4) + 128],
                               hq[kc % 2][:, 128 * k:128 * k + 128],
                               idn[:, :], start=True, stop=True)
            s_tpm.inc(mm)
            s_tpm.wait(V, 2 * ck + 1)
            s_he6.inc(V.tensor_copy(hqT[:, 1024 * kc:1024 * kc + 512],
                                    ps[:, 3072:3584]))
            s_tpm.wait(S, 2 * ck + 2)
            s_he7.inc(S.activation(hqT[:, 1024 * kc + 512:1024 * kc + 1024],
                                   ps[:, 3584:4096], AF.Copy))
            s_he6.wait(TE, 4 * i + kc + 1)
            s_he7.wait(TE, 4 * i + kc + 1)
            if kc == 0:
                s_f2e.wait(TE)
            for dc2 in range(2):
                pslot = ps[:, 2048 + dc2 * 512:2048 + dc2 * 512 + 512]
                for k8 in range(8):
                    fc = 8 * kc + k8
                    mm2 = TE.matmul(
                        pslot, hqT[:, 128 * fc:128 * fc + 128],
                        w2qT[:, D * fc + 512 * dc2:D * fc + 512 * dc2 + 512],
                        start=(fc == 0), stop=(fc == 31))
                if kc == 3:
                    s_ff2.inc(mm2)
        sp = spst[i % 2]
        s_ff2.wait(V)
        if i >= 2:
            s_sp.wait_mark(V, sp_marks[i - 2])
        for dc2 in range(2):
            pslot = ps[:, 2048 + dc2 * 512:2048 + dc2 * 512 + 512]
            s_f2e.inc(V.tensor_scalar(sp[:, 512 * dc2:512 * dc2 + 512], pslot,
                                      stc(10)[:, i:i + 1], None, ALU.mult))
        ff2_mark[i] = s_ff2.v
        s_f2e.wait(G)
        s_sp.begin(G)
        s_sp.issue(G.dma_start(spill_h[i * 128:(i + 1) * 128, :], sp[:, :]))
        sp_marks.append(s_sp.mark())

    for i in range(nt + 1):
        if i < nt:
            if i == 0:
                p1_loadX(0)
                p1_loadX(1)
                p1_prepX(0)
            if i + 2 < nt:
                p1_loadX(i + 2)
            if i + 1 < nt:
                p1_prepX(i + 1)
            if i >= 1:
                p1_post_head(i - 1)
            p1_ffn1(i)
        if i >= 1:
            if i == nt:
                p1_post_head(i - 1)
            p1_post_ffn2(i - 1)

    p1st.close()
    for eng in (G, SY, V, S, TE):
        s_sp.wait_all(eng)
    V.tensor_scalar(stc(15), stc(9), 1.0 / FF, EPS, ALU.mult, ALU.add)
    s_vec.inc(V.engine_nop())
    s_vec.wait(S)
    S.activation(stc(15), stc(15), AF.Ln)
    s_sca.inc(S.activation(stc(15), stc(15), AF.Exp, scale=0.5))
    s_sca.wait(V)
    s_vec.inc(V.reciprocal(stc(11), stc(15)))

    wcd_bf = sb("wcd_bf", [128, 8 * 512], BF16)
    wq_bf = sb("wq_bf", [128, 4 * 1024], BF16)
    wkv_bf = sb("wkv_bf", [128, 4 * 128], BF16)
    wpr_bf = sb("wpr_bf", [128, 8 * 1024], BF16)
    wtmp = sb("wtmp", [128, 1024], F32)

    def load_cast(src_h, nchunk, ccols, dst, scale=None):
        for c in range(nchunk):
            s_vec.wait(G)
            s_wc.begin(G)
            s_wc.issue(G.dma_start(wtmp[:, 0:ccols], src_h[c * 128:(c + 1) * 128, :]))
            s_wc.wait_mark(V, s_wc.mark())
            if scale is None:
                s_vec.inc(V.tensor_copy(dst[:, c * ccols:(c + 1) * ccols],
                                        wtmp[:, 0:ccols]))
            else:
                s_vec.inc(V.tensor_scalar(dst[:, c * ccols:(c + 1) * ccols],
                                          wtmp[:, 0:ccols], scale, None, ALU.mult))

    load_cast(wcd_h, 8, 512, wcd_bf)
    load_cast(wq_h, 4, 1024, wq_bf, scale=0.125)
    load_cast(wkv_h, 4, 128, wkv_bf)
    load_cast(wpr_h, 8, 1024, wpr_bf)
    wts_ready = s_vec.v
    mask4 = sb("mask4", [128, 2048], F32)
    s_bld.begin(SY)
    s_bld.issue(SY.dma_start(mask4[:, :], msk_h[:, :]))
    mask_mk = s_bld.mark()

    outst = [sb("outst0", [128, D], F32), sb("outst1", [128, D], F32)]
    fbk = [sb("fbk0", [128, D], BF16), sb("fbk1", [128, D], BF16)]
    kT64 = sb("kT64", [64, T], BF16)
    vext = sb("vext", [128, 66 * nt], BF16)
    latT = sb("latT", [128, 4 * T], BF16)
    xnbf = sb("xnbf", [128, D], BF16)
    xnT = sb("xnT", [128, 8 * 512], BF16)
    q64 = sb("q64", [64, 16 * 512], BF16)
    sq64 = sb("sq64", [64, 16 * 512], BF16)
    arow = sb("arow", [1, 512], BF16)
    amaxv = sb("amaxv", [128, 1], F32)
    nbias = sb("nbias", [128, 1], F32)
    knt = sb("knt", [128, 1], F32)
    kn2t = sb("kn2t", [128, 1], F32)
    sqk = sb("sqk", [64, 512], BF16)
    pbf = [sb("pbf0", [128, 1024], BF16), sb("pbf1", [128, 1024], BF16)]
    ybuf = sb("ybuf", [128, 4 * D], BF16)
    yT = sb("yT", [128, 8 * 512], BF16)
    drb = sb("drb", [128, 4], F32)

    s_vec.inc(V.memset(vext[:, :], 1.0))
    vext_ones = s_vec.v

    lat_marks = []
    for I in range(ns):
        for t4 in range(4):
            i = 4 * I + t4
            b = xb[i % 2]
            s_xfr.wait(SY, s_xfr.v - (1 if i > 0 else 0))
            s_xld.begin(SY)
            s_xld.issue(SY.dma_start(b[:, :], x_h[i * 128:(i + 1) * 128, :]))
            s_xld.wait_mark(V, s_xld.mark())
            if i > 0:
                s_tp.wait_mark(V, xn_mk)
            s_xfr.inc(V.tensor_scalar(xnbf[:, :], b[:, :], stc(4)[:, i:i + 1],
                                      stc(5)[:, i:i + 1], ALU.mult, ALU.add))
            s_xfr.wait(SY)
            if I > 0 and t4 == 0:
                s_mm.wait(SY, lat_marks[-1])
            s_tp.begin(SY)
            for k in range(8):
                s_tp.issue(SY.dma_start(
                    xnT[:, 512 * k + 128 * t4:512 * k + 128 * t4 + 128],
                    xnbf[:, 128 * k:128 * k + 128], transpose=True))
            xn_mk = s_tp.mark()
        s_tp.wait_all(TE)
        s_vec.wait(TE, wts_ready)
        for lc in range(4):
            pslot = ps[:, 3584:4096]
            if not (I == 0 and lc == 0):
                s_lat.wait(TE)
            for dc in range(8):
                mm = TE.matmul(pslot,
                               wcd_bf[:, 512 * dc + 128 * lc:512 * dc + 128 * lc + 128],
                               xnT[:, 512 * dc:512 * dc + 512],
                               start=(dc == 0), stop=(dc == 7))
            s_mm.inc(mm)
            s_mm.wait(S)
            s_lat.inc(S.activation(latT[:, T * lc + 512 * I:T * lc + 512 * I + 512],
                                   pslot, AF.Copy))
        lat_marks.append(s_mm.v)

    s_vec.wait(S, vext_ones)
    for I in range(ns):
        s_kv.wait(TE)
        s_lat.wait(TE)
        for lc in range(4):
            mm = TE.matmul(ps[:, 3584:4096], wkv_bf[:, 128 * lc:128 * lc + 128],
                           latT[:, T * lc + 512 * I:T * lc + 512 * I + 512],
                           start=(lc == 0), stop=(lc == 3))
        s_mm.inc(mm)
        s_mm.wait(V)
        s_kv.inc(V.tensor_copy(kT64[:, 512 * I:512 * I + 512], ps[0:64, 3584:4096]))
    for j in range(nt):
        s_kv.wait(TE)
        for lc in range(4):
            mm = TE.matmul(ps[:, 3584:3712],
                           latT[:, T * lc + 128 * j:T * lc + 128 * j + 128],
                           wkv_bf[:, 128 * lc:128 * lc + 128],
                           start=(lc == 0), stop=(lc == 3))
        s_mm.inc(mm)
        s_mm.wait(V)
        s_kv.inc(V.tensor_copy(vext[:, 66 * j:66 * j + 64], ps[:, 3648:3712]))
    V.memset(knt[:, :], 0.0)
    s_vec.inc(V.engine_nop())
    for I in range(ns):
        s_mm.wait(S)
        s_kv.wait(S)
        s_sca.inc(S.activation(sqk[:, :], kT64[:, 512 * I:512 * I + 512], AF.Square))
        s_sca.wait(TE)
        s_vec.wait(TE)
        pslot = ps[0:1, 3584:4096]
        s_mm.inc(TE.matmul(pslot, ones64[:, :], sqk[:, :], start=True, stop=True))
        s_mm.wait(V)
        V.tensor_copy(arow[:, :], pslot)
        s_vec.inc(V.engine_nop())
        s_vec.wait(TE)
        s_mm.inc(TE.matmul(ps[:, 3072:3584], ones1b[:, :], arow[:, :],
                           start=True, stop=True))
        s_mm.wait(V)
        V.reduce_max(kn2t[:, :], ps[:, 3072:3584], axis=AX.X)
        vsp()
        s_vec.inc(V.tensor_max(knt[:, :], knt[:, :], kn2t[:, :]))
    knt_ready = s_vec.v

    out_marks = []
    gpar = 0
    psum_free = {0: 0, 1: 0}
    pbf_free = {0: 0, 1: 0}
    for I in range(ns):
        nblk = 4 * I + 4
        ngrp = (nblk + 1) // 2
        s_asm.wait(TE)
        for h in range(H):
            pslot = ps[0:64, 3584:4096]
            if not (I == 0 and h == 0):
                s_qp.wait(TE)
            for lc in range(4):
                mm = TE.matmul(pslot,
                               wq_bf[:, 1024 * lc + 64 * h:1024 * lc + 64 * h + 64],
                               latT[:, T * lc + 512 * I:T * lc + 512 * I + 512],
                               start=(lc == 0), stop=(lc == 3))
            s_mm.inc(mm)
            s_mm.wait(V)
            s_qp.inc(V.tensor_copy(q64[:, 512 * h:512 * h + 512], pslot))
        s_qp.wait(V)
        s_vec.inc(V.tensor_mul(sq64[:, :], q64[:, :], q64[:, :]))
        s_vec.wait(TE)
        for h in range(H):
            mm = TE.matmul(ps[0:1, 3072:3584], ones64[:, :],
                           sq64[:, 512 * h:512 * h + 512],
                           start=(h == 0), stop=(h == H - 1))
        s_mm.inc(mm)
        s_mm.wait(V)
        s_vec.inc(V.tensor_copy(arow[:, :], ps[0:1, 3072:3584]))
        s_vec.wait(TE)
        s_mm.inc(TE.matmul(ps[:, 3584:4096], ones1b[:, :], arow[:, :],
                           start=True, stop=True))
        s_mm.wait(V)
        V.reduce_max(amaxv[:, :], ps[:, 3584:4096], axis=AX.X)
        vsp()
        s_vec.inc(V.tensor_scalar(amaxv[:, :], amaxv[:, :], 1e-30, None, ALU.max))
        s_vec.wait(S)
        S.activation(amaxv[:, :], amaxv[:, :], AF.Ln, scale=knt[:, 0:1])
        s_sca.inc(S.activation(amaxv[:, :], amaxv[:, :], AF.Exp, scale=0.5))
        s_sca.wait(V)
        s_nb.inc(V.tensor_scalar(nbias[:, :], amaxv[:, :], -1.0, 35.0,
                                 ALU.mult, ALU.add))
        nb_ready = s_nb.v
        for h in range(H):
            exp_at = {}
            par = {}
            sc_at = {}

            def emit_pv(gx):
                s_exp.wait(TE, exp_at[gx])
                if gx == 0:
                    s_ye.wait(TE)
                j0v = 2 * gx
                jnv = min(2, nblk - j0v)
                last = None
                for jj in range(jnv):
                    j = j0v + jj
                    for ti in range(4):
                        if j <= 4 * I + ti:
                            last = TE.matmul(
                                ps[:, 2048 + 512 * ti:2048 + 512 * ti + 65],
                                pbf[par[gx]][:, 512 * jj + 128 * ti:
                                             512 * jj + 128 * ti + 128],
                                vext[:, 66 * j:66 * j + 65],
                                start=(j == 0), stop=(j == 4 * I + ti))
                s_pv.inc(last)
                pbf_free[par[gx]] = s_pv.v

            for g in range(ngrp):
                pp = gpar % 2
                gpar += 1
                par[g] = pp
                j0 = 2 * g
                jn = min(2, nblk - j0)
                s_exp.wait(TE, psum_free[pp])
                for jj in range(jn):
                    j = j0 + jj
                    s_sc.inc(TE.matmul(
                        ps[:, 1024 * pp + 512 * jj:1024 * pp + 512 * jj + 512],
                        kT64[:, 128 * j:128 * j + 128],
                        q64[:, 512 * h:512 * h + 512],
                        start=True, stop=True))
                sc_at[g] = s_sc.v
                nmsk = 0
                for jj in range(jn):
                    di = j0 + jj - 4 * I
                    if di >= 0:
                        if nmsk == 0:
                            s_sc.wait(V, sc_at[g])
                            s_bld.wait_mark(V, mask_mk)
                        pvw = ps[:, 1024 * pp + 512 * jj:
                                 1024 * pp + 512 * jj + 128 * (di + 1)]
                        s_msk.inc(V.tensor_tensor(
                            pvw, pvw,
                            mask4[:, 512 * di:512 * di + 128 * (di + 1)], ALU.add))
                        nmsk += 1
                if g >= 1:
                    emit_pv(g - 1)
                s_sc.wait(S, sc_at[g])
                if nmsk:
                    s_msk.wait(S)
                s_pv.wait(S, pbf_free[pp])
                s_nb.wait(S, nb_ready)
                s_exp.inc(S.activation(pbf[pp][:, 0:512 * jn],
                                       ps[:, 1024 * pp:1024 * pp + 512 * jn],
                                       AF.Exp, bias=nbias[:, 0:1]))
                exp_at[g] = s_exp.v
                psum_free[pp] = s_exp.v
            emit_pv(ngrp - 1)
            s_pv.wait(V)
            for ti in range(4):
                V.reciprocal(drb[:, ti:ti + 1],
                             ps[:, 2048 + 512 * ti + 64:2048 + 512 * ti + 65])
            vsp()
            for ti in range(4):
                s_ye.inc(V.tensor_scalar(
                    ybuf[:, D * ti + 64 * h:D * ti + 64 * h + 64],
                    ps[:, 2048 + 512 * ti:2048 + 512 * ti + 64],
                    drb[:, ti:ti + 1], None, ALU.mult))
        s_ye.wait(SY)
        if I > 0:
            s_pr.wait(SY)
        s_tp.begin(SY)
        for ti in range(4):
            for c in range(8):
                s_tp.issue(SY.dma_start(
                    yT[:, 512 * c + 128 * ti:512 * c + 128 * ti + 128],
                    ybuf[:, D * ti + 128 * c:D * ti + 128 * c + 128],
                    transpose=True))
        s_tp.wait_all(TE)
        for ti in range(4):
            i = 4 * I + ti
            b = xb[i % 2]
            fb = fbk[i % 2]
            s_xfr.wait(SY, s_xfr.v - (1 if i > 0 else 0))
            s_xld.begin(SY)
            s_xld.issue(SY.dma_start(b[:, :], x_h[i * 128:(i + 1) * 128, :]))
            asm_xmk = s_xld.mark()
            if i >= 2:
                s_asm.wait(G, s_asm.v - 2)
            s_fb.begin(G)
            s_fb.issue(G.dma_start(fb[:, :], spill_h[i * 128:(i + 1) * 128, :]))
            fb_mk = s_fb.mark()
            for half in range(2):
                pslot = ps[:, 3072 + 512 * half:3072 + 512 * half + 512]
                if not (I == 0 and ti == 0):
                    s_asm.wait(TE)
                for c in range(8):
                    mm = TE.matmul(
                        pslot, yT[:, 512 * c + 128 * ti:512 * c + 128 * ti + 128],
                        wpr_bf[:, 1024 * c + 512 * half:1024 * c + 512 * half + 512],
                        start=(c == 0), stop=(c == 7))
                s_pr.inc(mm)
            o = outst[i % 2]
            s_pr.wait(V)
            s_xld.wait_mark(V, asm_xmk)
            s_fb.wait_mark(V, fb_mk)
            if i >= 2:
                s_out.wait_mark(V, out_marks[i - 2])
            for half in range(2):
                pslot = ps[:, 3072 + 512 * half:3072 + 512 * half + 512]
                sl = slice(512 * half, 512 * half + 512)
                V.tensor_add(o[:, sl], pslot, b[:, sl])
                V.scalar_tensor_tensor(o[:, sl], fb[:, sl], stc(11)[:, i:i + 1],
                                       o[:, sl], ALU.mult, ALU.add)
                s_asm.inc(V.tensor_add(o[:, sl], o[:, sl], b2rep[:, sl]))
            s_xfr.inc(V.engine_nop())
            s_asm.wait(G)
            s_out.begin(G)
            s_out.issue(G.dma_start(out_h[i * 128:(i + 1) * 128, :], o[:, :]))
            out_marks.append(s_out.mark())

    s_out.wait_all(G)
    st.close()
    return nc


def _mask4_np():
    m = np.zeros((128, 4 * 512), dtype=np.float32)
    r = np.arange(128)
    for di in range(4):
        blk = m[:, 512 * di:512 * di + 512]
        blk[:, 0:128 * di] = NEG
        tri = blk[:, 128 * di:128 * di + 128]
        tri[r[:, None] > r[None, :]] = NEG
    return m


def kernel(L0_attn_x, L0_attn_Wc_down, L0_attn_Wq_up, L0_attn_Wkv_up,
           L0_attn_Wc_proj, L0_ffn_w1, L0_ffn_b1, L0_ffn_w2, L0_ffn_b2):
    T = 4096
    nc = build_nc(T)
    x = np.ascontiguousarray(np.asarray(L0_attn_x, dtype=np.float32))
    shared = {
        "wcd": np.ascontiguousarray(np.asarray(L0_attn_Wc_down, dtype=np.float32)),
        "wq": np.ascontiguousarray(np.asarray(L0_attn_Wq_up, dtype=np.float32)),
        "wkv": np.ascontiguousarray(np.asarray(L0_attn_Wkv_up, dtype=np.float32)),
        "wpr": np.ascontiguousarray(np.asarray(L0_attn_Wc_proj, dtype=np.float32)),
        "w1": np.ascontiguousarray(np.asarray(L0_ffn_w1, dtype=np.float32)),
        "b1": np.ascontiguousarray(np.asarray(L0_ffn_b1, dtype=np.float32)).reshape(1, -1),
        "w2": np.ascontiguousarray(np.asarray(L0_ffn_w2, dtype=np.float32)),
        "b2": np.ascontiguousarray(np.asarray(L0_ffn_b2, dtype=np.float32)).reshape(1, -1),
        "msk": _mask4_np(),
        "idn": np.eye(128, dtype=ml_dtypes.bfloat16),
    }
    in_maps = [dict(shared, x=x[c]) for c in range(8)]
    res = run_bass_kernel_spmd(nc, in_maps, core_ids=list(range(8)))
    return np.stack([np.asarray(res.results[c]["out"]) for c in range(8)], axis=0)


# version 11
# speedup vs baseline: 1.4906x; 1.1357x over previous
import numpy as np
import ml_dtypes
from contextlib import ExitStack

import concourse.bass as bass
import concourse.mybir as mybir
from concourse.bass_utils import run_bass_kernel_spmd

F32 = mybir.dt.float32
BF16 = mybir.dt.bfloat16
AF = mybir.ActivationFunctionType
ALU = mybir.AluOpType
AX = mybir.AxisListType

MAGIC = 12582912.0
NEG = -1.0e30
D = 1024
L = 512
FF = 4096
H = 16
DH = 64
EPS = 1e-5


def build_nc(T=4096):
    nc = bass.Bass("TRN2", target_bir_lowering=False, detect_race_conditions=False)
    nt = T // 128
    ns = T // 512

    x_h = nc.dram_tensor("x", [T, D], F32, kind="ExternalInput")
    wcd_h = nc.dram_tensor("wcd", [D, L], F32, kind="ExternalInput")
    wq_h = nc.dram_tensor("wq", [L, D], F32, kind="ExternalInput")
    wkv_h = nc.dram_tensor("wkv", [L, 2 * DH], F32, kind="ExternalInput")
    wpr_h = nc.dram_tensor("wpr", [D, D], F32, kind="ExternalInput")
    w1_h = nc.dram_tensor("w1", [FF, D], F32, kind="ExternalInput")
    b1_h = nc.dram_tensor("b1", [1, FF], F32, kind="ExternalInput")
    w2_h = nc.dram_tensor("w2", [D, FF], F32, kind="ExternalInput")
    b2_h = nc.dram_tensor("b2", [1, D], F32, kind="ExternalInput")
    msk_h = nc.dram_tensor("msk", [128, 4 * 512], F32, kind="ExternalInput")
    idn_h = nc.dram_tensor("idn", [128, 128], BF16, kind="ExternalInput")
    out_h = nc.dram_tensor("out", [T, D], F32, kind="ExternalOutput")
    spill_h = nc.dram_tensor("ffn_spill", [T, D], BF16, kind="ExternalOutput")

    st = ExitStack()
    sb = lambda name, shape, dt: st.enter_context(nc.sbuf_tensor(name, shape, dt))
    sem = lambda name: st.enter_context(nc.semaphore(name))

    xb = [sb("xb0", [128, D], F32), sb("xb1", [128, D], F32)]
    b2rep = sb("b2rep", [128, D], BF16)
    ones1f = sb("ones1f", [1, 128], F32)
    ones1b = sb("ones1b", [1, 128], BF16)
    onesff = sb("onesff", [128, 128], F32)
    ones64 = sb("ones64", [64, 1], BF16)
    idn = sb("idn_sb", [128, 128], BF16)
    stt = sb("stt", [128, 16 * nt], F32)

    def stc(i):
        return stt[:, i * nt:(i + 1) * nt]

    w1s = sb("w1s", [128, 1], F32)
    mw1s = sb("mw1s", [128, 1], F32)
    w2s = sb("w2s", [128, 1], F32)
    mw2s = sb("mw2s", [128, 1], F32)
    wabs = sb("wabs", [128, 40], F32)
    brow = sb("brow", [1, 512], F32)
    tmp1 = sb("tmp1", [128, 1], F32)
    tmp2 = sb("tmp2", [128, 1], F32)
    tmp3 = sb("tmp3", [128, 128], F32)

    ps = st.enter_context(nc.psum_tensor("ps", [128, 4096], F32))

    SY, V, S, TE, G = nc.sync, nc.vector, nc.scalar, nc.tensor, nc.gpsimd

    class Ctr:
        def __init__(self, name):
            self.h = sem(name)
            self.v = 0

        def inc(self, inst, n=1):
            inst.then_inc(self.h, n)
            self.v += n

        def wait(self, eng, val=None):
            val = self.v if val is None else val
            if val > 0:
                eng.wait_ge(self.h, val)

    class DCtr:

        def __init__(self, name):
            self.h = [sem(name + "a"), sem(name + "b")]
            self.v = [0, 0]
            self.b = 0
            self.cur = None

        def __init_serial__(self):
            pass

        def begin(self, eng):
            p = self.b % 2
            self.b += 1
            self.cur = p
            if getattr(self, "serial", False):
                for pp in (0, 1):
                    if self.v[pp] > 0:
                        eng.wait_ge(self.h[pp], self.v[pp])
            elif self.v[p] > 0:
                eng.wait_ge(self.h[p], self.v[p])
            return p

        def issue(self, inst):
            p = self.cur
            inst.then_inc(self.h[p], 16)
            self.v[p] += 16

        def mark(self):
            return (self.cur, self.v[self.cur])

        def wait_mark(self, eng, mk):
            if mk is not None:
                eng.wait_ge(self.h[mk[0]], mk[1])

        def wait_all(self, eng):
            for p in (0, 1):
                if self.v[p] > 0:
                    eng.wait_ge(self.h[p], self.v[p])

    s_xld = DCtr("s_xld")
    s_xfr = Ctr("s_xfr")
    s_xfs = Ctr("s_xfs")
    s_bld = DCtr("s_bld")
    s_mm = Ctr("s_mm")
    s_vec = Ctr("s_vec")
    s_sca = Ctr("s_sca")
    s_wld = DCtr("s_wld")
    s_wfr = Ctr("s_wfr")
    s_tp = DCtr("s_tp")
    s_tpm = Ctr("s_tpm")
    s_he6 = Ctr("s_he6")
    s_he7 = Ctr("s_he7")
    s_wtp = Ctr("s_wtp")
    s_we6 = Ctr("s_we6")
    s_we7 = Ctr("s_we7")
    s_xtp = Ctr("s_xtp")
    s_xe4 = Ctr("s_xe4")
    s_xe5 = Ctr("s_xe5")
    s_ff1 = Ctr("s_ff1")
    s_hev = Ctr("s_hev")
    s_ff2 = Ctr("s_ff2")
    s_f2e = Ctr("s_f2e")
    s_sp = DCtr("s_sp")
    s_q1 = Ctr("s_q1")
    s_q2 = Ctr("s_q2")
    s_g = Ctr("s_g")
    s_lat = Ctr("s_lat")
    s_sc = Ctr("s_sc")
    s_msk = Ctr("s_msk")
    s_exp = Ctr("s_exp")
    s_pv = Ctr("s_pv")
    s_ye = Ctr("s_ye")
    s_qp = Ctr("s_qp")
    s_nb = Ctr("s_nb")
    s_out = DCtr("s_out")
    s_fb = DCtr("s_fb")
    s_pr = Ctr("s_pr")
    s_asm = Ctr("s_asm")
    s_kv = Ctr("s_kv")
    s_wc = DCtr("s_wc")

    def vsp():
        V.tensor_scalar(tmp3[:, :], tmp3[:, :], 1.0, None, ALU.mult)

    s_bld.begin(SY)
    s_bld.issue(SY.dma_start(idn[:, :], idn_h[:, :]))
    idn_mk = s_bld.mark()
    V.memset(tmp3[:, :], 0.0)
    V.memset(tmp1[:, :], 0.0)
    V.memset(ones1f[:, :], 1.0)
    V.memset(ones1b[:, :], 1.0)
    V.memset(onesff[:, :], 1.0)
    s_vec.inc(V.memset(ones64[:, :], 1.0))

    p1st = ExitStack()
    sb1 = lambda name, shape, dt: p1st.enter_context(nc.sbuf_tensor(name, shape, dt))
    t1b = sb1("t1b", [128, D], F32)
    b1rep = sb1("b1rep", [128, FF], BF16)
    w1qT = sb1("w1qT", [128, 8 * FF], BF16)
    w2qT = sb1("w2qT", [128, 32 * D], BF16)
    p0wst = ExitStack()
    big_a = p0wst.enter_context(nc.sbuf_tensor("big_a", [128, FF], F32))
    wq_sb = p0wst.enter_context(nc.sbuf_tensor("wqs0", [128, 1024], BF16))

    for i in range(nt):
        b = xb[i % 2]
        if i >= 2:
            s_xfr.wait(SY, i - 1)
            s_xfs.wait(SY, i - 1)
        s_xld.begin(SY)
        s_xld.issue(SY.dma_start(b[:, :], x_h[i * 128:(i + 1) * 128, :]))
        xmk = s_xld.mark()
        s_xld.wait_mark(V, xmk)
        s_xfr.inc(V.reduce_sum(stc(0)[:, i:i + 1], b[:, :], axis=AX.X))
        V.reduce_max(stc(2)[:, i:i + 1], b[:, :], axis=AX.X)
        V.tensor_reduce(stc(3)[:, i:i + 1], b[:, :], axis=AX.X, op=ALU.min)
        s_xld.wait_mark(S, xmk)
        s_xfs.inc(S.activation(t1b[:, :], b[:, :], AF.Square,
                               accum_out=stc(1)[:, i:i + 1]))

    s_xld.wait_all(V)
    s_xfr.wait(V, nt)
    s_xfs.wait(V, nt)
    inv_d = 1.0 / D
    m = stc(12)
    V.tensor_scalar(m, stc(0), inv_d, None, ALU.mult)
    V.tensor_scalar(stc(13), stc(1), inv_d, None, ALU.mult)
    V.tensor_mul(stc(14), m, m)
    vsp()
    V.tensor_sub(stc(1), stc(13), stc(14))
    vsp()
    s_vec.inc(V.tensor_scalar(stc(15), stc(1), EPS, None, ALU.add))
    s_vec.wait(S)
    s_sca.inc(S.activation(stc(14), stc(15), AF.Ln))
    s_sca.wait(V)
    s_vec.inc(V.tensor_scalar(stc(14), stc(14), -0.5, None, ALU.mult))
    s_vec.wait(S)
    s_sca.inc(S.activation(stc(4), stc(14), AF.Exp))
    s_sca.wait(V)
    V.scalar_tensor_tensor(stc(5), stc(4), -1.0, m, ALU.mult, ALU.mult)
    V.tensor_sub(stc(2), stc(2), m)
    V.tensor_sub(stc(3), m, stc(3))
    vsp()
    V.tensor_max(stc(2), stc(2), stc(3))
    vsp()
    V.tensor_mul(stc(2), stc(2), stc(4))
    V.tensor_mul(stc(14), stc(4), stc(4))
    vsp()
    V.tensor_mul(stc(14), stc(1), stc(14))
    vsp()
    s_vec.inc(V.tensor_scalar(stc(14), stc(14), EPS, None, ALU.add))
    s_vec.wait(S)
    s_sca.inc(S.activation(stc(15), stc(14), AF.Ln))
    s_sca.wait(V)
    s_vec.inc(V.tensor_scalar(stc(15), stc(15), -0.5, None, ALU.mult))
    s_vec.wait(S)
    s_sca.inc(S.activation(stc(3), stc(15), AF.Exp))
    s_sca.wait(V)
    V.tensor_mul(stc(2), stc(2), stc(3))
    vsp()
    s_vec.inc(V.tensor_scalar(stc(2), stc(2), 1e-5, None, ALU.max))
    s_vec.wait(S)
    s_sca.inc(S.activation(stc(15), stc(2), AF.Ln))
    s_sca.wait(V)
    s_vec.inc(V.tensor_scalar(stc(15), stc(15), -1.0, None, ALU.mult))
    s_vec.wait(S)
    s_sca.inc(S.activation(stc(14), stc(15), AF.Exp))
    s_sca.wait(V)
    V.tensor_scalar(stc(14), stc(14), 127.0, None, ALU.mult)
    V.tensor_mul(stc(6), stc(4), stc(3))
    V.tensor_mul(stc(7), stc(5), stc(3))
    V.tensor_mul(stc(6), stc(6), stc(14))
    s_vec.inc(V.tensor_mul(stc(7), stc(7), stc(14)))


    s_vec.wait(TE)
    for c in range(FF // 512 + D // 512):
        if c < FF // 512:
            src_ap = b1_h[0:1, c * 512:(c + 1) * 512]
            dst = b1rep[:, c * 512:(c + 1) * 512]
        else:
            c2 = c - FF // 512
            src_ap = b2_h[0:1, c2 * 512:(c2 + 1) * 512]
            dst = b2rep[:, c2 * 512:(c2 + 1) * 512]
        pslot = ps[:, (c % 2) * 512:(c % 2) * 512 + 512]
        if c >= 1:
            s_mm.wait(SY)
        s_bld.begin(SY)
        s_bld.issue(SY.dma_start(brow[:, :], src_ap))
        s_bld.wait_mark(TE, s_bld.mark())
        if c >= 2:
            s_vec.wait(TE, s_vec.v - 1)
        s_mm.inc(TE.matmul(pslot, ones1f[:, :], brow[:, :], start=True, stop=True))
        s_mm.wait(V)
        s_vec.inc(V.tensor_copy(dst, pslot))


    def w_pass1(w_h, cols, nstr, off):
        for sx in range(nstr):
            b = big_a
            s_wfr.wait(G)
            s_wld.begin(G)
            s_wld.issue(G.dma_start(b[:, 0:cols], w_h[sx * 128:(sx + 1) * 128, :]))
            s_wld.wait_mark(V, s_wld.mark())
            s_wfr.inc(V.reduce_sum(wabs[:, off + sx:off + sx + 1], b[:, 0:cols],
                                   axis=AX.X, apply_absolute_value=True))

    w_pass1(w1_h, D, FF // 128, 0)
    w_pass1(w2_h, FF, D // 128, 32)
    s_wfr.wait(V)
    s_vec.inc(V.engine_nop())
    s_vec.wait(TE)
    s_mm.inc(TE.matmul(ps[:, 2048:2088], onesff[:, :], wabs[:, 0:40],
                       start=True, stop=True))
    s_mm.wait(V)
    V.reduce_sum(w1s[:, :], ps[:, 2048:2080], axis=AX.X)
    V.reduce_sum(w2s[:, :], ps[:, 2080:2088], axis=AX.X)
    for wsx, mwx, nel in ((w1s, mw1s, FF * D), (w2s, mw2s, D * FF)):
        V.tensor_scalar(tmp2[:, :], wsx[:, :], 1.0 / nel, None, ALU.mult)
        vsp()
        s_vec.inc(V.tensor_scalar(tmp2[:, :], tmp2[:, :], 1e-5, None, ALU.max))
        s_vec.wait(S)
        s_sca.inc(S.activation(tmp1[:, :], tmp2[:, :], AF.Copy))
        s_sca.wait(V)
        V.reciprocal(wsx[:, :], tmp1[:, :])
        vsp()
        s_vec.inc(V.tensor_scalar(mwx[:, :], tmp2[:, :], 1.0 / 127.0, None, ALU.mult))
    s_vec.inc(V.engine_nop())
    V.tensor_scalar(stc(8), stc(2), mw1s[:, 0:1], None, ALU.mult)

    wqc = {"n": 0}

    def w_pass2(w_h, cols, nstr, s_rec, qT, chunk_cols):
        q = wq_sb
        for sx in range(nstr):
            b = big_a
            s_wfr.wait(G)
            s_wld.begin(G)
            s_wld.issue(G.dma_start(b[:, 0:cols], w_h[sx * 128:(sx + 1) * 128, :]))
            s_wld.wait_mark(V, s_wld.mark())
            V.tensor_scalar(b[:, 0:cols], b[:, 0:cols], s_rec[:, 0:1], MAGIC,
                            ALU.mult, ALU.add)
            V.tensor_scalar(b[:, 0:cols], b[:, 0:cols], MAGIC, 1.0,
                            ALU.subtract, ALU.min)
            for qc in range(cols // 1024):
                n = wqc["n"]
                s_wtp.wait(V)
                s_vec.inc(V.tensor_scalar(q[:, :], b[:, 1024 * qc:1024 * qc + 1024],
                                           -1.0, None, ALU.max))
                if qc == cols // 1024 - 1:
                    s_wfr.inc(V.engine_nop())
                s_vec.wait(TE)
                if n == 0:
                    s_bld.wait_mark(TE, idn_mk)
                if n >= 1:
                    s_we6.wait(TE, n)
                for k in range(4):
                    mm = TE.matmul(ps[:, 3072 + 128 * k:3072 + 128 * k + 128],
                                   q[:, 128 * k:128 * k + 128], idn[:, :],
                                   start=True, stop=True)
                s_wtp.inc(mm)
                if n >= 1:
                    s_we7.wait(TE, n)
                for k in range(4, 8):
                    mm = TE.matmul(ps[:, 3584 + 128 * (k - 4):3584 + 128 * (k - 4) + 128],
                                   q[:, 128 * k:128 * k + 128], idn[:, :],
                                   start=True, stop=True)
                s_wtp.inc(mm)
                s_wtp.wait(S, 2 * n + 1)
                for k in range(4):
                    kk = 8 * qc + k
                    act = S.activation(
                        qT[:, chunk_cols * kk + 128 * sx:chunk_cols * kk + 128 * sx + 128],
                        ps[:, 3072 + 128 * k:3072 + 128 * k + 128], AF.Copy)
                s_we6.inc(act)
                s_wtp.wait(S, 2 * n + 2)
                for k in range(4, 8):
                    kk = 8 * qc + k
                    act = S.activation(
                        qT[:, chunk_cols * kk + 128 * sx:chunk_cols * kk + 128 * sx + 128],
                        ps[:, 3584 + 128 * (k - 4):3584 + 128 * (k - 4) + 128], AF.Copy)
                s_we7.inc(act)
                wqc["n"] = n + 1

    w_pass2(w1_h, D, FF // 128, w1s, w1qT, FF)
    w_pass2(w2_h, FF, D // 128, w2s, w2qT, D)

    p0wst.close()
    hg = [sb1("hg0", [128, FF], BF16), sb1("hg1", [128, FF], BF16)]
    xq1 = sb1("xq1", [128, D], BF16)
    xq1T = [sb1("xq1T0", [128, D], BF16), sb1("xq1T1", [128, D], BF16)]
    hq = [sb1("hq0", [128, 1024], BF16), sb1("hq1", [128, 1024], BF16)]
    hqT = sb1("hqT", [128, FF], BF16)
    qtmp = sb1("qtmp", [128, 1024], F32)
    spst = [sb1("spst0", [128, D], BF16), sb1("spst1", [128, D], BF16)]
    sp_marks = []
    hev_mark = {}
    ff2_mark = {}
    xq1_mks = {}
    xld_mks = {}
    sq_mark = {}

    def p1_loadX(i):
        b = xb[i % 2]
        if i == 0:
            s_xfr.wait(SY)
            s_xfs.wait(SY)
            p1_loadX.base = s_xfr.v
        elif i >= 2:
            s_xfr.wait(SY, p1_loadX.base + i - 1)
        s_xld.begin(SY)
        s_xld.issue(SY.dma_start(b[:, :], x_h[i * 128:(i + 1) * 128, :]))
        xld_mks[i] = s_xld.mark()

    def p1_prepX(i):
        b = xb[i % 2]
        s_xld.wait_mark(V, xld_mks[i])
        s_xfr.inc(V.tensor_scalar(t1b[:, :], b[:, :], stc(6)[:, i:i + 1],
                                  stc(7)[:, i:i + 1], ALU.mult, ALU.add))
        if i > 0:
            s_tp.wait_mark(V, xq1_mks[i - 1])
        else:
            s_wtp.wait(V)
        s_q1.inc(V.tensor_scalar(xq1[:, :], t1b[:, :], MAGIC, MAGIC,
                                 ALU.add, ALU.subtract))
        s_q1.wait(SY)
        if i > 1:
            s_ff1.wait(SY, 8 * (i - 1))
        elif i == 0:
            s_wtp.wait(SY)
            s_wfr.wait(SY)
        s_tp.begin(SY)
        for k in range(8):
            s_tp.issue(SY.dma_start(xq1T[i % 2][:, 128 * k:128 * k + 128],
                                    xq1[:, 128 * k:128 * k + 128], transpose=True))
        xq1_mks[i] = s_tp.mark()

    def p1_ffn1(i):
        if i == 0:
            s_we6.wait(TE)
            s_we7.wait(TE)
            s_wfr.wait(V)
        s_tp.wait_mark(TE, xq1_mks[i])
        for fc in range(8):
            pslot = ps[:, (fc % 4) * 512:(fc % 4) * 512 + 512]
            s_hev.wait(TE, s_hev.v - 3)
            for dc in range(8):
                mm = TE.matmul(pslot, xq1T[i % 2][:, 128 * dc:128 * dc + 128],
                               w1qT[:, FF * dc + 512 * fc:FF * dc + 512 * fc + 512],
                               start=(dc == 0), stop=(dc == 7))
            s_ff1.inc(mm)
            s_ff1.wait(V)
            if fc == 0 and i >= 2:
                s_sca.wait(V, sq_mark[i - 2])
            s_hev.inc(V.scalar_tensor_tensor(
                hg[i % 2][:, 512 * fc:512 * fc + 512], pslot, stc(8)[:, i:i + 1],
                b1rep[:, 512 * fc:512 * fc + 512], ALU.mult, ALU.add))
        hev_mark[i] = s_hev.v

    def p1_post_head(i):
        h = hg[i % 2]
        g = hqT
        s_hev.wait(S, hev_mark[i])
        if i > 0:
            s_ff2.wait(S, ff2_mark[i - 1])
        s_g.inc(S.activation(g[:, :], h[:, :], AF.Gelu))
        s_g.wait(V)
        s_vec.inc(V.reduce_max(stc(13)[:, i:i + 1], g[:, :], axis=AX.X,
                               apply_absolute_value=True))
        s_g.wait(S)
        s_sca.inc(S.activation(h[:, :], g[:, :], AF.Square,
                               accum_out=stc(9)[:, i:i + 1]))
        sq_mark[i] = s_sca.v
        s_vec.wait(S)
        s_sca.inc(S.activation(stc(15)[:, i:i + 1], stc(13)[:, i:i + 1], AF.Copy))
        s_sca.wait(V)
        V.reciprocal(stc(14)[:, i:i + 1], stc(15)[:, i:i + 1])
        V.tensor_scalar(stc(10)[:, i:i + 1], stc(13)[:, i:i + 1], mw2s[:, 0:1],
                        None, ALU.mult)
        vsp()
        V.tensor_scalar(stc(14)[:, i:i + 1], stc(14)[:, i:i + 1],
                        127.0, None, ALU.mult)
        vsp()
        s_vec.inc(V.engine_nop())

    def p1_post_ffn2(i):
        g = hqT
        for kc in range(4):
            ck = i * 4 + kc
            pk = ck - 2
            if pk >= 0:
                s_tpm.wait(V, 2 * (pk + 1))
            V.tensor_scalar(qtmp[:, :], g[:, 1024 * kc:1024 * kc + 1024],
                            stc(14)[:, i:i + 1], MAGIC, ALU.mult, ALU.add)
            s_q2.inc(V.tensor_scalar(hq[kc % 2][:, :], qtmp[:, :], MAGIC, None,
                                     ALU.subtract))
            s_q2.wait(TE)
            if ck == 0:
                s_bld.wait_mark(TE, idn_mk)
            if ck >= 1:
                s_he6.wait(TE, ck)
            for k in range(4):
                mm = TE.matmul(ps[:, 3072 + 128 * k:3072 + 128 * k + 128],
                               hq[kc % 2][:, 128 * k:128 * k + 128],
                               idn[:, :], start=True, stop=True)
            s_tpm.inc(mm)
            if ck >= 1:
                s_he7.wait(TE, ck)
            for k in range(4, 8):
                mm = TE.matmul(ps[:, 3584 + 128 * (k - 4):3584 + 128 * (k - 4) + 128],
                               hq[kc % 2][:, 128 * k:128 * k + 128],
                               idn[:, :], start=True, stop=True)
            s_tpm.inc(mm)
            s_tpm.wait(V, 2 * ck + 1)
            s_he6.inc(V.tensor_copy(hqT[:, 1024 * kc:1024 * kc + 512],
                                    ps[:, 3072:3584]))
            s_tpm.wait(S, 2 * ck + 2)
            s_he7.inc(S.activation(hqT[:, 1024 * kc + 512:1024 * kc + 1024],
                                   ps[:, 3584:4096], AF.Copy))
            s_he6.wait(TE, 4 * i + kc + 1)
            s_he7.wait(TE, 4 * i + kc + 1)
            if kc == 0:
                s_f2e.wait(TE)
            for dc2 in range(2):
                pslot = ps[:, 2048 + dc2 * 512:2048 + dc2 * 512 + 512]
                for k8 in range(8):
                    fc = 8 * kc + k8
                    mm2 = TE.matmul(
                        pslot, hqT[:, 128 * fc:128 * fc + 128],
                        w2qT[:, D * fc + 512 * dc2:D * fc + 512 * dc2 + 512],
                        start=(fc == 0), stop=(fc == 31))
                if kc == 3:
                    s_ff2.inc(mm2)
        sp = spst[i % 2]
        s_ff2.wait(V)
        if i >= 2:
            s_sp.wait_mark(V, sp_marks[i - 2])
        for dc2 in range(2):
            pslot = ps[:, 2048 + dc2 * 512:2048 + dc2 * 512 + 512]
            s_f2e.inc(V.tensor_scalar(sp[:, 512 * dc2:512 * dc2 + 512], pslot,
                                      stc(10)[:, i:i + 1], None, ALU.mult))
        ff2_mark[i] = s_ff2.v
        s_f2e.wait(G)
        s_sp.begin(G)
        s_sp.issue(G.dma_start(spill_h[i * 128:(i + 1) * 128, :], sp[:, :]))
        sp_marks.append(s_sp.mark())

    for i in range(nt + 1):
        if i < nt:
            if i == 0:
                p1_loadX(0)
                p1_loadX(1)
                p1_prepX(0)
            if i + 2 < nt:
                p1_loadX(i + 2)
            if i + 1 < nt:
                p1_prepX(i + 1)
            if i >= 1:
                p1_post_head(i - 1)
            p1_ffn1(i)
        if i >= 1:
            if i == nt:
                p1_post_head(i - 1)
            p1_post_ffn2(i - 1)

    p1st.close()
    for eng in (G, SY, V, S, TE):
        s_sp.wait_all(eng)
    V.tensor_scalar(stc(15), stc(9), 1.0 / FF, EPS, ALU.mult, ALU.add)
    s_vec.inc(V.engine_nop())
    s_vec.wait(S)
    S.activation(stc(15), stc(15), AF.Ln)
    s_sca.inc(S.activation(stc(15), stc(15), AF.Exp, scale=0.5))
    s_sca.wait(V)
    s_vec.inc(V.reciprocal(stc(11), stc(15)))

    wcd_bf = sb("wcd_bf", [128, 8 * 512], BF16)
    wq_bf = sb("wq_bf", [128, 4 * 1024], BF16)
    wkv_bf = sb("wkv_bf", [128, 4 * 128], BF16)
    wpr_bf = sb("wpr_bf", [128, 8 * 1024], BF16)
    wtmp = sb("wtmp", [128, 1024], F32)

    def load_cast(src_h, nchunk, ccols, dst, scale=None):
        for c in range(nchunk):
            s_vec.wait(G)
            s_wc.begin(G)
            s_wc.issue(G.dma_start(wtmp[:, 0:ccols], src_h[c * 128:(c + 1) * 128, :]))
            s_wc.wait_mark(V, s_wc.mark())
            if scale is None:
                s_vec.inc(V.tensor_copy(dst[:, c * ccols:(c + 1) * ccols],
                                        wtmp[:, 0:ccols]))
            else:
                s_vec.inc(V.tensor_scalar(dst[:, c * ccols:(c + 1) * ccols],
                                          wtmp[:, 0:ccols], scale, None, ALU.mult))

    load_cast(wcd_h, 8, 512, wcd_bf)
    load_cast(wq_h, 4, 1024, wq_bf, scale=0.125)
    load_cast(wkv_h, 4, 128, wkv_bf)
    load_cast(wpr_h, 8, 1024, wpr_bf)
    wts_ready = s_vec.v
    mask4 = sb("mask4", [128, 2048], F32)
    s_bld.begin(SY)
    s_bld.issue(SY.dma_start(mask4[:, :], msk_h[:, :]))
    mask_mk = s_bld.mark()

    outst = [sb("outst0", [128, D], F32), sb("outst1", [128, D], F32)]
    fbk = [sb("fbk0", [128, D], BF16), sb("fbk1", [128, D], BF16)]
    kT64 = sb("kT64", [64, T], BF16)
    vext = sb("vext", [128, 66 * nt], BF16)
    latT = sb("latT", [128, 4 * T], BF16)
    xnbf = sb("xnbf", [128, D], BF16)
    xnT = sb("xnT", [128, 8 * 512], BF16)
    q64 = sb("q64", [64, 16 * 512], BF16)
    sq64 = sb("sq64", [64, 16 * 512], BF16)
    arow = sb("arow", [1, 512], BF16)
    amaxv = sb("amaxv", [128, 1], F32)
    nbias = sb("nbias", [128, 1], F32)
    knt = sb("knt", [128, 1], F32)
    kn2t = sb("kn2t", [128, 1], F32)
    sqk = sb("sqk", [64, 512], BF16)
    pbf = [sb("pbf0", [128, 1024], BF16), sb("pbf1", [128, 1024], BF16)]
    ybuf = sb("ybuf", [128, 4 * D], BF16)
    yT = sb("yT", [128, 8 * 512], BF16)
    drb = sb("drb", [128, 4], F32)

    s_vec.inc(V.memset(vext[:, :], 1.0))
    vext_ones = s_vec.v

    lat_marks = []
    for I in range(ns):
        for t4 in range(4):
            i = 4 * I + t4
            b = xb[i % 2]
            s_xfr.wait(SY, s_xfr.v - (1 if i > 0 else 0))
            s_xld.begin(SY)
            s_xld.issue(SY.dma_start(b[:, :], x_h[i * 128:(i + 1) * 128, :]))
            s_xld.wait_mark(V, s_xld.mark())
            if i > 0:
                s_xtp.wait(V, 2 * i)
            s_xfr.inc(V.tensor_scalar(xnbf[:, :], b[:, :], stc(4)[:, i:i + 1],
                                      stc(5)[:, i:i + 1], ALU.mult, ALU.add))
            s_xfr.wait(TE)
            if i >= 1:
                s_xe4.wait(TE, i)
            for k in range(4):
                mm = TE.matmul(ps[:, 2048 + 128 * k:2048 + 128 * k + 128],
                               xnbf[:, 128 * k:128 * k + 128], idn[:, :],
                               start=True, stop=True)
            s_xtp.inc(mm)
            if i >= 1:
                s_xe5.wait(TE, i)
            for k in range(4, 8):
                mm = TE.matmul(ps[:, 2560 + 128 * (k - 4):2560 + 128 * (k - 4) + 128],
                               xnbf[:, 128 * k:128 * k + 128], idn[:, :],
                               start=True, stop=True)
            s_xtp.inc(mm)
            s_xtp.wait(V, 2 * i + 1)
            if I > 0 and t4 == 0:
                s_mm.wait(V, lat_marks[-1])
            for k in range(4):
                cp = V.tensor_copy(xnT[:, 512 * k + 128 * t4:512 * k + 128 * t4 + 128],
                                   ps[:, 2048 + 128 * k:2048 + 128 * k + 128])
            s_xe4.inc(cp)
            s_xtp.wait(S, 2 * i + 2)
            if I > 0 and t4 == 0:
                s_mm.wait(S, lat_marks[-1])
            for k in range(4, 8):
                cp = S.activation(xnT[:, 512 * k + 128 * t4:512 * k + 128 * t4 + 128],
                                  ps[:, 2560 + 128 * (k - 4):2560 + 128 * (k - 4) + 128],
                                  AF.Copy)
            s_xe5.inc(cp)
        s_xe4.wait(TE)
        s_xe5.wait(TE)
        s_vec.wait(TE, wts_ready)
        for lc in range(4):
            pslot = ps[:, 3584:4096]
            if not (I == 0 and lc == 0):
                s_lat.wait(TE)
            for dc in range(8):
                mm = TE.matmul(pslot,
                               wcd_bf[:, 512 * dc + 128 * lc:512 * dc + 128 * lc + 128],
                               xnT[:, 512 * dc:512 * dc + 512],
                               start=(dc == 0), stop=(dc == 7))
            s_mm.inc(mm)
            s_mm.wait(S)
            s_lat.inc(S.activation(latT[:, T * lc + 512 * I:T * lc + 512 * I + 512],
                                   pslot, AF.Copy))
        lat_marks.append(s_mm.v)

    s_vec.wait(S, vext_ones)
    for I in range(ns):
        s_kv.wait(TE)
        s_lat.wait(TE)
        for lc in range(4):
            mm = TE.matmul(ps[:, 3584:4096], wkv_bf[:, 128 * lc:128 * lc + 128],
                           latT[:, T * lc + 512 * I:T * lc + 512 * I + 512],
                           start=(lc == 0), stop=(lc == 3))
        s_mm.inc(mm)
        s_mm.wait(V)
        s_kv.inc(V.tensor_copy(kT64[:, 512 * I:512 * I + 512], ps[0:64, 3584:4096]))
    for j in range(nt):
        s_kv.wait(TE)
        for lc in range(4):
            mm = TE.matmul(ps[:, 3584:3712],
                           latT[:, T * lc + 128 * j:T * lc + 128 * j + 128],
                           wkv_bf[:, 128 * lc:128 * lc + 128],
                           start=(lc == 0), stop=(lc == 3))
        s_mm.inc(mm)
        s_mm.wait(V)
        s_kv.inc(V.tensor_copy(vext[:, 66 * j:66 * j + 64], ps[:, 3648:3712]))
    V.memset(knt[:, :], 0.0)
    s_vec.inc(V.engine_nop())
    for I in range(ns):
        s_mm.wait(S)
        s_kv.wait(S)
        s_sca.inc(S.activation(sqk[:, :], kT64[:, 512 * I:512 * I + 512], AF.Square))
        s_sca.wait(TE)
        s_vec.wait(TE)
        pslot = ps[0:1, 3584:4096]
        s_mm.inc(TE.matmul(pslot, ones64[:, :], sqk[:, :], start=True, stop=True))
        s_mm.wait(V)
        V.tensor_copy(arow[:, :], pslot)
        s_vec.inc(V.engine_nop())
        s_vec.wait(TE)
        s_mm.inc(TE.matmul(ps[:, 3072:3584], ones1b[:, :], arow[:, :],
                           start=True, stop=True))
        s_mm.wait(V)
        V.reduce_max(kn2t[:, :], ps[:, 3072:3584], axis=AX.X)
        vsp()
        s_vec.inc(V.tensor_max(knt[:, :], knt[:, :], kn2t[:, :]))
    knt_ready = s_vec.v

    out_marks = []
    gpar = 0
    psum_free = {0: 0, 1: 0}
    pbf_free = {0: 0, 1: 0}
    for I in range(ns):
        nblk = 4 * I + 4
        ngrp = (nblk + 1) // 2
        s_asm.wait(TE)
        for h in range(H):
            pslot = ps[0:64, 3584:4096]
            if not (I == 0 and h == 0):
                s_qp.wait(TE)
            for lc in range(4):
                mm = TE.matmul(pslot,
                               wq_bf[:, 1024 * lc + 64 * h:1024 * lc + 64 * h + 64],
                               latT[:, T * lc + 512 * I:T * lc + 512 * I + 512],
                               start=(lc == 0), stop=(lc == 3))
            s_mm.inc(mm)
            s_mm.wait(V)
            s_qp.inc(V.tensor_copy(q64[:, 512 * h:512 * h + 512], pslot))
        s_qp.wait(V)
        s_vec.inc(V.tensor_mul(sq64[:, :], q64[:, :], q64[:, :]))
        s_vec.wait(TE)
        for h in range(H):
            mm = TE.matmul(ps[0:1, 3072:3584], ones64[:, :],
                           sq64[:, 512 * h:512 * h + 512],
                           start=(h == 0), stop=(h == H - 1))
        s_mm.inc(mm)
        s_mm.wait(V)
        s_vec.inc(V.tensor_copy(arow[:, :], ps[0:1, 3072:3584]))
        s_vec.wait(TE)
        s_mm.inc(TE.matmul(ps[:, 3584:4096], ones1b[:, :], arow[:, :],
                           start=True, stop=True))
        s_mm.wait(V)
        V.reduce_max(amaxv[:, :], ps[:, 3584:4096], axis=AX.X)
        vsp()
        s_vec.inc(V.tensor_scalar(amaxv[:, :], amaxv[:, :], 1e-30, None, ALU.max))
        s_vec.wait(S)
        S.activation(amaxv[:, :], amaxv[:, :], AF.Ln, scale=knt[:, 0:1])
        s_sca.inc(S.activation(amaxv[:, :], amaxv[:, :], AF.Exp, scale=0.5))
        s_sca.wait(V)
        s_nb.inc(V.tensor_scalar(nbias[:, :], amaxv[:, :], -1.0, 35.0,
                                 ALU.mult, ALU.add))
        nb_ready = s_nb.v
        for h in range(H):
            exp_at = {}
            par = {}
            sc_at = {}

            def emit_pv(gx):
                s_exp.wait(TE, exp_at[gx])
                if gx == 0:
                    s_ye.wait(TE)
                j0v = 2 * gx
                jnv = min(2, nblk - j0v)
                last = None
                for jj in range(jnv):
                    j = j0v + jj
                    for ti in range(4):
                        if j <= 4 * I + ti:
                            last = TE.matmul(
                                ps[:, 2048 + 512 * ti:2048 + 512 * ti + 65],
                                pbf[par[gx]][:, 512 * jj + 128 * ti:
                                             512 * jj + 128 * ti + 128],
                                vext[:, 66 * j:66 * j + 65],
                                start=(j == 0), stop=(j == 4 * I + ti))
                s_pv.inc(last)
                pbf_free[par[gx]] = s_pv.v

            for g in range(ngrp):
                pp = gpar % 2
                gpar += 1
                par[g] = pp
                j0 = 2 * g
                jn = min(2, nblk - j0)
                s_exp.wait(TE, psum_free[pp])
                for jj in range(jn):
                    j = j0 + jj
                    s_sc.inc(TE.matmul(
                        ps[:, 1024 * pp + 512 * jj:1024 * pp + 512 * jj + 512],
                        kT64[:, 128 * j:128 * j + 128],
                        q64[:, 512 * h:512 * h + 512],
                        start=True, stop=True))
                sc_at[g] = s_sc.v
                nmsk = 0
                for jj in range(jn):
                    di = j0 + jj - 4 * I
                    if di >= 0:
                        if nmsk == 0:
                            s_sc.wait(V, sc_at[g])
                            s_bld.wait_mark(V, mask_mk)
                        pvw = ps[:, 1024 * pp + 512 * jj:
                                 1024 * pp + 512 * jj + 128 * (di + 1)]
                        s_msk.inc(V.tensor_tensor(
                            pvw, pvw,
                            mask4[:, 512 * di:512 * di + 128 * (di + 1)], ALU.add))
                        nmsk += 1
                if g >= 1:
                    emit_pv(g - 1)
                s_sc.wait(S, sc_at[g])
                if nmsk:
                    s_msk.wait(S)
                s_pv.wait(S, pbf_free[pp])
                s_nb.wait(S, nb_ready)
                s_exp.inc(S.activation(pbf[pp][:, 0:512 * jn],
                                       ps[:, 1024 * pp:1024 * pp + 512 * jn],
                                       AF.Exp, bias=nbias[:, 0:1]))
                exp_at[g] = s_exp.v
                psum_free[pp] = s_exp.v
            emit_pv(ngrp - 1)
            s_pv.wait(V)
            for ti in range(4):
                V.reciprocal(drb[:, ti:ti + 1],
                             ps[:, 2048 + 512 * ti + 64:2048 + 512 * ti + 65])
            vsp()
            for ti in range(4):
                s_ye.inc(V.tensor_scalar(
                    ybuf[:, D * ti + 64 * h:D * ti + 64 * h + 64],
                    ps[:, 2048 + 512 * ti:2048 + 512 * ti + 64],
                    drb[:, ti:ti + 1], None, ALU.mult))
        s_ye.wait(SY)
        if I > 0:
            s_pr.wait(SY)
        s_tp.begin(SY)
        for ti in range(4):
            for c in range(8):
                s_tp.issue(SY.dma_start(
                    yT[:, 512 * c + 128 * ti:512 * c + 128 * ti + 128],
                    ybuf[:, D * ti + 128 * c:D * ti + 128 * c + 128],
                    transpose=True))
        s_tp.wait_all(TE)
        for ti in range(4):
            i = 4 * I + ti
            b = xb[i % 2]
            fb = fbk[i % 2]
            s_xfr.wait(SY, s_xfr.v - (1 if i > 0 else 0))
            s_xld.begin(SY)
            s_xld.issue(SY.dma_start(b[:, :], x_h[i * 128:(i + 1) * 128, :]))
            asm_xmk = s_xld.mark()
            if i >= 2:
                s_asm.wait(G, s_asm.v - 2)
            s_fb.begin(G)
            s_fb.issue(G.dma_start(fb[:, :], spill_h[i * 128:(i + 1) * 128, :]))
            fb_mk = s_fb.mark()
            for half in range(2):
                pslot = ps[:, 3072 + 512 * half:3072 + 512 * half + 512]
                if not (I == 0 and ti == 0):
                    s_asm.wait(TE)
                for c in range(8):
                    mm = TE.matmul(
                        pslot, yT[:, 512 * c + 128 * ti:512 * c + 128 * ti + 128],
                        wpr_bf[:, 1024 * c + 512 * half:1024 * c + 512 * half + 512],
                        start=(c == 0), stop=(c == 7))
                s_pr.inc(mm)
            o = outst[i % 2]
            s_pr.wait(V)
            s_xld.wait_mark(V, asm_xmk)
            s_fb.wait_mark(V, fb_mk)
            if i >= 2:
                s_out.wait_mark(V, out_marks[i - 2])
            for half in range(2):
                pslot = ps[:, 3072 + 512 * half:3072 + 512 * half + 512]
                sl = slice(512 * half, 512 * half + 512)
                V.tensor_add(o[:, sl], pslot, b[:, sl])
                V.scalar_tensor_tensor(o[:, sl], fb[:, sl], stc(11)[:, i:i + 1],
                                       o[:, sl], ALU.mult, ALU.add)
                s_asm.inc(V.tensor_add(o[:, sl], o[:, sl], b2rep[:, sl]))
            s_xfr.inc(V.engine_nop())
            s_asm.wait(G)
            s_out.begin(G)
            s_out.issue(G.dma_start(out_h[i * 128:(i + 1) * 128, :], o[:, :]))
            out_marks.append(s_out.mark())

    s_out.wait_all(G)
    st.close()
    return nc


def _mask4_np():
    m = np.zeros((128, 4 * 512), dtype=np.float32)
    r = np.arange(128)
    for di in range(4):
        blk = m[:, 512 * di:512 * di + 512]
        blk[:, 0:128 * di] = NEG
        tri = blk[:, 128 * di:128 * di + 128]
        tri[r[:, None] > r[None, :]] = NEG
    return m


def kernel(L0_attn_x, L0_attn_Wc_down, L0_attn_Wq_up, L0_attn_Wkv_up,
           L0_attn_Wc_proj, L0_ffn_w1, L0_ffn_b1, L0_ffn_w2, L0_ffn_b2):
    T = 4096
    nc = build_nc(T)
    x = np.ascontiguousarray(np.asarray(L0_attn_x, dtype=np.float32))
    shared = {
        "wcd": np.ascontiguousarray(np.asarray(L0_attn_Wc_down, dtype=np.float32)),
        "wq": np.ascontiguousarray(np.asarray(L0_attn_Wq_up, dtype=np.float32)),
        "wkv": np.ascontiguousarray(np.asarray(L0_attn_Wkv_up, dtype=np.float32)),
        "wpr": np.ascontiguousarray(np.asarray(L0_attn_Wc_proj, dtype=np.float32)),
        "w1": np.ascontiguousarray(np.asarray(L0_ffn_w1, dtype=np.float32)),
        "b1": np.ascontiguousarray(np.asarray(L0_ffn_b1, dtype=np.float32)).reshape(1, -1),
        "w2": np.ascontiguousarray(np.asarray(L0_ffn_w2, dtype=np.float32)),
        "b2": np.ascontiguousarray(np.asarray(L0_ffn_b2, dtype=np.float32)).reshape(1, -1),
        "msk": _mask4_np(),
        "idn": np.eye(128, dtype=ml_dtypes.bfloat16),
    }
    in_maps = [dict(shared, x=x[c]) for c in range(8)]
    res = run_bass_kernel_spmd(nc, in_maps, core_ids=list(range(8)))
    return np.stack([np.asarray(res.results[c]["out"]) for c in range(8)], axis=0)


# version 13
# speedup vs baseline: 1.6992x; 1.1400x over previous
import numpy as np
import ml_dtypes
from contextlib import ExitStack

import concourse.bass as bass
import concourse.mybir as mybir
from concourse.bass_utils import run_bass_kernel_spmd

F32 = mybir.dt.float32
BF16 = mybir.dt.bfloat16
AF = mybir.ActivationFunctionType
ALU = mybir.AluOpType
AX = mybir.AxisListType

MAGIC = 12582912.0
NEG = -1.0e30
D = 1024
L = 512
FF = 4096
H = 16
DH = 64
EPS = 1e-5


def build_nc(T=4096):
    nc = bass.Bass("TRN2", target_bir_lowering=False, detect_race_conditions=False)
    nt = T // 128
    ns = T // 512

    x_h = nc.dram_tensor("x", [T, D], F32, kind="ExternalInput")
    wcd_h = nc.dram_tensor("wcd", [D, L], F32, kind="ExternalInput")
    wq_h = nc.dram_tensor("wq", [L, D], F32, kind="ExternalInput")
    wkv_h = nc.dram_tensor("wkv", [L, 2 * DH], F32, kind="ExternalInput")
    wpr_h = nc.dram_tensor("wpr", [D, D], F32, kind="ExternalInput")
    w1_h = nc.dram_tensor("w1", [FF, D], F32, kind="ExternalInput")
    b1_h = nc.dram_tensor("b1", [1, FF], F32, kind="ExternalInput")
    w2_h = nc.dram_tensor("w2", [D, FF], F32, kind="ExternalInput")
    b2_h = nc.dram_tensor("b2", [1, D], F32, kind="ExternalInput")
    msk_h = nc.dram_tensor("msk", [128, 4 * 512], F32, kind="ExternalInput")
    idn_h = nc.dram_tensor("idn", [128, 128], BF16, kind="ExternalInput")
    out_h = nc.dram_tensor("out", [T, D], F32, kind="ExternalOutput")
    spill_h = nc.dram_tensor("ffn_spill", [T, D], BF16, kind="ExternalOutput")

    st = ExitStack()
    sb = lambda name, shape, dt: st.enter_context(nc.sbuf_tensor(name, shape, dt))
    sem = lambda name: st.enter_context(nc.semaphore(name))

    xb = [sb("xb0", [128, D], F32), sb("xb1", [128, D], F32)]
    b2rep = sb("b2rep", [128, D], BF16)
    ones1f = sb("ones1f", [1, 128], F32)
    ones1b = sb("ones1b", [1, 128], BF16)
    onesff = sb("onesff", [128, 128], F32)
    ones64 = sb("ones64", [64, 1], BF16)
    idn = sb("idn_sb", [128, 128], BF16)
    stt = sb("stt", [128, 16 * nt], F32)

    def stc(i):
        return stt[:, i * nt:(i + 1) * nt]

    w1s = sb("w1s", [128, 1], F32)
    mw1s = sb("mw1s", [128, 1], F32)
    w2s = sb("w2s", [128, 1], F32)
    mw2s = sb("mw2s", [128, 1], F32)
    wabs = sb("wabs", [128, 40], F32)
    brow = sb("brow", [1, 512], F32)
    tmp1 = sb("tmp1", [128, 1], F32)
    tmp2 = sb("tmp2", [128, 1], F32)
    tmp3 = sb("tmp3", [128, 128], F32)

    ps = st.enter_context(nc.psum_tensor("ps", [128, 4096], F32))

    SY, V, S, TE, G = nc.sync, nc.vector, nc.scalar, nc.tensor, nc.gpsimd

    class Ctr:
        def __init__(self, name):
            self.h = sem(name)
            self.v = 0

        def inc(self, inst, n=1):
            inst.then_inc(self.h, n)
            self.v += n

        def wait(self, eng, val=None):
            val = self.v if val is None else val
            if val > 0:
                eng.wait_ge(self.h, val)

    class DCtr:

        def __init__(self, name):
            self.h = [sem(name + "a"), sem(name + "b")]
            self.v = [0, 0]
            self.b = 0
            self.cur = None

        def __init_serial__(self):
            pass

        def begin(self, eng):
            p = self.b % 2
            self.b += 1
            self.cur = p
            if getattr(self, "serial", False):
                for pp in (0, 1):
                    if self.v[pp] > 0:
                        eng.wait_ge(self.h[pp], self.v[pp])
            elif self.v[p] > 0:
                eng.wait_ge(self.h[p], self.v[p])
            return p

        def issue(self, inst):
            p = self.cur
            inst.then_inc(self.h[p], 16)
            self.v[p] += 16

        def mark(self):
            return (self.cur, self.v[self.cur])

        def wait_mark(self, eng, mk):
            if mk is not None:
                eng.wait_ge(self.h[mk[0]], mk[1])

        def wait_all(self, eng):
            for p in (0, 1):
                if self.v[p] > 0:
                    eng.wait_ge(self.h[p], self.v[p])

    s_xld = DCtr("s_xld")
    s_xfr = Ctr("s_xfr")
    s_xfs = Ctr("s_xfs")
    s_bld = DCtr("s_bld")
    s_mm = Ctr("s_mm")
    s_vec = Ctr("s_vec")
    s_sca = Ctr("s_sca")
    s_wld = DCtr("s_wld")
    s_wfr = Ctr("s_wfr")
    s_tp = DCtr("s_tp")
    s_tpm = Ctr("s_tpm")
    s_he6 = Ctr("s_he6")
    s_he7 = Ctr("s_he7")
    s_wtp = Ctr("s_wtp")
    s_we6 = Ctr("s_we6")
    s_we7 = Ctr("s_we7")
    s_xtp = Ctr("s_xtp")
    s_xe4 = Ctr("s_xe4")
    s_xe5 = Ctr("s_xe5")
    s_ytp = Ctr("s_ytp")
    s_ye4 = Ctr("s_ye4")
    s_ye5 = Ctr("s_ye5")
    s_ff1 = Ctr("s_ff1")
    s_hev = Ctr("s_hev")
    s_ff2 = Ctr("s_ff2")
    s_f2e = Ctr("s_f2e")
    s_sp = DCtr("s_sp")
    s_q1 = Ctr("s_q1")
    s_q2 = Ctr("s_q2")
    s_g = Ctr("s_g")
    s_lat = Ctr("s_lat")
    s_sc = Ctr("s_sc")
    s_msk = Ctr("s_msk")
    s_exp = Ctr("s_exp")
    s_pv = Ctr("s_pv")
    s_ye = Ctr("s_ye")
    s_qp = Ctr("s_qp")
    s_nb = Ctr("s_nb")
    s_out = DCtr("s_out")
    s_fb = DCtr("s_fb")
    s_pr = Ctr("s_pr")
    s_asm = Ctr("s_asm")
    s_kv = Ctr("s_kv")
    s_wc = DCtr("s_wc")

    def vsp():
        V.tensor_scalar(tmp3[:, :], tmp3[:, :], 1.0, None, ALU.mult)

    s_bld.begin(SY)
    s_bld.issue(SY.dma_start(idn[:, :], idn_h[:, :]))
    idn_mk = s_bld.mark()
    V.memset(tmp3[:, :], 0.0)
    V.memset(tmp1[:, :], 0.0)
    V.memset(ones1f[:, :], 1.0)
    V.memset(ones1b[:, :], 1.0)
    V.memset(onesff[:, :], 1.0)
    s_vec.inc(V.memset(ones64[:, :], 1.0))

    p1st = ExitStack()
    sb1 = lambda name, shape, dt: p1st.enter_context(nc.sbuf_tensor(name, shape, dt))
    t1b = sb1("t1b", [128, D], F32)
    b1rep = sb1("b1rep", [128, FF], BF16)
    w1qT = sb1("w1qT", [128, 8 * FF], BF16)
    w2qT = sb1("w2qT", [128, 32 * D], BF16)
    p0wst = ExitStack()
    big_a = p0wst.enter_context(nc.sbuf_tensor("big_a", [128, FF], F32))
    wq_sb = p0wst.enter_context(nc.sbuf_tensor("wqs0", [128, 1024], BF16))

    for i in range(nt):
        b = xb[i % 2]
        if i >= 2:
            s_xfr.wait(SY, i - 1)
            s_xfs.wait(SY, i - 1)
        s_xld.begin(SY)
        s_xld.issue(SY.dma_start(b[:, :], x_h[i * 128:(i + 1) * 128, :]))
        xmk = s_xld.mark()
        s_xld.wait_mark(V, xmk)
        s_xfr.inc(V.reduce_sum(stc(0)[:, i:i + 1], b[:, :], axis=AX.X))
        V.reduce_max(stc(2)[:, i:i + 1], b[:, :], axis=AX.X)
        V.tensor_reduce(stc(3)[:, i:i + 1], b[:, :], axis=AX.X, op=ALU.min)
        s_xld.wait_mark(S, xmk)
        s_xfs.inc(S.activation(t1b[:, :], b[:, :], AF.Square,
                               accum_out=stc(1)[:, i:i + 1]))
        wslot = big_a[:, (i % 4) * D:(i % 4) * D + D]
        s_wfr.wait(G, s_wfr.v - 3)
        s_wld.begin(G)
        s_wld.issue(G.dma_start(wslot, w1_h[i * 128:(i + 1) * 128, :]))
        s_wld.wait_mark(S, s_wld.mark())
        s_wfr.inc(S.activation(t1b[:, :], wslot, AF.Abs,
                               accum_out=wabs[:, i:i + 1]))

    s_xld.wait_all(V)
    s_xfr.wait(V, nt)
    s_xfs.wait(V, nt)
    inv_d = 1.0 / D
    m = stc(12)
    V.tensor_scalar(m, stc(0), inv_d, None, ALU.mult)
    V.tensor_scalar(stc(13), stc(1), inv_d, None, ALU.mult)
    V.tensor_mul(stc(14), m, m)
    vsp()
    V.tensor_sub(stc(1), stc(13), stc(14))
    vsp()
    s_vec.inc(V.tensor_scalar(stc(15), stc(1), EPS, None, ALU.add))
    s_vec.wait(S)
    s_sca.inc(S.activation(stc(14), stc(15), AF.Ln))
    s_sca.wait(V)
    s_vec.inc(V.tensor_scalar(stc(14), stc(14), -0.5, None, ALU.mult))
    s_vec.wait(S)
    s_sca.inc(S.activation(stc(4), stc(14), AF.Exp))
    s_sca.wait(V)
    V.scalar_tensor_tensor(stc(5), stc(4), -1.0, m, ALU.mult, ALU.mult)
    V.tensor_sub(stc(2), stc(2), m)
    V.tensor_sub(stc(3), m, stc(3))
    vsp()
    V.tensor_max(stc(2), stc(2), stc(3))
    vsp()
    V.tensor_mul(stc(2), stc(2), stc(4))
    V.tensor_mul(stc(14), stc(4), stc(4))
    vsp()
    V.tensor_mul(stc(14), stc(1), stc(14))
    vsp()
    s_vec.inc(V.tensor_scalar(stc(14), stc(14), EPS, None, ALU.add))
    s_vec.wait(S)
    s_sca.inc(S.activation(stc(15), stc(14), AF.Ln))
    s_sca.wait(V)
    s_vec.inc(V.tensor_scalar(stc(15), stc(15), -0.5, None, ALU.mult))
    s_vec.wait(S)
    s_sca.inc(S.activation(stc(3), stc(15), AF.Exp))
    s_sca.wait(V)
    V.tensor_mul(stc(2), stc(2), stc(3))
    vsp()
    s_vec.inc(V.tensor_scalar(stc(2), stc(2), 1e-5, None, ALU.max))
    s_vec.wait(S)
    s_sca.inc(S.activation(stc(15), stc(2), AF.Ln))
    s_sca.wait(V)
    s_vec.inc(V.tensor_scalar(stc(15), stc(15), -1.0, None, ALU.mult))
    s_vec.wait(S)
    s_sca.inc(S.activation(stc(14), stc(15), AF.Exp))
    s_sca.wait(V)
    V.tensor_scalar(stc(14), stc(14), 127.0, None, ALU.mult)
    V.tensor_mul(stc(6), stc(4), stc(3))
    V.tensor_mul(stc(7), stc(5), stc(3))
    V.tensor_mul(stc(6), stc(6), stc(14))
    s_vec.inc(V.tensor_mul(stc(7), stc(7), stc(14)))


    s_vec.wait(TE)
    for c in range(FF // 512 + D // 512):
        if c < FF // 512:
            src_ap = b1_h[0:1, c * 512:(c + 1) * 512]
            dst = b1rep[:, c * 512:(c + 1) * 512]
        else:
            c2 = c - FF // 512
            src_ap = b2_h[0:1, c2 * 512:(c2 + 1) * 512]
            dst = b2rep[:, c2 * 512:(c2 + 1) * 512]
        pslot = ps[:, (c % 2) * 512:(c % 2) * 512 + 512]
        if c >= 1:
            s_mm.wait(SY)
        s_bld.begin(SY)
        s_bld.issue(SY.dma_start(brow[:, :], src_ap))
        s_bld.wait_mark(TE, s_bld.mark())
        if c >= 2:
            s_vec.wait(TE, s_vec.v - 1)
        s_mm.inc(TE.matmul(pslot, ones1f[:, :], brow[:, :], start=True, stop=True))
        s_mm.wait(V)
        s_vec.inc(V.tensor_copy(dst, pslot))


    def w_pass1(w_h, cols, nstr, off):
        for sx in range(nstr):
            b = big_a
            s_wfr.wait(G)
            s_wld.begin(G)
            s_wld.issue(G.dma_start(b[:, 0:cols], w_h[sx * 128:(sx + 1) * 128, :]))
            s_wld.wait_mark(V, s_wld.mark())
            s_wfr.inc(V.reduce_sum(wabs[:, off + sx:off + sx + 1], b[:, 0:cols],
                                   axis=AX.X, apply_absolute_value=True))

    w_pass1(w2_h, FF, D // 128, 32)
    s_wfr.wait(V)
    s_vec.inc(V.engine_nop())
    s_vec.wait(TE)
    s_wfr.wait(TE)
    s_mm.inc(TE.matmul(ps[:, 2048:2088], onesff[:, :], wabs[:, 0:40],
                       start=True, stop=True))
    s_mm.wait(V)
    V.reduce_sum(w1s[:, :], ps[:, 2048:2080], axis=AX.X)
    V.reduce_sum(w2s[:, :], ps[:, 2080:2088], axis=AX.X)
    for wsx, mwx, nel in ((w1s, mw1s, FF * D), (w2s, mw2s, D * FF)):
        V.tensor_scalar(tmp2[:, :], wsx[:, :], 1.0 / nel, None, ALU.mult)
        vsp()
        s_vec.inc(V.tensor_scalar(tmp2[:, :], tmp2[:, :], 1e-5, None, ALU.max))
        s_vec.wait(S)
        s_sca.inc(S.activation(tmp1[:, :], tmp2[:, :], AF.Copy))
        s_sca.wait(V)
        V.reciprocal(wsx[:, :], tmp1[:, :])
        vsp()
        s_vec.inc(V.tensor_scalar(mwx[:, :], tmp2[:, :], 1.0 / 127.0, None, ALU.mult))
    s_vec.inc(V.engine_nop())
    V.tensor_scalar(stc(8), stc(2), mw1s[:, 0:1], None, ALU.mult)

    wqc = {"n": 0}

    def w_pass2(w_h, cols, nstr, s_rec, qT, chunk_cols):
        q = wq_sb
        for sx in range(nstr):
            b = big_a
            s_wfr.wait(G)
            s_wld.begin(G)
            s_wld.issue(G.dma_start(b[:, 0:cols], w_h[sx * 128:(sx + 1) * 128, :]))
            s_wld.wait_mark(V, s_wld.mark())
            V.tensor_scalar(b[:, 0:cols], b[:, 0:cols], s_rec[:, 0:1], MAGIC,
                            ALU.mult, ALU.add)
            V.tensor_scalar(b[:, 0:cols], b[:, 0:cols], MAGIC, 1.0,
                            ALU.subtract, ALU.min)
            for qc in range(cols // 1024):
                n = wqc["n"]
                s_wtp.wait(V)
                s_vec.inc(V.tensor_scalar(q[:, :], b[:, 1024 * qc:1024 * qc + 1024],
                                           -1.0, None, ALU.max))
                if qc == cols // 1024 - 1:
                    s_wfr.inc(V.engine_nop())
                s_vec.wait(TE)
                if n == 0:
                    s_bld.wait_mark(TE, idn_mk)
                if n >= 1:
                    s_we6.wait(TE, n)
                for k in range(4):
                    mm = TE.matmul(ps[:, 3072 + 128 * k:3072 + 128 * k + 128],
                                   q[:, 128 * k:128 * k + 128], idn[:, :],
                                   start=True, stop=True)
                s_wtp.inc(mm)
                if n >= 1:
                    s_we7.wait(TE, n)
                for k in range(4, 8):
                    mm = TE.matmul(ps[:, 3584 + 128 * (k - 4):3584 + 128 * (k - 4) + 128],
                                   q[:, 128 * k:128 * k + 128], idn[:, :],
                                   start=True, stop=True)
                s_wtp.inc(mm)
                s_wtp.wait(S, 2 * n + 1)
                for k in range(4):
                    kk = 8 * qc + k
                    act = S.activation(
                        qT[:, chunk_cols * kk + 128 * sx:chunk_cols * kk + 128 * sx + 128],
                        ps[:, 3072 + 128 * k:3072 + 128 * k + 128], AF.Copy)
                s_we6.inc(act)
                s_wtp.wait(S, 2 * n + 2)
                for k in range(4, 8):
                    kk = 8 * qc + k
                    act = S.activation(
                        qT[:, chunk_cols * kk + 128 * sx:chunk_cols * kk + 128 * sx + 128],
                        ps[:, 3584 + 128 * (k - 4):3584 + 128 * (k - 4) + 128], AF.Copy)
                s_we7.inc(act)
                wqc["n"] = n + 1

    w_pass2(w1_h, D, FF // 128, w1s, w1qT, FF)
    w_pass2(w2_h, FF, D // 128, w2s, w2qT, D)

    p0wst.close()
    hg = [sb1("hg0", [128, FF], BF16), sb1("hg1", [128, FF], BF16)]
    xq1 = sb1("xq1", [128, D], BF16)
    xq1T = [sb1("xq1T0", [128, D], BF16), sb1("xq1T1", [128, D], BF16)]
    hq = [sb1("hq0", [128, 1024], BF16), sb1("hq1", [128, 1024], BF16)]
    hqT = sb1("hqT", [128, FF], BF16)
    qtmp = sb1("qtmp", [128, 1024], F32)
    spst = [sb1("spst0", [128, D], BF16), sb1("spst1", [128, D], BF16)]
    sp_marks = []
    hev_mark = {}
    ff2_mark = {}
    xq1_mks = {}
    xld_mks = {}
    sq_mark = {}

    def p1_loadX(i):
        b = xb[i % 2]
        if i == 0:
            s_xfr.wait(SY)
            s_xfs.wait(SY)
            p1_loadX.base = s_xfr.v
        elif i >= 2:
            s_xfr.wait(SY, p1_loadX.base + i - 1)
        s_xld.begin(SY)
        s_xld.issue(SY.dma_start(b[:, :], x_h[i * 128:(i + 1) * 128, :]))
        xld_mks[i] = s_xld.mark()

    def p1_prepX(i):
        b = xb[i % 2]
        s_xld.wait_mark(V, xld_mks[i])
        s_xfr.inc(V.tensor_scalar(t1b[:, :], b[:, :], stc(6)[:, i:i + 1],
                                  stc(7)[:, i:i + 1], ALU.mult, ALU.add))
        if i > 0:
            s_tp.wait_mark(V, xq1_mks[i - 1])
        else:
            s_wtp.wait(V)
        s_q1.inc(V.tensor_scalar(xq1[:, :], t1b[:, :], MAGIC, MAGIC,
                                 ALU.add, ALU.subtract))
        s_q1.wait(SY)
        if i > 1:
            s_ff1.wait(SY, 8 * (i - 1))
        elif i == 0:
            s_wtp.wait(SY)
            s_wfr.wait(SY)
        s_tp.begin(SY)
        for k in range(8):
            s_tp.issue(SY.dma_start(xq1T[i % 2][:, 128 * k:128 * k + 128],
                                    xq1[:, 128 * k:128 * k + 128], transpose=True))
        xq1_mks[i] = s_tp.mark()

    def p1_ffn1(i):
        if i == 0:
            s_we6.wait(TE)
            s_we7.wait(TE)
            s_wfr.wait(V)
        s_tp.wait_mark(TE, xq1_mks[i])
        for fc in range(8):
            pslot = ps[:, (fc % 4) * 512:(fc % 4) * 512 + 512]
            s_hev.wait(TE, s_hev.v - 3)
            for dc in range(8):
                mm = TE.matmul(pslot, xq1T[i % 2][:, 128 * dc:128 * dc + 128],
                               w1qT[:, FF * dc + 512 * fc:FF * dc + 512 * fc + 512],
                               start=(dc == 0), stop=(dc == 7))
            s_ff1.inc(mm)
            s_ff1.wait(V)
            if fc == 0 and i >= 2:
                s_sca.wait(V, sq_mark[i - 2])
            s_hev.inc(V.scalar_tensor_tensor(
                hg[i % 2][:, 512 * fc:512 * fc + 512], pslot, stc(8)[:, i:i + 1],
                b1rep[:, 512 * fc:512 * fc + 512], ALU.mult, ALU.add))
        hev_mark[i] = s_hev.v

    def p1_post_head(i):
        h = hg[i % 2]
        g = hqT
        s_hev.wait(S, hev_mark[i])
        if i > 0:
            s_ff2.wait(S, ff2_mark[i - 1])
        s_g.inc(S.activation(g[:, :], h[:, :], AF.Gelu))
        s_g.wait(V)
        s_vec.inc(V.reduce_max(stc(13)[:, i:i + 1], g[:, :], axis=AX.X,
                               apply_absolute_value=True))
        s_g.wait(S)
        s_sca.inc(S.activation(h[:, :], g[:, :], AF.Square,
                               accum_out=stc(9)[:, i:i + 1]))
        sq_mark[i] = s_sca.v
        s_vec.wait(S)
        s_sca.inc(S.activation(stc(15)[:, i:i + 1], stc(13)[:, i:i + 1], AF.Copy))
        s_sca.wait(V)
        V.reciprocal(stc(14)[:, i:i + 1], stc(15)[:, i:i + 1])
        V.tensor_scalar(stc(10)[:, i:i + 1], stc(13)[:, i:i + 1], mw2s[:, 0:1],
                        None, ALU.mult)
        vsp()
        V.tensor_scalar(stc(14)[:, i:i + 1], stc(14)[:, i:i + 1],
                        127.0, None, ALU.mult)
        vsp()
        s_vec.inc(V.engine_nop())
        for kc in (0, 1):
            ck = i * 4 + kc
            pk = ck - 2
            if pk >= 0:
                s_tpm.wait(V, 2 * (pk + 1))
            V.tensor_scalar(qtmp[:, :], g[:, 1024 * kc:1024 * kc + 1024],
                            stc(14)[:, i:i + 1], MAGIC, ALU.mult, ALU.add)
            s_q2.inc(V.tensor_scalar(hq[kc % 2][:, :], qtmp[:, :], MAGIC, None,
                                     ALU.subtract))

    def p1_post_ffn2(i):
        g = hqT
        for kc in range(4):
            ck = i * 4 + kc
            pk = ck - 2
            if kc >= 2:
                s_tpm.wait(V, 2 * (pk + 1))
                V.tensor_scalar(qtmp[:, :], g[:, 1024 * kc:1024 * kc + 1024],
                                stc(14)[:, i:i + 1], MAGIC, ALU.mult, ALU.add)
                s_q2.inc(V.tensor_scalar(hq[kc % 2][:, :], qtmp[:, :], MAGIC,
                                         None, ALU.subtract))
            s_q2.wait(TE)
            if ck == 0:
                s_bld.wait_mark(TE, idn_mk)
            if ck >= 1:
                s_he6.wait(TE, ck)
            for k in range(4):
                mm = TE.matmul(ps[:, 3072 + 128 * k:3072 + 128 * k + 128],
                               hq[kc % 2][:, 128 * k:128 * k + 128],
                               idn[:, :], start=True, stop=True)
            s_tpm.inc(mm)
            if ck >= 1:
                s_he7.wait(TE, ck)
            for k in range(4, 8):
                mm = TE.matmul(ps[:, 3584 + 128 * (k - 4):3584 + 128 * (k - 4) + 128],
                               hq[kc % 2][:, 128 * k:128 * k + 128],
                               idn[:, :], start=True, stop=True)
            s_tpm.inc(mm)
            s_tpm.wait(V, 2 * ck + 1)
            s_he6.inc(V.tensor_copy(hqT[:, 1024 * kc:1024 * kc + 512],
                                    ps[:, 3072:3584]))
            s_tpm.wait(S, 2 * ck + 2)
            s_he7.inc(S.activation(hqT[:, 1024 * kc + 512:1024 * kc + 1024],
                                   ps[:, 3584:4096], AF.Copy))
            s_he6.wait(TE, 4 * i + kc + 1)
            s_he7.wait(TE, 4 * i + kc + 1)
            if kc == 0:
                s_f2e.wait(TE)
            for dc2 in range(2):
                pslot = ps[:, 2048 + dc2 * 512:2048 + dc2 * 512 + 512]
                for k8 in range(8):
                    fc = 8 * kc + k8
                    mm2 = TE.matmul(
                        pslot, hqT[:, 128 * fc:128 * fc + 128],
                        w2qT[:, D * fc + 512 * dc2:D * fc + 512 * dc2 + 512],
                        start=(fc == 0), stop=(fc == 31))
                if kc == 3:
                    s_ff2.inc(mm2)
        sp = spst[i % 2]
        s_ff2.wait(V)
        if i >= 2:
            s_sp.wait_mark(V, sp_marks[i - 2])
        for dc2 in range(2):
            pslot = ps[:, 2048 + dc2 * 512:2048 + dc2 * 512 + 512]
            s_f2e.inc(V.tensor_scalar(sp[:, 512 * dc2:512 * dc2 + 512], pslot,
                                      stc(10)[:, i:i + 1], None, ALU.mult))
        ff2_mark[i] = s_ff2.v
        s_f2e.wait(G)
        s_sp.begin(G)
        s_sp.issue(G.dma_start(spill_h[i * 128:(i + 1) * 128, :], sp[:, :]))
        sp_marks.append(s_sp.mark())

    for i in range(nt + 1):
        if i < nt:
            if i == 0:
                p1_loadX(0)
                p1_loadX(1)
                p1_prepX(0)
            if i + 2 < nt:
                p1_loadX(i + 2)
            if i + 1 < nt:
                p1_prepX(i + 1)
            if i >= 1:
                p1_post_head(i - 1)
            p1_ffn1(i)
        if i >= 1:
            if i == nt:
                p1_post_head(i - 1)
            p1_post_ffn2(i - 1)

    p1st.close()
    for eng in (G, SY, V, S, TE):
        s_sp.wait_all(eng)
    V.tensor_scalar(stc(15), stc(9), 1.0 / FF, EPS, ALU.mult, ALU.add)
    s_vec.inc(V.engine_nop())
    s_vec.wait(S)
    S.activation(stc(15), stc(15), AF.Ln)
    s_sca.inc(S.activation(stc(15), stc(15), AF.Exp, scale=0.5))
    s_sca.wait(V)
    s_vec.inc(V.reciprocal(stc(11), stc(15)))

    wcd_bf = sb("wcd_bf", [128, 8 * 512], BF16)
    wq_bf = sb("wq_bf", [128, 4 * 1024], BF16)
    wkv_bf = sb("wkv_bf", [128, 4 * 128], BF16)
    wpr_bf = sb("wpr_bf", [128, 8 * 1024], BF16)
    wtmp = sb("wtmp", [128, 1024], F32)

    def load_cast(src_h, nchunk, ccols, dst, scale=None):
        for c in range(nchunk):
            s_vec.wait(G)
            s_wc.begin(G)
            s_wc.issue(G.dma_start(wtmp[:, 0:ccols], src_h[c * 128:(c + 1) * 128, :]))
            s_wc.wait_mark(V, s_wc.mark())
            if scale is None:
                s_vec.inc(V.tensor_copy(dst[:, c * ccols:(c + 1) * ccols],
                                        wtmp[:, 0:ccols]))
            else:
                s_vec.inc(V.tensor_scalar(dst[:, c * ccols:(c + 1) * ccols],
                                          wtmp[:, 0:ccols], scale, None, ALU.mult))

    load_cast(wcd_h, 8, 512, wcd_bf)
    load_cast(wq_h, 4, 1024, wq_bf, scale=0.125)
    load_cast(wkv_h, 4, 128, wkv_bf)
    load_cast(wpr_h, 8, 1024, wpr_bf)
    wts_ready = s_vec.v
    mask4 = sb("mask4", [128, 2048], F32)
    s_bld.begin(SY)
    s_bld.issue(SY.dma_start(mask4[:, :], msk_h[:, :]))
    mask_mk = s_bld.mark()

    outst = [sb("outst0", [128, D], F32), sb("outst1", [128, D], F32)]
    fbk = [sb("fbk0", [128, D], BF16), sb("fbk1", [128, D], BF16)]
    kT64 = sb("kT64", [64, T], BF16)
    vext = sb("vext", [128, 66 * nt], BF16)
    latT = sb("latT", [128, 4 * T], BF16)
    xnbf = sb("xnbf", [128, D], BF16)
    xnT = sb("xnT", [128, 8 * 512], BF16)
    q64 = sb("q64", [64, 16 * 512], BF16)
    sq64 = sb("sq64", [64, 16 * 512], BF16)
    arow = sb("arow", [1, 512], BF16)
    amaxv = sb("amaxv", [128, 1], F32)
    nbias = sb("nbias", [128, 1], F32)
    knt = sb("knt", [128, 1], F32)
    kn2t = sb("kn2t", [128, 1], F32)
    sqk = sb("sqk", [64, 512], BF16)
    pbf = [sb("pbf0", [128, 1024], BF16), sb("pbf1", [128, 1024], BF16)]
    ybuf = sb("ybuf", [128, 4 * D], BF16)
    yT = sb("yT", [128, 8 * 512], BF16)
    drb = sb("drb", [128, 4], F32)

    s_vec.inc(V.memset(vext[:, :], 1.0))
    vext_ones = s_vec.v

    lat_marks = []
    for I in range(ns):
        for t4 in range(4):
            i = 4 * I + t4
            b = xb[i % 2]
            s_xfr.wait(SY, s_xfr.v - (1 if i > 0 else 0))
            s_xld.begin(SY)
            s_xld.issue(SY.dma_start(b[:, :], x_h[i * 128:(i + 1) * 128, :]))
            s_xld.wait_mark(V, s_xld.mark())
            if i > 0:
                s_xtp.wait(V, 2 * i)
            s_xfr.inc(V.tensor_scalar(xnbf[:, :], b[:, :], stc(4)[:, i:i + 1],
                                      stc(5)[:, i:i + 1], ALU.mult, ALU.add))
            s_xfr.wait(TE)
            if i >= 1:
                s_xe4.wait(TE, i)
            for k in range(4):
                mm = TE.matmul(ps[:, 2048 + 128 * k:2048 + 128 * k + 128],
                               xnbf[:, 128 * k:128 * k + 128], idn[:, :],
                               start=True, stop=True)
            s_xtp.inc(mm)
            if i >= 1:
                s_xe5.wait(TE, i)
            for k in range(4, 8):
                mm = TE.matmul(ps[:, 2560 + 128 * (k - 4):2560 + 128 * (k - 4) + 128],
                               xnbf[:, 128 * k:128 * k + 128], idn[:, :],
                               start=True, stop=True)
            s_xtp.inc(mm)
            s_xtp.wait(V, 2 * i + 1)
            if I > 0 and t4 == 0:
                s_mm.wait(V, lat_marks[-1])
            for k in range(4):
                cp = V.tensor_copy(xnT[:, 512 * k + 128 * t4:512 * k + 128 * t4 + 128],
                                   ps[:, 2048 + 128 * k:2048 + 128 * k + 128])
            s_xe4.inc(cp)
            s_xtp.wait(S, 2 * i + 2)
            if I > 0 and t4 == 0:
                s_mm.wait(S, lat_marks[-1])
            for k in range(4, 8):
                cp = S.activation(xnT[:, 512 * k + 128 * t4:512 * k + 128 * t4 + 128],
                                  ps[:, 2560 + 128 * (k - 4):2560 + 128 * (k - 4) + 128],
                                  AF.Copy)
            s_xe5.inc(cp)
        s_xe4.wait(TE)
        s_xe5.wait(TE)
        s_vec.wait(TE, wts_ready)
        for lc in range(4):
            pslot = ps[:, 3584:4096]
            if not (I == 0 and lc == 0):
                s_lat.wait(TE)
            for dc in range(8):
                mm = TE.matmul(pslot,
                               wcd_bf[:, 512 * dc + 128 * lc:512 * dc + 128 * lc + 128],
                               xnT[:, 512 * dc:512 * dc + 512],
                               start=(dc == 0), stop=(dc == 7))
            s_mm.inc(mm)
            s_mm.wait(S)
            s_lat.inc(S.activation(latT[:, T * lc + 512 * I:T * lc + 512 * I + 512],
                                   pslot, AF.Copy))
        lat_marks.append(s_mm.v)

    s_vec.wait(S, vext_ones)
    for I in range(ns):
        s_kv.wait(TE)
        s_lat.wait(TE)
        for lc in range(4):
            mm = TE.matmul(ps[:, 3584:4096], wkv_bf[:, 128 * lc:128 * lc + 128],
                           latT[:, T * lc + 512 * I:T * lc + 512 * I + 512],
                           start=(lc == 0), stop=(lc == 3))
        s_mm.inc(mm)
        s_mm.wait(V)
        s_kv.inc(V.tensor_copy(kT64[:, 512 * I:512 * I + 512], ps[0:64, 3584:4096]))
    for j in range(nt):
        s_kv.wait(TE)
        for lc in range(4):
            mm = TE.matmul(ps[:, 3584:3712],
                           latT[:, T * lc + 128 * j:T * lc + 128 * j + 128],
                           wkv_bf[:, 128 * lc:128 * lc + 128],
                           start=(lc == 0), stop=(lc == 3))
        s_mm.inc(mm)
        s_mm.wait(V)
        s_kv.inc(V.tensor_copy(vext[:, 66 * j:66 * j + 64], ps[:, 3648:3712]))
    V.memset(knt[:, :], 0.0)
    s_vec.inc(V.engine_nop())
    for I in range(ns):
        s_mm.wait(S)
        s_kv.wait(S)
        s_sca.inc(S.activation(sqk[:, :], kT64[:, 512 * I:512 * I + 512], AF.Square))
        s_sca.wait(TE)
        s_vec.wait(TE)
        pslot = ps[0:1, 3584:4096]
        s_mm.inc(TE.matmul(pslot, ones64[:, :], sqk[:, :], start=True, stop=True))
        s_mm.wait(V)
        V.tensor_copy(arow[:, :], pslot)
        s_vec.inc(V.engine_nop())
        s_vec.wait(TE)
        s_mm.inc(TE.matmul(ps[:, 3072:3584], ones1b[:, :], arow[:, :],
                           start=True, stop=True))
        s_mm.wait(V)
        V.reduce_max(kn2t[:, :], ps[:, 3072:3584], axis=AX.X)
        vsp()
        s_vec.inc(V.tensor_max(knt[:, :], knt[:, :], kn2t[:, :]))
    knt_ready = s_vec.v

    out_marks = []
    gpar = 0
    psum_free = {0: 0, 1: 0}
    pbf_free = {0: 0, 1: 0}
    for I in range(ns):
        nblk = 4 * I + 4
        ngrp = (nblk + 1) // 2
        s_asm.wait(TE)
        qp_base = s_qp.v
        for h in range(H):
            pslot = ps[0:64, 3584:4096] if h % 2 == 0 else ps[0:64, 3072:3584]
            if I == 0 and h == 1:
                s_vec.wait(TE, knt_ready)
            if h >= 2:
                s_qp.wait(TE, qp_base + h - 1)
            for lc in range(4):
                mm = TE.matmul(pslot,
                               wq_bf[:, 1024 * lc + 64 * h:1024 * lc + 64 * h + 64],
                               latT[:, T * lc + 512 * I:T * lc + 512 * I + 512],
                               start=(lc == 0), stop=(lc == 3))
            s_mm.inc(mm)
            s_mm.wait(V)
            s_qp.inc(V.tensor_copy(q64[:, 512 * h:512 * h + 512], pslot))
        s_qp.wait(V)
        s_vec.inc(V.tensor_mul(sq64[:, :], q64[:, :], q64[:, :]))
        s_vec.wait(TE)
        for h in range(H):
            mm = TE.matmul(ps[0:1, 3072:3584], ones64[:, :],
                           sq64[:, 512 * h:512 * h + 512],
                           start=(h == 0), stop=(h == H - 1))
        s_mm.inc(mm)
        s_mm.wait(V)
        s_vec.inc(V.tensor_copy(arow[:, :], ps[0:1, 3072:3584]))
        s_vec.wait(TE)
        s_mm.inc(TE.matmul(ps[:, 3584:4096], ones1b[:, :], arow[:, :],
                           start=True, stop=True))
        s_mm.wait(V)
        V.reduce_max(amaxv[:, :], ps[:, 3584:4096], axis=AX.X)
        vsp()
        s_vec.inc(V.tensor_scalar(amaxv[:, :], amaxv[:, :], 1e-30, None, ALU.max))
        s_vec.wait(S)
        S.activation(amaxv[:, :], amaxv[:, :], AF.Ln, scale=knt[:, 0:1])
        s_sca.inc(S.activation(amaxv[:, :], amaxv[:, :], AF.Exp, scale=0.5))
        s_sca.wait(V)
        s_nb.inc(V.tensor_scalar(nbias[:, :], amaxv[:, :], -1.0, 35.0,
                                 ALU.mult, ALU.add))
        nb_ready = s_nb.v
        for h in range(H):
            exp_at = {}
            par = {}
            sc_at = {}

            def emit_pv(gx):
                s_exp.wait(TE, exp_at[gx])
                if gx == 0:
                    s_ye.wait(TE)
                j0v = 2 * gx
                jnv = min(2, nblk - j0v)
                last = None
                for jj in range(jnv):
                    j = j0v + jj
                    for ti in range(4):
                        if j <= 4 * I + ti:
                            last = TE.matmul(
                                ps[:, 2048 + 512 * ti:2048 + 512 * ti + 65],
                                pbf[par[gx]][:, 512 * jj + 128 * ti:
                                             512 * jj + 128 * ti + 128],
                                vext[:, 66 * j:66 * j + 65],
                                start=(j == 0), stop=(j == 4 * I + ti))
                s_pv.inc(last)
                pbf_free[par[gx]] = s_pv.v

            for g in range(ngrp):
                pp = gpar % 2
                gpar += 1
                par[g] = pp
                j0 = 2 * g
                jn = min(2, nblk - j0)
                s_exp.wait(TE, psum_free[pp])
                for jj in range(jn):
                    j = j0 + jj
                    s_sc.inc(TE.matmul(
                        ps[:, 1024 * pp + 512 * jj:1024 * pp + 512 * jj + 512],
                        kT64[:, 128 * j:128 * j + 128],
                        q64[:, 512 * h:512 * h + 512],
                        start=True, stop=True))
                sc_at[g] = s_sc.v
                nmsk = 0
                for jj in range(jn):
                    di = j0 + jj - 4 * I
                    if di >= 0:
                        if nmsk == 0:
                            s_sc.wait(V, sc_at[g])
                            s_bld.wait_mark(V, mask_mk)
                        pvw = ps[:, 1024 * pp + 512 * jj:
                                 1024 * pp + 512 * jj + 128 * (di + 1)]
                        s_msk.inc(V.tensor_tensor(
                            pvw, pvw,
                            mask4[:, 512 * di:512 * di + 128 * (di + 1)], ALU.add))
                        nmsk += 1
                if g >= 1:
                    emit_pv(g - 1)
                s_sc.wait(S, sc_at[g])
                if nmsk:
                    s_msk.wait(S)
                s_pv.wait(S, pbf_free[pp])
                s_nb.wait(S, nb_ready)
                s_exp.inc(S.activation(pbf[pp][:, 0:512 * jn],
                                       ps[:, 1024 * pp:1024 * pp + 512 * jn],
                                       AF.Exp, bias=nbias[:, 0:1]))
                exp_at[g] = s_exp.v
                psum_free[pp] = s_exp.v
            emit_pv(ngrp - 1)
            s_pv.wait(V)
            for ti in range(4):
                V.reciprocal(drb[:, ti:ti + 1],
                             ps[:, 2048 + 512 * ti + 64:2048 + 512 * ti + 65])
            vsp()
            for ti in range(4):
                s_ye.inc(V.tensor_scalar(
                    ybuf[:, D * ti + 64 * h:D * ti + 64 * h + 64],
                    ps[:, 2048 + 512 * ti:2048 + 512 * ti + 64],
                    drb[:, ti:ti + 1], None, ALU.mult))
        yb4 = s_ye4.v
        yb5 = s_ye5.v
        s_ye.wait(TE)
        for ti in range(4):
            if ti >= 1:
                s_ye4.wait(TE, yb4 + ti)
            for c in range(4):
                mm = TE.matmul(ps[:, 2048 + 128 * c:2048 + 128 * c + 128],
                               ybuf[:, D * ti + 128 * c:D * ti + 128 * c + 128],
                               idn[:, :], start=True, stop=True)
            s_ytp.inc(mm)
            if ti >= 1:
                s_ye5.wait(TE, yb5 + ti)
            for c in range(4, 8):
                mm = TE.matmul(ps[:, 2560 + 128 * (c - 4):2560 + 128 * (c - 4) + 128],
                               ybuf[:, D * ti + 128 * c:D * ti + 128 * c + 128],
                               idn[:, :], start=True, stop=True)
            s_ytp.inc(mm)
            s_ytp.wait(V, s_ytp.v - 1)
            if I > 0 and ti == 0:
                s_pr.wait(V)
            for c in range(4):
                cp = V.tensor_copy(yT[:, 512 * c + 128 * ti:512 * c + 128 * ti + 128],
                                   ps[:, 2048 + 128 * c:2048 + 128 * c + 128])
            s_ye4.inc(cp)
            s_ytp.wait(S)
            if I > 0 and ti == 0:
                s_pr.wait(S)
            for c in range(4, 8):
                cp = S.activation(yT[:, 512 * c + 128 * ti:512 * c + 128 * ti + 128],
                                  ps[:, 2560 + 128 * (c - 4):2560 + 128 * (c - 4) + 128],
                                  AF.Copy)
            s_ye5.inc(cp)
        s_ye4.wait(TE)
        s_ye5.wait(TE)
        for ti in range(4):
            i = 4 * I + ti
            b = xb[i % 2]
            fb = fbk[i % 2]
            s_xfr.wait(SY, s_xfr.v - (1 if i > 0 else 0))
            s_xld.begin(SY)
            s_xld.issue(SY.dma_start(b[:, :], x_h[i * 128:(i + 1) * 128, :]))
            asm_xmk = s_xld.mark()
            if i >= 2:
                s_asm.wait(G, s_asm.v - 2)
            s_fb.begin(G)
            s_fb.issue(G.dma_start(fb[:, :], spill_h[i * 128:(i + 1) * 128, :]))
            fb_mk = s_fb.mark()
            for half in range(2):
                pslot = ps[:, 3072 + 512 * half:3072 + 512 * half + 512]
                if not (I == 0 and ti == 0):
                    s_asm.wait(TE)
                for c in range(8):
                    mm = TE.matmul(
                        pslot, yT[:, 512 * c + 128 * ti:512 * c + 128 * ti + 128],
                        wpr_bf[:, 1024 * c + 512 * half:1024 * c + 512 * half + 512],
                        start=(c == 0), stop=(c == 7))
                s_pr.inc(mm)
            o = outst[i % 2]
            s_pr.wait(V)
            s_xld.wait_mark(V, asm_xmk)
            s_fb.wait_mark(V, fb_mk)
            if i >= 2:
                s_out.wait_mark(V, out_marks[i - 2])
            for half in range(2):
                pslot = ps[:, 3072 + 512 * half:3072 + 512 * half + 512]
                sl = slice(512 * half, 512 * half + 512)
                V.tensor_add(o[:, sl], pslot, b[:, sl])
                V.scalar_tensor_tensor(o[:, sl], fb[:, sl], stc(11)[:, i:i + 1],
                                       o[:, sl], ALU.mult, ALU.add)
                s_asm.inc(V.tensor_add(o[:, sl], o[:, sl], b2rep[:, sl]))
            s_xfr.inc(V.engine_nop())
            s_asm.wait(G)
            s_out.begin(G)
            s_out.issue(G.dma_start(out_h[i * 128:(i + 1) * 128, :], o[:, :]))
            out_marks.append(s_out.mark())

    s_out.wait_all(G)
    st.close()
    return nc


def _mask4_np():
    m = np.zeros((128, 4 * 512), dtype=np.float32)
    r = np.arange(128)
    for di in range(4):
        blk = m[:, 512 * di:512 * di + 512]
        blk[:, 0:128 * di] = NEG
        tri = blk[:, 128 * di:128 * di + 128]
        tri[r[:, None] > r[None, :]] = NEG
    return m


def kernel(L0_attn_x, L0_attn_Wc_down, L0_attn_Wq_up, L0_attn_Wkv_up,
           L0_attn_Wc_proj, L0_ffn_w1, L0_ffn_b1, L0_ffn_w2, L0_ffn_b2):
    T = 4096
    nc = build_nc(T)
    x = np.ascontiguousarray(np.asarray(L0_attn_x, dtype=np.float32))
    shared = {
        "wcd": np.ascontiguousarray(np.asarray(L0_attn_Wc_down, dtype=np.float32)),
        "wq": np.ascontiguousarray(np.asarray(L0_attn_Wq_up, dtype=np.float32)),
        "wkv": np.ascontiguousarray(np.asarray(L0_attn_Wkv_up, dtype=np.float32)),
        "wpr": np.ascontiguousarray(np.asarray(L0_attn_Wc_proj, dtype=np.float32)),
        "w1": np.ascontiguousarray(np.asarray(L0_ffn_w1, dtype=np.float32)),
        "b1": np.ascontiguousarray(np.asarray(L0_ffn_b1, dtype=np.float32)).reshape(1, -1),
        "w2": np.ascontiguousarray(np.asarray(L0_ffn_w2, dtype=np.float32)),
        "b2": np.ascontiguousarray(np.asarray(L0_ffn_b2, dtype=np.float32)).reshape(1, -1),
        "msk": _mask4_np(),
        "idn": np.eye(128, dtype=ml_dtypes.bfloat16),
    }
    in_maps = [dict(shared, x=x[c]) for c in range(8)]
    res = run_bass_kernel_spmd(nc, in_maps, core_ids=list(range(8)))
    return np.stack([np.asarray(res.results[c]["out"]) for c in range(8)], axis=0)


# version 15
# speedup vs baseline: 1.7054x; 1.0036x over previous
import numpy as np
import ml_dtypes
from contextlib import ExitStack

import concourse.bass as bass
import concourse.mybir as mybir
from concourse.bass_utils import run_bass_kernel_spmd

F32 = mybir.dt.float32
BF16 = mybir.dt.bfloat16
AF = mybir.ActivationFunctionType
ALU = mybir.AluOpType
AX = mybir.AxisListType

MAGIC = 12582912.0
NEG = -1.0e30
D = 1024
L = 512
FF = 4096
H = 16
DH = 64
EPS = 1e-5


def build_nc(T=4096):
    nc = bass.Bass("TRN2", target_bir_lowering=False, detect_race_conditions=False)
    nt = T // 128
    ns = T // 512

    x_h = nc.dram_tensor("x", [T, D], F32, kind="ExternalInput")
    wcd_h = nc.dram_tensor("wcd", [D, L], F32, kind="ExternalInput")
    wq_h = nc.dram_tensor("wq", [L, D], F32, kind="ExternalInput")
    wkv_h = nc.dram_tensor("wkv", [L, 2 * DH], F32, kind="ExternalInput")
    wpr_h = nc.dram_tensor("wpr", [D, D], F32, kind="ExternalInput")
    w1_h = nc.dram_tensor("w1", [FF, D], F32, kind="ExternalInput")
    b1_h = nc.dram_tensor("b1", [1, FF], F32, kind="ExternalInput")
    w2_h = nc.dram_tensor("w2", [D, FF], F32, kind="ExternalInput")
    b2_h = nc.dram_tensor("b2", [1, D], F32, kind="ExternalInput")
    msk_h = nc.dram_tensor("msk", [128, 4 * 512], F32, kind="ExternalInput")
    idn_h = nc.dram_tensor("idn", [128, 128], BF16, kind="ExternalInput")
    out_h = nc.dram_tensor("out", [T, D], F32, kind="ExternalOutput")
    spill_h = nc.dram_tensor("ffn_spill", [T, D], BF16, kind="ExternalOutput")

    st = ExitStack()
    sb = lambda name, shape, dt: st.enter_context(nc.sbuf_tensor(name, shape, dt))
    sem = lambda name: st.enter_context(nc.semaphore(name))

    xb = [sb("xb0", [128, D], F32), sb("xb1", [128, D], F32)]
    b2rep = sb("b2rep", [128, D], BF16)
    ones1f = sb("ones1f", [1, 128], F32)
    ones1b = sb("ones1b", [1, 128], BF16)
    onesff = sb("onesff", [128, 128], F32)
    ones64 = sb("ones64", [64, 1], BF16)
    idn = sb("idn_sb", [128, 128], BF16)
    stt = sb("stt", [128, 16 * nt], F32)

    def stc(i):
        return stt[:, i * nt:(i + 1) * nt]

    w1s = sb("w1s", [128, 1], F32)
    mw1s = sb("mw1s", [128, 1], F32)
    w2s = sb("w2s", [128, 1], F32)
    mw2s = sb("mw2s", [128, 1], F32)
    wabs = sb("wabs", [128, 40], F32)
    brow = sb("brow", [1, 512], F32)
    tmp1 = sb("tmp1", [128, 1], F32)
    tmp2 = sb("tmp2", [128, 1], F32)
    tmp3 = sb("tmp3", [128, 128], F32)

    ps = st.enter_context(nc.psum_tensor("ps", [128, 4096], F32))

    SY, V, S, TE, G = nc.sync, nc.vector, nc.scalar, nc.tensor, nc.gpsimd

    class Ctr:
        def __init__(self, name):
            self.h = sem(name)
            self.v = 0

        def inc(self, inst, n=1):
            inst.then_inc(self.h, n)
            self.v += n

        def wait(self, eng, val=None):
            val = self.v if val is None else val
            if val > 0:
                eng.wait_ge(self.h, val)

    class DCtr:

        def __init__(self, name):
            self.h = [sem(name + "a"), sem(name + "b")]
            self.v = [0, 0]
            self.b = 0
            self.cur = None

        def __init_serial__(self):
            pass

        def begin(self, eng):
            p = self.b % 2
            self.b += 1
            self.cur = p
            if getattr(self, "serial", False):
                for pp in (0, 1):
                    if self.v[pp] > 0:
                        eng.wait_ge(self.h[pp], self.v[pp])
            elif self.v[p] > 0:
                eng.wait_ge(self.h[p], self.v[p])
            return p

        def issue(self, inst):
            p = self.cur
            inst.then_inc(self.h[p], 16)
            self.v[p] += 16

        def mark(self):
            return (self.cur, self.v[self.cur])

        def wait_mark(self, eng, mk):
            if mk is not None:
                eng.wait_ge(self.h[mk[0]], mk[1])

        def wait_all(self, eng):
            for p in (0, 1):
                if self.v[p] > 0:
                    eng.wait_ge(self.h[p], self.v[p])

    s_xld = DCtr("s_xld")
    s_xfr = Ctr("s_xfr")
    s_xfs = Ctr("s_xfs")
    s_bld = DCtr("s_bld")
    s_mm = Ctr("s_mm")
    s_vec = Ctr("s_vec")
    s_sca = Ctr("s_sca")
    s_wld = DCtr("s_wld")
    s_wfr = Ctr("s_wfr")
    s_tp = DCtr("s_tp")
    s_tpm = Ctr("s_tpm")
    s_he6 = Ctr("s_he6")
    s_he7 = Ctr("s_he7")
    s_wtp = Ctr("s_wtp")
    s_we6 = Ctr("s_we6")
    s_we7 = Ctr("s_we7")
    s_xtp = Ctr("s_xtp")
    s_xe4 = Ctr("s_xe4")
    s_xe5 = Ctr("s_xe5")
    s_ytp = Ctr("s_ytp")
    s_ye4 = Ctr("s_ye4")
    s_ye5 = Ctr("s_ye5")
    s_ff1 = Ctr("s_ff1")
    s_hev = Ctr("s_hev")
    s_ff2 = Ctr("s_ff2")
    s_f2e = Ctr("s_f2e")
    s_sp = DCtr("s_sp")
    s_q1 = Ctr("s_q1")
    s_q2 = Ctr("s_q2")
    s_g = Ctr("s_g")
    s_lat = Ctr("s_lat")
    s_sc = Ctr("s_sc")
    s_msk = Ctr("s_msk")
    s_exp = Ctr("s_exp")
    s_pv = Ctr("s_pv")
    s_ye = Ctr("s_ye")
    s_qp = Ctr("s_qp")
    s_nb = Ctr("s_nb")
    s_out = DCtr("s_out")
    s_fb = DCtr("s_fb")
    s_pr = Ctr("s_pr")
    s_asm = Ctr("s_asm")
    s_kv = Ctr("s_kv")
    s_wc = DCtr("s_wc")

    def vsp():
        V.tensor_scalar(tmp3[:, :], tmp3[:, :], 1.0, None, ALU.mult)

    s_bld.begin(SY)
    s_bld.issue(SY.dma_start(idn[:, :], idn_h[:, :]))
    idn_mk = s_bld.mark()
    V.memset(tmp3[:, :], 0.0)
    V.memset(tmp1[:, :], 0.0)
    V.memset(ones1f[:, :], 1.0)
    V.memset(ones1b[:, :], 1.0)
    V.memset(onesff[:, :], 1.0)
    s_vec.inc(V.memset(ones64[:, :], 1.0))

    p1st = ExitStack()
    sb1 = lambda name, shape, dt: p1st.enter_context(nc.sbuf_tensor(name, shape, dt))
    t1b = sb1("t1b", [128, D], F32)
    b1rep = sb1("b1rep", [128, FF], BF16)
    w1qT = sb1("w1qT", [128, 8 * FF], BF16)
    w2qT = sb1("w2qT", [128, 32 * D], BF16)
    p0wst = ExitStack()
    big_a = p0wst.enter_context(nc.sbuf_tensor("big_a", [128, FF], F32))
    wq_sb = p0wst.enter_context(nc.sbuf_tensor("wqs0", [128, 1024], BF16))

    for i in range(nt):
        b = xb[i % 2]
        if i >= 2:
            s_xfr.wait(SY, i - 1)
            s_xfs.wait(SY, i - 1)
        s_xld.begin(SY)
        s_xld.issue(SY.dma_start(b[:, :], x_h[i * 128:(i + 1) * 128, :]))
        xmk = s_xld.mark()
        s_xld.wait_mark(V, xmk)
        s_xfr.inc(V.reduce_sum(stc(0)[:, i:i + 1], b[:, :], axis=AX.X))
        V.reduce_max(stc(2)[:, i:i + 1], b[:, :], axis=AX.X)
        V.tensor_reduce(stc(3)[:, i:i + 1], b[:, :], axis=AX.X, op=ALU.min)
        s_xld.wait_mark(S, xmk)
        s_xfs.inc(S.activation(t1b[:, :], b[:, :], AF.Square,
                               accum_out=stc(1)[:, i:i + 1]))
        wslot = big_a[:, (i % 4) * D:(i % 4) * D + D]
        s_wfr.wait(G, s_wfr.v - 3)
        s_wld.begin(G)
        s_wld.issue(G.dma_start(wslot, w1_h[i * 128:(i + 1) * 128, :]))
        s_wld.wait_mark(S, s_wld.mark())
        s_wfr.inc(S.activation(t1b[:, :], wslot, AF.Abs,
                               accum_out=wabs[:, i:i + 1]))

    s_xld.wait_all(V)
    s_xfr.wait(V, nt)
    s_xfs.wait(V, nt)
    inv_d = 1.0 / D
    m = stc(12)
    V.tensor_scalar(m, stc(0), inv_d, None, ALU.mult)
    V.tensor_scalar(stc(13), stc(1), inv_d, None, ALU.mult)
    V.tensor_mul(stc(14), m, m)
    vsp()
    V.tensor_sub(stc(1), stc(13), stc(14))
    vsp()
    s_vec.inc(V.tensor_scalar(stc(15), stc(1), EPS, None, ALU.add))
    s_vec.wait(S)
    s_sca.inc(S.activation(stc(14), stc(15), AF.Ln))
    s_sca.wait(V)
    s_vec.inc(V.tensor_scalar(stc(14), stc(14), -0.5, None, ALU.mult))
    s_vec.wait(S)
    s_sca.inc(S.activation(stc(4), stc(14), AF.Exp))
    s_sca.wait(V)
    V.scalar_tensor_tensor(stc(5), stc(4), -1.0, m, ALU.mult, ALU.mult)
    V.tensor_sub(stc(2), stc(2), m)
    V.tensor_sub(stc(3), m, stc(3))
    vsp()
    V.tensor_max(stc(2), stc(2), stc(3))
    vsp()
    V.tensor_mul(stc(2), stc(2), stc(4))
    V.tensor_mul(stc(14), stc(4), stc(4))
    vsp()
    V.tensor_mul(stc(14), stc(1), stc(14))
    vsp()
    s_vec.inc(V.tensor_scalar(stc(14), stc(14), EPS, None, ALU.add))
    s_vec.wait(S)
    s_sca.inc(S.activation(stc(15), stc(14), AF.Ln))
    s_sca.wait(V)
    s_vec.inc(V.tensor_scalar(stc(15), stc(15), -0.5, None, ALU.mult))
    s_vec.wait(S)
    s_sca.inc(S.activation(stc(3), stc(15), AF.Exp))
    s_sca.wait(V)
    V.tensor_mul(stc(2), stc(2), stc(3))
    vsp()
    s_vec.inc(V.tensor_scalar(stc(2), stc(2), 1e-5, None, ALU.max))
    s_vec.wait(S)
    s_sca.inc(S.activation(stc(15), stc(2), AF.Ln))
    s_sca.wait(V)
    s_vec.inc(V.tensor_scalar(stc(15), stc(15), -1.0, None, ALU.mult))
    s_vec.wait(S)
    s_sca.inc(S.activation(stc(14), stc(15), AF.Exp))
    s_sca.wait(V)
    V.tensor_scalar(stc(14), stc(14), 127.0, None, ALU.mult)
    V.tensor_mul(stc(6), stc(4), stc(3))
    V.tensor_mul(stc(7), stc(5), stc(3))
    V.tensor_mul(stc(6), stc(6), stc(14))
    s_vec.inc(V.tensor_mul(stc(7), stc(7), stc(14)))


    s_vec.wait(TE)
    for c in range(FF // 512 + D // 512):
        if c < FF // 512:
            src_ap = b1_h[0:1, c * 512:(c + 1) * 512]
            dst = b1rep[:, c * 512:(c + 1) * 512]
        else:
            c2 = c - FF // 512
            src_ap = b2_h[0:1, c2 * 512:(c2 + 1) * 512]
            dst = b2rep[:, c2 * 512:(c2 + 1) * 512]
        pslot = ps[:, (c % 2) * 512:(c % 2) * 512 + 512]
        if c >= 1:
            s_mm.wait(SY)
        s_bld.begin(SY)
        s_bld.issue(SY.dma_start(brow[:, :], src_ap))
        s_bld.wait_mark(TE, s_bld.mark())
        if c >= 2:
            s_vec.wait(TE, s_vec.v - 1)
        s_mm.inc(TE.matmul(pslot, ones1f[:, :], brow[:, :], start=True, stop=True))
        s_mm.wait(V)
        s_vec.inc(V.tensor_copy(dst, pslot))


    def w_pass1(w_h, cols, nstr, off):
        for sx in range(nstr):
            b = big_a
            s_wfr.wait(G)
            s_wld.begin(G)
            s_wld.issue(G.dma_start(b[:, 0:cols], w_h[sx * 128:(sx + 1) * 128, :]))
            s_wld.wait_mark(V, s_wld.mark())
            s_wfr.inc(V.reduce_sum(wabs[:, off + sx:off + sx + 1], b[:, 0:cols],
                                   axis=AX.X, apply_absolute_value=True))

    w_pass1(w2_h, FF, D // 128, 32)
    s_wfr.wait(V)
    s_vec.inc(V.engine_nop())
    s_vec.wait(TE)
    s_wfr.wait(TE)
    s_mm.inc(TE.matmul(ps[:, 2048:2088], onesff[:, :], wabs[:, 0:40],
                       start=True, stop=True))
    s_mm.wait(V)
    V.reduce_sum(w1s[:, :], ps[:, 2048:2080], axis=AX.X)
    V.reduce_sum(w2s[:, :], ps[:, 2080:2088], axis=AX.X)
    for wsx, mwx, nel in ((w1s, mw1s, FF * D), (w2s, mw2s, D * FF)):
        V.tensor_scalar(tmp2[:, :], wsx[:, :], 1.0 / nel, None, ALU.mult)
        vsp()
        s_vec.inc(V.tensor_scalar(tmp2[:, :], tmp2[:, :], 1e-5, None, ALU.max))
        s_vec.wait(S)
        s_sca.inc(S.activation(tmp1[:, :], tmp2[:, :], AF.Copy))
        s_sca.wait(V)
        V.reciprocal(wsx[:, :], tmp1[:, :])
        vsp()
        s_vec.inc(V.tensor_scalar(mwx[:, :], tmp2[:, :], 1.0 / 127.0, None, ALU.mult))
    s_vec.inc(V.engine_nop())
    V.tensor_scalar(stc(8), stc(2), mw1s[:, 0:1], None, ALU.mult)

    wqc = {"n": 0}

    def w_pass2(w_h, cols, nstr, s_rec, qT, chunk_cols):
        q = wq_sb
        for sx in range(nstr):
            b = big_a
            s_wfr.wait(G)
            s_wld.begin(G)
            s_wld.issue(G.dma_start(b[:, 0:cols], w_h[sx * 128:(sx + 1) * 128, :]))
            s_wld.wait_mark(V, s_wld.mark())
            V.tensor_scalar(b[:, 0:cols], b[:, 0:cols], s_rec[:, 0:1], MAGIC,
                            ALU.mult, ALU.add)
            V.tensor_scalar(b[:, 0:cols], b[:, 0:cols], MAGIC, 1.0,
                            ALU.subtract, ALU.min)
            for qc in range(cols // 1024):
                n = wqc["n"]
                s_wtp.wait(V)
                s_vec.inc(V.tensor_scalar(q[:, :], b[:, 1024 * qc:1024 * qc + 1024],
                                           -1.0, None, ALU.max))
                if qc == cols // 1024 - 1:
                    s_wfr.inc(V.engine_nop())
                s_vec.wait(TE)
                if n == 0:
                    s_bld.wait_mark(TE, idn_mk)
                if n >= 1:
                    s_we6.wait(TE, n)
                for k in range(4):
                    mm = TE.matmul(ps[:, 3072 + 128 * k:3072 + 128 * k + 128],
                                   q[:, 128 * k:128 * k + 128], idn[:, :],
                                   start=True, stop=True)
                s_wtp.inc(mm)
                if n >= 1:
                    s_we7.wait(TE, n)
                for k in range(4, 8):
                    mm = TE.matmul(ps[:, 3584 + 128 * (k - 4):3584 + 128 * (k - 4) + 128],
                                   q[:, 128 * k:128 * k + 128], idn[:, :],
                                   start=True, stop=True)
                s_wtp.inc(mm)
                s_wtp.wait(S, 2 * n + 1)
                for k in range(4):
                    kk = 8 * qc + k
                    act = S.activation(
                        qT[:, chunk_cols * kk + 128 * sx:chunk_cols * kk + 128 * sx + 128],
                        ps[:, 3072 + 128 * k:3072 + 128 * k + 128], AF.Copy)
                s_we6.inc(act)
                s_wtp.wait(S, 2 * n + 2)
                for k in range(4, 8):
                    kk = 8 * qc + k
                    act = S.activation(
                        qT[:, chunk_cols * kk + 128 * sx:chunk_cols * kk + 128 * sx + 128],
                        ps[:, 3584 + 128 * (k - 4):3584 + 128 * (k - 4) + 128], AF.Copy)
                s_we7.inc(act)
                wqc["n"] = n + 1

    w_pass2(w1_h, D, FF // 128, w1s, w1qT, FF)
    w_pass2(w2_h, FF, D // 128, w2s, w2qT, D)

    p0wst.close()
    hg = [sb1("hg0", [128, FF], BF16), sb1("hg1", [128, FF], BF16)]
    xq1 = sb1("xq1", [128, D], BF16)
    xq1T = [sb1("xq1T0", [128, D], BF16), sb1("xq1T1", [128, D], BF16)]
    hq = [sb1("hq0", [128, 1024], BF16), sb1("hq1", [128, 1024], BF16)]
    hqT = sb1("hqT", [128, FF], BF16)
    qtmp = sb1("qtmp", [128, 1024], F32)
    spst = [sb1("spst0", [128, D], BF16), sb1("spst1", [128, D], BF16)]
    sp_marks = []
    hev_mark = {}
    ff2_mark = {}
    xq1_mks = {}
    xld_mks = {}
    sq_mark = {}

    def p1_loadX(i):
        b = xb[i % 2]
        if i == 0:
            s_xfr.wait(SY)
            s_xfs.wait(SY)
            p1_loadX.base = s_xfr.v
        elif i >= 2:
            s_xfr.wait(SY, p1_loadX.base + i - 1)
        s_xld.begin(SY)
        s_xld.issue(SY.dma_start(b[:, :], x_h[i * 128:(i + 1) * 128, :]))
        xld_mks[i] = s_xld.mark()

    def p1_prepX(i):
        b = xb[i % 2]
        s_xld.wait_mark(V, xld_mks[i])
        s_xfr.inc(V.tensor_scalar(t1b[:, :], b[:, :], stc(6)[:, i:i + 1],
                                  stc(7)[:, i:i + 1], ALU.mult, ALU.add))
        if i > 0:
            s_tp.wait_mark(V, xq1_mks[i - 1])
        else:
            s_wtp.wait(V)
        s_q1.inc(V.tensor_scalar(xq1[:, :], t1b[:, :], MAGIC, MAGIC,
                                 ALU.add, ALU.subtract))
        s_q1.wait(SY)
        if i > 1:
            s_ff1.wait(SY, 8 * (i - 1))
        elif i == 0:
            s_wtp.wait(SY)
            s_wfr.wait(SY)
        s_tp.begin(SY)
        for k in range(8):
            s_tp.issue(SY.dma_start(xq1T[i % 2][:, 128 * k:128 * k + 128],
                                    xq1[:, 128 * k:128 * k + 128], transpose=True))
        xq1_mks[i] = s_tp.mark()

    def p1_ffn1(i):
        if i == 0:
            s_we6.wait(TE)
            s_we7.wait(TE)
            s_wfr.wait(V)
        s_tp.wait_mark(TE, xq1_mks[i])
        for fc in range(8):
            pslot = ps[:, (fc % 4) * 512:(fc % 4) * 512 + 512]
            s_hev.wait(TE, s_hev.v - 3)
            for dc in range(8):
                mm = TE.matmul(pslot, xq1T[i % 2][:, 128 * dc:128 * dc + 128],
                               w1qT[:, FF * dc + 512 * fc:FF * dc + 512 * fc + 512],
                               start=(dc == 0), stop=(dc == 7))
            s_ff1.inc(mm)
            s_ff1.wait(V)
            if fc == 0 and i >= 2:
                s_sca.wait(V, sq_mark[i - 2])
            s_hev.inc(V.scalar_tensor_tensor(
                hg[i % 2][:, 512 * fc:512 * fc + 512], pslot, stc(8)[:, i:i + 1],
                b1rep[:, 512 * fc:512 * fc + 512], ALU.mult, ALU.add))
        hev_mark[i] = s_hev.v

    def p1_post_head(i):
        h = hg[i % 2]
        g = hqT
        s_hev.wait(S, hev_mark[i])
        if i > 0:
            s_ff2.wait(S, ff2_mark[i - 1])
        s_g.inc(S.activation(g[:, :], h[:, :], AF.Gelu))
        s_g.wait(V)
        s_vec.inc(V.reduce_max(stc(13)[:, i:i + 1], g[:, :], axis=AX.X,
                               apply_absolute_value=True))
        s_g.wait(S)
        s_sca.inc(S.activation(h[:, :], g[:, :], AF.Square,
                               accum_out=stc(9)[:, i:i + 1]))
        sq_mark[i] = s_sca.v
        s_vec.wait(S)
        s_sca.inc(S.activation(stc(15)[:, i:i + 1], stc(13)[:, i:i + 1], AF.Copy))
        s_sca.wait(V)
        V.reciprocal(stc(14)[:, i:i + 1], stc(15)[:, i:i + 1])
        V.tensor_scalar(stc(10)[:, i:i + 1], stc(13)[:, i:i + 1], mw2s[:, 0:1],
                        None, ALU.mult)
        vsp()
        V.tensor_scalar(stc(14)[:, i:i + 1], stc(14)[:, i:i + 1],
                        127.0, None, ALU.mult)
        vsp()
        s_vec.inc(V.engine_nop())
        for kc in (0, 1):
            ck = i * 4 + kc
            pk = ck - 2
            if pk >= 0:
                s_tpm.wait(V, 2 * (pk + 1))
            V.tensor_scalar(qtmp[:, :], g[:, 1024 * kc:1024 * kc + 1024],
                            stc(14)[:, i:i + 1], MAGIC, ALU.mult, ALU.add)
            s_q2.inc(V.tensor_scalar(hq[kc % 2][:, :], qtmp[:, :], MAGIC, None,
                                     ALU.subtract))

    def p1_post_ffn2(i):
        g = hqT
        for kc in range(4):
            ck = i * 4 + kc
            pk = ck - 2
            if kc >= 2:
                s_tpm.wait(V, 2 * (pk + 1))
                V.tensor_scalar(qtmp[:, :], g[:, 1024 * kc:1024 * kc + 1024],
                                stc(14)[:, i:i + 1], MAGIC, ALU.mult, ALU.add)
                s_q2.inc(V.tensor_scalar(hq[kc % 2][:, :], qtmp[:, :], MAGIC,
                                         None, ALU.subtract))
            s_q2.wait(TE)
            if ck == 0:
                s_bld.wait_mark(TE, idn_mk)
            if ck >= 1:
                s_he6.wait(TE, ck)
            for k in range(4):
                mm = TE.matmul(ps[:, 3072 + 128 * k:3072 + 128 * k + 128],
                               hq[kc % 2][:, 128 * k:128 * k + 128],
                               idn[:, :], start=True, stop=True)
            s_tpm.inc(mm)
            if ck >= 1:
                s_he7.wait(TE, ck)
            for k in range(4, 8):
                mm = TE.matmul(ps[:, 3584 + 128 * (k - 4):3584 + 128 * (k - 4) + 128],
                               hq[kc % 2][:, 128 * k:128 * k + 128],
                               idn[:, :], start=True, stop=True)
            s_tpm.inc(mm)
            s_tpm.wait(V, 2 * ck + 1)
            s_he6.inc(V.tensor_copy(hqT[:, 1024 * kc:1024 * kc + 512],
                                    ps[:, 3072:3584]))
            s_tpm.wait(S, 2 * ck + 2)
            s_he7.inc(S.activation(hqT[:, 1024 * kc + 512:1024 * kc + 1024],
                                   ps[:, 3584:4096], AF.Copy))
            s_he6.wait(TE, 4 * i + kc + 1)
            s_he7.wait(TE, 4 * i + kc + 1)
            if kc == 0:
                s_f2e.wait(TE)
            for dc2 in range(2):
                pslot = ps[:, 2048 + dc2 * 512:2048 + dc2 * 512 + 512]
                for k8 in range(8):
                    fc = 8 * kc + k8
                    mm2 = TE.matmul(
                        pslot, hqT[:, 128 * fc:128 * fc + 128],
                        w2qT[:, D * fc + 512 * dc2:D * fc + 512 * dc2 + 512],
                        start=(fc == 0), stop=(fc == 31))
                if kc == 3:
                    s_ff2.inc(mm2)
        sp = spst[i % 2]
        s_ff2.wait(V)
        if i >= 2:
            s_sp.wait_mark(V, sp_marks[i - 2])
        for dc2 in range(2):
            pslot = ps[:, 2048 + dc2 * 512:2048 + dc2 * 512 + 512]
            s_f2e.inc(V.tensor_scalar(sp[:, 512 * dc2:512 * dc2 + 512], pslot,
                                      stc(10)[:, i:i + 1], None, ALU.mult))
        ff2_mark[i] = s_ff2.v
        s_f2e.wait(G)
        s_sp.begin(G)
        s_sp.issue(G.dma_start(spill_h[i * 128:(i + 1) * 128, :], sp[:, :]))
        sp_marks.append(s_sp.mark())

    for i in range(nt + 1):
        if i < nt:
            if i == 0:
                p1_loadX(0)
                p1_loadX(1)
                p1_prepX(0)
            if i + 2 < nt:
                p1_loadX(i + 2)
            if i + 1 < nt:
                p1_prepX(i + 1)
            if i >= 1:
                p1_post_head(i - 1)
            p1_ffn1(i)
        if i >= 1:
            if i == nt:
                p1_post_head(i - 1)
            p1_post_ffn2(i - 1)

    p1st.close()
    for eng in (G, SY, V, S, TE):
        s_sp.wait_all(eng)
    V.tensor_scalar(stc(15), stc(9), 1.0 / FF, EPS, ALU.mult, ALU.add)
    s_vec.inc(V.engine_nop())
    s_vec.wait(S)
    S.activation(stc(15), stc(15), AF.Ln)
    s_sca.inc(S.activation(stc(15), stc(15), AF.Exp, scale=0.5))
    s_sca.wait(V)
    s_vec.inc(V.reciprocal(stc(11), stc(15)))

    wcd_bf = sb("wcd_bf", [128, 8 * 512], BF16)
    wq_bf = sb("wq_bf", [128, 4 * 1024], BF16)
    wkv_bf = sb("wkv_bf", [128, 4 * 128], BF16)
    wpr_bf = sb("wpr_bf", [128, 8 * 1024], BF16)
    wtmp = sb("wtmp", [128, 1024], F32)

    def load_cast(src_h, nchunk, ccols, dst, scale=None):
        for c in range(nchunk):
            s_vec.wait(G)
            s_wc.begin(G)
            s_wc.issue(G.dma_start(wtmp[:, 0:ccols], src_h[c * 128:(c + 1) * 128, :]))
            s_wc.wait_mark(V, s_wc.mark())
            if scale is None:
                s_vec.inc(V.tensor_copy(dst[:, c * ccols:(c + 1) * ccols],
                                        wtmp[:, 0:ccols]))
            else:
                s_vec.inc(V.tensor_scalar(dst[:, c * ccols:(c + 1) * ccols],
                                          wtmp[:, 0:ccols], scale, None, ALU.mult))

    load_cast(wcd_h, 8, 512, wcd_bf)
    load_cast(wq_h, 4, 1024, wq_bf, scale=0.125)
    load_cast(wkv_h, 4, 128, wkv_bf)
    load_cast(wpr_h, 8, 1024, wpr_bf)
    wts_ready = s_vec.v
    mask4 = sb("mask4", [128, 2048], F32)
    s_bld.begin(SY)
    s_bld.issue(SY.dma_start(mask4[:, :], msk_h[:, :]))
    mask_mk = s_bld.mark()

    outst = [sb("outst0", [128, D], F32), sb("outst1", [128, D], F32)]
    fbk = [sb("fbk0", [128, D], BF16), sb("fbk1", [128, D], BF16)]
    kT64 = sb("kT64", [64, T], BF16)
    vext = sb("vext", [128, 66 * nt], BF16)
    latT = sb("latT", [128, 4 * T], BF16)
    xnbf = sb("xnbf", [128, D], BF16)
    xnT = sb("xnT", [128, 8 * 512], BF16)
    q64 = sb("q64", [64, 16 * 512], BF16)
    sq64 = sb("sq64", [64, 16 * 512], BF16)
    arow = sb("arow", [1, 512], BF16)
    amaxv = sb("amaxv", [128, 1], F32)
    nbias = sb("nbias", [128, 1], F32)
    knt = sb("knt", [128, 1], F32)
    kn2t = sb("kn2t", [128, 1], F32)
    sqk = sb("sqk", [64, 512], BF16)
    pbf = [sb("pbf0", [128, 1024], BF16), sb("pbf1", [128, 1024], BF16)]
    ybuf = sb("ybuf", [128, 4 * D], BF16)
    yT = sb("yT", [128, 8 * 512], BF16)
    drb = sb("drb", [128, 4], F32)

    s_vec.inc(V.memset(vext[:, :], 1.0))
    vext_ones = s_vec.v

    lat_marks = []
    for I in range(ns):
        for t4 in range(4):
            i = 4 * I + t4
            b = xb[i % 2]
            s_xfr.wait(SY, s_xfr.v - (1 if i > 0 else 0))
            s_xld.begin(SY)
            s_xld.issue(SY.dma_start(b[:, :], x_h[i * 128:(i + 1) * 128, :]))
            s_xld.wait_mark(V, s_xld.mark())
            if i > 0:
                s_xtp.wait(V, 2 * i)
            s_xfr.inc(V.tensor_scalar(xnbf[:, :], b[:, :], stc(4)[:, i:i + 1],
                                      stc(5)[:, i:i + 1], ALU.mult, ALU.add))
            s_xfr.wait(TE)
            if i >= 1:
                s_xe4.wait(TE, i)
            for k in range(4):
                mm = TE.matmul(ps[:, 2048 + 128 * k:2048 + 128 * k + 128],
                               xnbf[:, 128 * k:128 * k + 128], idn[:, :],
                               start=True, stop=True)
            s_xtp.inc(mm)
            if i >= 1:
                s_xe5.wait(TE, i)
            for k in range(4, 8):
                mm = TE.matmul(ps[:, 2560 + 128 * (k - 4):2560 + 128 * (k - 4) + 128],
                               xnbf[:, 128 * k:128 * k + 128], idn[:, :],
                               start=True, stop=True)
            s_xtp.inc(mm)
            s_xtp.wait(V, 2 * i + 1)
            if I > 0 and t4 == 0:
                s_mm.wait(V, lat_marks[-1])
            for k in range(4):
                cp = V.tensor_copy(xnT[:, 512 * k + 128 * t4:512 * k + 128 * t4 + 128],
                                   ps[:, 2048 + 128 * k:2048 + 128 * k + 128])
            s_xe4.inc(cp)
            s_xtp.wait(S, 2 * i + 2)
            if I > 0 and t4 == 0:
                s_mm.wait(S, lat_marks[-1])
            for k in range(4, 8):
                cp = S.activation(xnT[:, 512 * k + 128 * t4:512 * k + 128 * t4 + 128],
                                  ps[:, 2560 + 128 * (k - 4):2560 + 128 * (k - 4) + 128],
                                  AF.Copy)
            s_xe5.inc(cp)
        s_xe4.wait(TE)
        s_xe5.wait(TE)
        s_vec.wait(TE, wts_ready)
        for lc in range(4):
            pslot = ps[:, 3584:4096]
            if not (I == 0 and lc == 0):
                s_lat.wait(TE)
            for dc in range(8):
                mm = TE.matmul(pslot,
                               wcd_bf[:, 512 * dc + 128 * lc:512 * dc + 128 * lc + 128],
                               xnT[:, 512 * dc:512 * dc + 512],
                               start=(dc == 0), stop=(dc == 7))
            s_mm.inc(mm)
            s_mm.wait(S)
            s_lat.inc(S.activation(latT[:, T * lc + 512 * I:T * lc + 512 * I + 512],
                                   pslot, AF.Copy))
        lat_marks.append(s_mm.v)

    s_vec.wait(S, vext_ones)
    for I in range(ns):
        s_kv.wait(TE)
        s_lat.wait(TE)
        for lc in range(4):
            mm = TE.matmul(ps[:, 3584:4096], wkv_bf[:, 128 * lc:128 * lc + 128],
                           latT[:, T * lc + 512 * I:T * lc + 512 * I + 512],
                           start=(lc == 0), stop=(lc == 3))
        s_mm.inc(mm)
        s_mm.wait(V)
        s_kv.inc(V.tensor_copy(kT64[:, 512 * I:512 * I + 512], ps[0:64, 3584:4096]))
    for j in range(nt):
        s_kv.wait(TE)
        for lc in range(4):
            mm = TE.matmul(ps[:, 3584:3712],
                           latT[:, T * lc + 128 * j:T * lc + 128 * j + 128],
                           wkv_bf[:, 128 * lc:128 * lc + 128],
                           start=(lc == 0), stop=(lc == 3))
        s_mm.inc(mm)
        s_mm.wait(V)
        s_kv.inc(V.tensor_copy(vext[:, 66 * j:66 * j + 64], ps[:, 3648:3712]))
    V.memset(knt[:, :], 0.0)
    s_vec.inc(V.engine_nop())
    for I in range(ns):
        s_mm.wait(S)
        s_kv.wait(S)
        s_sca.inc(S.activation(sqk[:, :], kT64[:, 512 * I:512 * I + 512], AF.Square))
        s_sca.wait(TE)
        s_vec.wait(TE)
        pslot = ps[0:1, 3584:4096]
        s_mm.inc(TE.matmul(pslot, ones64[:, :], sqk[:, :], start=True, stop=True))
        s_mm.wait(V)
        V.tensor_copy(arow[:, :], pslot)
        s_vec.inc(V.engine_nop())
        s_vec.wait(TE)
        s_mm.inc(TE.matmul(ps[:, 3072:3584], ones1b[:, :], arow[:, :],
                           start=True, stop=True))
        s_mm.wait(V)
        V.reduce_max(kn2t[:, :], ps[:, 3072:3584], axis=AX.X)
        vsp()
        s_vec.inc(V.tensor_max(knt[:, :], knt[:, :], kn2t[:, :]))
    knt_ready = s_vec.v

    out_marks = []
    gpar = 0
    psum_free = {0: 0, 1: 0}
    pbf_free = {0: 0, 1: 0}
    for I in range(ns):
        nblk = 4 * I + 4
        ngrp = (nblk + 1) // 2
        s_asm.wait(TE)
        qp_base = s_qp.v
        for h in range(H):
            pslot = ps[0:64, 3584:4096] if h % 2 == 0 else ps[0:64, 3072:3584]
            if I == 0 and h == 1:
                s_vec.wait(TE, knt_ready)
            if h >= 2:
                s_qp.wait(TE, qp_base + h - 1)
            for lc in range(4):
                mm = TE.matmul(pslot,
                               wq_bf[:, 1024 * lc + 64 * h:1024 * lc + 64 * h + 64],
                               latT[:, T * lc + 512 * I:T * lc + 512 * I + 512],
                               start=(lc == 0), stop=(lc == 3))
            s_mm.inc(mm)
            s_mm.wait(V)
            s_qp.inc(V.tensor_copy(q64[:, 512 * h:512 * h + 512], pslot))
        s_qp.wait(V)
        s_vec.inc(V.tensor_mul(sq64[:, :], q64[:, :], q64[:, :]))
        s_vec.wait(TE)
        for h in range(H):
            mm = TE.matmul(ps[0:1, 3072:3584], ones64[:, :],
                           sq64[:, 512 * h:512 * h + 512],
                           start=(h == 0), stop=(h == H - 1))
        s_mm.inc(mm)
        s_mm.wait(V)
        s_vec.inc(V.tensor_copy(arow[:, :], ps[0:1, 3072:3584]))
        s_vec.wait(TE)
        s_mm.inc(TE.matmul(ps[:, 3584:4096], ones1b[:, :], arow[:, :],
                           start=True, stop=True))
        s_mm.wait(V)
        V.reduce_max(amaxv[:, :], ps[:, 3584:4096], axis=AX.X)
        vsp()
        s_vec.inc(V.tensor_scalar(amaxv[:, :], amaxv[:, :], 1e-30, None, ALU.max))
        s_vec.wait(S)
        S.activation(amaxv[:, :], amaxv[:, :], AF.Ln, scale=knt[:, 0:1])
        s_sca.inc(S.activation(amaxv[:, :], amaxv[:, :], AF.Exp, scale=0.5))
        s_sca.wait(V)
        s_nb.inc(V.tensor_scalar(nbias[:, :], amaxv[:, :], -1.0, 35.0,
                                 ALU.mult, ALU.add))
        nb_ready = s_nb.v
        for h in range(H):
            exp_at = {}
            par = {}
            sc_at = {}

            def emit_pv(gx):
                s_exp.wait(TE, exp_at[gx])
                if gx == 0:
                    s_ye.wait(TE)
                j0v = 2 * gx
                jnv = min(2, nblk - j0v)
                last = None
                for jj in range(jnv):
                    j = j0v + jj
                    for ti in range(4):
                        if j <= 4 * I + ti:
                            last = TE.matmul(
                                ps[:, 2048 + 512 * ti:2048 + 512 * ti + 65],
                                pbf[par[gx]][:, 512 * jj + 128 * ti:
                                             512 * jj + 128 * ti + 128],
                                vext[:, 66 * j:66 * j + 65],
                                start=(j == 0), stop=(j == 4 * I + ti))
                s_pv.inc(last)
                pbf_free[par[gx]] = s_pv.v

            for g in range(ngrp):
                pp = gpar % 2
                gpar += 1
                par[g] = pp
                j0 = 2 * g
                jn = min(2, nblk - j0)
                s_exp.wait(TE, psum_free[pp])
                for jj in range(jn):
                    j = j0 + jj
                    s_sc.inc(TE.matmul(
                        ps[:, 1024 * pp + 512 * jj:1024 * pp + 512 * jj + 512],
                        kT64[:, 128 * j:128 * j + 128],
                        q64[:, 512 * h:512 * h + 512],
                        start=True, stop=True))
                sc_at[g] = s_sc.v
                nmsk = 0
                for jj in range(jn):
                    di = j0 + jj - 4 * I
                    if di >= 0:
                        if nmsk == 0:
                            s_sc.wait(V, sc_at[g])
                            s_bld.wait_mark(V, mask_mk)
                        pvw = ps[:, 1024 * pp + 512 * jj:
                                 1024 * pp + 512 * jj + 128 * (di + 1)]
                        s_msk.inc(V.tensor_tensor(
                            pvw, pvw,
                            mask4[:, 512 * di:512 * di + 128 * (di + 1)], ALU.add))
                        nmsk += 1
                if g >= 1:
                    emit_pv(g - 1)
                s_sc.wait(S, sc_at[g])
                if nmsk:
                    s_msk.wait(S)
                s_pv.wait(S, pbf_free[pp])
                s_nb.wait(S, nb_ready)
                s_exp.inc(S.activation(pbf[pp][:, 0:512 * jn],
                                       ps[:, 1024 * pp:1024 * pp + 512 * jn],
                                       AF.Exp, bias=nbias[:, 0:1]))
                exp_at[g] = s_exp.v
                psum_free[pp] = s_exp.v
            emit_pv(ngrp - 1)
            s_pv.wait(V)
            if h == 0 and I > 0:
                s_ytp.wait(V, 8 * I)
            for ti in range(4):
                V.reciprocal(drb[:, ti:ti + 1],
                             ps[:, 2048 + 512 * ti + 64:2048 + 512 * ti + 65])
            vsp()
            for ti in range(4):
                s_ye.inc(V.tensor_scalar(
                    ybuf[:, D * ti + 64 * h:D * ti + 64 * h + 64],
                    ps[:, 2048 + 512 * ti:2048 + 512 * ti + 64],
                    drb[:, ti:ti + 1], None, ALU.mult))
        yb4 = s_ye4.v
        yb5 = s_ye5.v
        s_ye.wait(TE)
        for ti in range(4):
            if ti >= 1:
                s_ye4.wait(TE, yb4 + ti)
            for c in range(4):
                mm = TE.matmul(ps[:, 2048 + 128 * c:2048 + 128 * c + 128],
                               ybuf[:, D * ti + 128 * c:D * ti + 128 * c + 128],
                               idn[:, :], start=True, stop=True)
            s_ytp.inc(mm)
            if ti >= 1:
                s_ye5.wait(TE, yb5 + ti)
            for c in range(4, 8):
                mm = TE.matmul(ps[:, 2560 + 128 * (c - 4):2560 + 128 * (c - 4) + 128],
                               ybuf[:, D * ti + 128 * c:D * ti + 128 * c + 128],
                               idn[:, :], start=True, stop=True)
            s_ytp.inc(mm)
            s_ytp.wait(V, s_ytp.v - 1)
            if I > 0 and ti == 0:
                s_pr.wait(V)
            for c in range(4):
                cp = V.tensor_copy(yT[:, 512 * c + 128 * ti:512 * c + 128 * ti + 128],
                                   ps[:, 2048 + 128 * c:2048 + 128 * c + 128])
            s_ye4.inc(cp)
            s_ytp.wait(S)
            if I > 0 and ti == 0:
                s_pr.wait(S)
            for c in range(4, 8):
                cp = S.activation(yT[:, 512 * c + 128 * ti:512 * c + 128 * ti + 128],
                                  ps[:, 2560 + 128 * (c - 4):2560 + 128 * (c - 4) + 128],
                                  AF.Copy)
            s_ye5.inc(cp)
        s_ye4.wait(TE)
        s_ye5.wait(TE)
        for ti in range(4):
            i = 4 * I + ti
            b = xb[i % 2]
            fb = fbk[i % 2]
            s_xfr.wait(SY, s_xfr.v - (1 if i > 0 else 0))
            s_xld.begin(SY)
            s_xld.issue(SY.dma_start(b[:, :], x_h[i * 128:(i + 1) * 128, :]))
            asm_xmk = s_xld.mark()
            if i >= 2:
                s_asm.wait(G, s_asm.v - 2)
            s_fb.begin(G)
            s_fb.issue(G.dma_start(fb[:, :], spill_h[i * 128:(i + 1) * 128, :]))
            fb_mk = s_fb.mark()
            for half in range(2):
                pslot = ps[:, 3072 + 512 * half:3072 + 512 * half + 512]
                if not (I == 0 and ti == 0):
                    s_asm.wait(TE)
                for c in range(8):
                    mm = TE.matmul(
                        pslot, yT[:, 512 * c + 128 * ti:512 * c + 128 * ti + 128],
                        wpr_bf[:, 1024 * c + 512 * half:1024 * c + 512 * half + 512],
                        start=(c == 0), stop=(c == 7))
                s_pr.inc(mm)
            o = outst[i % 2]
            s_pr.wait(V)
            s_xld.wait_mark(V, asm_xmk)
            s_fb.wait_mark(V, fb_mk)
            if i >= 2:
                s_out.wait_mark(V, out_marks[i - 2])
            for half in range(2):
                pslot = ps[:, 3072 + 512 * half:3072 + 512 * half + 512]
                sl = slice(512 * half, 512 * half + 512)
                V.tensor_add(o[:, sl], pslot, b[:, sl])
                V.scalar_tensor_tensor(o[:, sl], fb[:, sl], stc(11)[:, i:i + 1],
                                       o[:, sl], ALU.mult, ALU.add)
                s_asm.inc(V.tensor_add(o[:, sl], o[:, sl], b2rep[:, sl]))
            s_xfr.inc(V.engine_nop())
            s_asm.wait(G)
            s_out.begin(G)
            s_out.issue(G.dma_start(out_h[i * 128:(i + 1) * 128, :], o[:, :]))
            out_marks.append(s_out.mark())

    s_out.wait_all(G)
    st.close()
    return nc


def _mask4_np():
    m = np.zeros((128, 4 * 512), dtype=np.float32)
    r = np.arange(128)
    for di in range(4):
        blk = m[:, 512 * di:512 * di + 512]
        blk[:, 0:128 * di] = NEG
        tri = blk[:, 128 * di:128 * di + 128]
        tri[r[:, None] > r[None, :]] = NEG
    return m


def kernel(L0_attn_x, L0_attn_Wc_down, L0_attn_Wq_up, L0_attn_Wkv_up,
           L0_attn_Wc_proj, L0_ffn_w1, L0_ffn_b1, L0_ffn_w2, L0_ffn_b2):
    T = 4096
    nc = build_nc(T)
    x = np.ascontiguousarray(np.asarray(L0_attn_x, dtype=np.float32))
    shared = {
        "wcd": np.ascontiguousarray(np.asarray(L0_attn_Wc_down, dtype=np.float32)),
        "wq": np.ascontiguousarray(np.asarray(L0_attn_Wq_up, dtype=np.float32)),
        "wkv": np.ascontiguousarray(np.asarray(L0_attn_Wkv_up, dtype=np.float32)),
        "wpr": np.ascontiguousarray(np.asarray(L0_attn_Wc_proj, dtype=np.float32)),
        "w1": np.ascontiguousarray(np.asarray(L0_ffn_w1, dtype=np.float32)),
        "b1": np.ascontiguousarray(np.asarray(L0_ffn_b1, dtype=np.float32)).reshape(1, -1),
        "w2": np.ascontiguousarray(np.asarray(L0_ffn_w2, dtype=np.float32)),
        "b2": np.ascontiguousarray(np.asarray(L0_ffn_b2, dtype=np.float32)).reshape(1, -1),
        "msk": _mask4_np(),
        "idn": np.eye(128, dtype=ml_dtypes.bfloat16),
    }
    in_maps = [dict(shared, x=x[c]) for c in range(8)]
    res = run_bass_kernel_spmd(nc, in_maps, core_ids=list(range(8)))
    return np.stack([np.asarray(res.results[c]["out"]) for c in range(8)], axis=0)
